# version 05
# baseline (speedup 1.0000x reference)
import numpy as np
import concourse.bass as bass
import concourse.mybir as mybir
from concourse.bass_utils import run_bass_kernel_spmd

F32 = mybir.dt.float32
BF16 = mybir.dt.bfloat16
I32 = mybir.dt.int32
AX = mybir.AxisListType.X
ALU = mybir.AluOpType
ACT = mybir.ActivationFunctionType

T = 2048
D = 1024
NT = T // 128
ND = D // 128


class Ser:

    def __init__(self):
        self.steps = []
        self.val = 0
        self.gval = 0

    def step(self, eng, fn):
        self.steps.append((eng, self.val, self.gval, fn, 0))
        self.val += 1

    def dma_n(self, eng, n, fn):
        if eng == "gpsimd":
            self.steps.append((eng, self.val, self.gval, fn, 2))
            self.gval += 16 * n
        else:
            self.steps.append((eng, self.val, self.gval, fn, 1))
            self.val += 16 * n

    def emit(self, block, sem, gsem):
        def section(name):
            def _(e):
                for eng, wv, gwv, fn, kind in self.steps:
                    if eng != name:
                        continue
                    if wv > 0:
                        e.wait_ge(sem, wv)
                    if gwv > 0:
                        e.wait_ge(gsem, gwv)
                    if kind == 2:
                        fn(e, gsem)
                    elif kind == 1:
                        fn(e, sem)
                    else:
                        fn(e).then_inc(sem, 1)
            return _
        block.sync(section("sync"))
        block.scalar(section("scalar"))
        block.vector(section("vector"))
        block.tensor(section("tensor"))
        block.gpsimd(section("gpsimd"))


def build_nc(block_ids=(0, 1, 2, 3), dbg=None, l1a_stop=99):
    nc = bass.Bass("TRN2", target_bir_lowering=False, detect_race_conditions=False)
    ddbg = nc.dram_tensor("dbg", [128, 16384], F32, kind="ExternalOutput") if dbg else None

    dx = nc.dram_tensor("x", [T, D], F32, kind="ExternalInput")
    dout = nc.dram_tensor("out", [T, D], F32, kind="ExternalOutput")
    dident = nc.dram_tensor("ident", [128, 128], F32, kind="ExternalInput")
    didentb = nc.dram_tensor("identb", [128, 128], BF16, kind="ExternalInput")
    donesrow = nc.dram_tensor("onesrow", [1, 128], F32, kind="ExternalInput")
    donescol = nc.dram_tensor("onescol", [128, 1], F32, kind="ExternalInput")

    W = {}
    for nm, shp, dt in [
        ("L0_attn_Wq", [D, D], F32), ("L0_attn_Wk", [D, D], F32),
        ("L0_attn_Wv", [D, D], F32), ("L0_attn_Wo", [D, D], F32),
        ("L0_attn_bo", [1, D], F32),
        ("L0_ffn_w_int8", [D, D], I32), ("L0_ffn_weight_scale", [1, 1], F32),
        ("L0_ffn_bias", [1, D], F32),
        ("L1_attn_Wq", [D, 512], F32), ("L1_attn_bq", [512, 1], F32),
        ("L1_attn_Wk", [D, 512], F32), ("L1_attn_bk", [512, 1], F32),
        ("L1_attn_Wv", [D, 512], F32), ("L1_attn_bv", [512, 1], F32),
        ("L1_attn_Wo", [512, D], F32), ("L1_attn_bo", [1, D], F32),
        ("L1_attn_alpha", [512, 1], F32), ("L1_attn_aug", [512, 1], F32),
        ("L1_attn_ln_scale", [1, D], F32), ("L1_attn_ln_bias", [1, D], F32),
        ("L1_ffn_w1", [4096, D], F32), ("L1_ffn_conv_w", [4096, 3], F32),
        ("L1_ffn_conv_b", [4096, 1], F32), ("L1_ffn_w2", [D, 4096], F32),
        ("sel_qk", [512, 8], F32), ("sel_rep", [8, 512], F32),
    ]:
        W[nm] = nc.dram_tensor(nm, shp, dt, kind="ExternalInput")

    d_xspill = nc.dram_tensor("xspill", [T, D], F32, kind="Internal")
    d_attns = nc.dram_tensor("attnspill", [D, T], F32, kind="Internal")
    d_xq1 = nc.dram_tensor("xq1spill", [D, T], BF16, kind="Internal")

    RX = nc.alloc_sbuf_tensor("RX", [128, 16384], F32)
    RA = nc.alloc_sbuf_tensor("RA", [128, 16384], F32)
    RB = nc.alloc_sbuf_tensor("RB", [128, 8448], F32)
    _ovl_b = {}
    _ovl_i = {}
    for _r in (RX, RA, RB):
        _addr = nc.lookup_mloc(_r).addr
        _cols = _r.shape[1]
        _ovl_b[_r.name] = nc.alloc_sbuf_tensor_at(
            _r.name + "_b16", [128, 2 * _cols], BF16, offset=_addr)
        _ovl_i[_r.name] = nc.alloc_sbuf_tensor_at(
            _r.name + "_i32", [128, _cols], I32, offset=_addr)

    def vf32(reg, byte_off, n):
        return reg[:, byte_off // 4: byte_off // 4 + n]

    def vb16(reg, byte_off, n):
        return _ovl_b[reg.tensor.name if hasattr(reg, "tensor") else reg.name][
            :, byte_off // 2: byte_off // 2 + n]

    def vi32(reg, byte_off, n):
        return _ovl_i[reg.tensor.name if hasattr(reg, "tensor") else reg.name][
            :, byte_off // 4: byte_off // 4 + n]

    ident = nc.alloc_sbuf_tensor("identS", [128, 128], F32)
    identb = nc.alloc_sbuf_tensor("identbS", [128, 128], BF16)
    onesrow = nc.alloc_sbuf_tensor("onesrowS", [1, 128], F32)
    onescol = nc.alloc_sbuf_tensor("onescolS", [128, 1], F32)
    cw = nc.alloc_sbuf_tensor("cwS", [128, 96], F32)
    cb = nc.alloc_sbuf_tensor("cbS", [128, 32], F32)
    xq1c = nc.alloc_sbuf_tensor("xq1cS", [128, 8 * 130], BF16)
    c1rep = nc.alloc_sbuf_tensor("c1repS", [128, 130], F32)
    s2rep = nc.alloc_sbuf_tensor("s2repS", [128, 128], F32)
    rowbuf = nc.alloc_sbuf_tensor("rowbufS", [16, 128], F32)
    selq = nc.alloc_sbuf_tensor("selqS", [128, 32], F32)
    selrep = nc.alloc_sbuf_tensor("selrepS", [8, 512], F32)
    lachan = nc.alloc_sbuf_tensor("lachanS", [128, 32], F32)
    d_xspill2 = nc.dram_tensor("xspill2", [T, D], F32, kind="Internal")
    rowflat = nc.alloc_sbuf_tensor("rowflatS", [1, 2048], F32)
    stat = nc.alloc_sbuf_tensor("stat", [128, 128], F32)
    stat2 = nc.alloc_sbuf_tensor("stat2", [128, 128], F32)
    tiny = nc.alloc_sbuf_tensor("tinyS", [128, 64], F32)
    rep4k = nc.alloc_sbuf_tensor("rep4k", [128, D], F32)
    rep4k2 = nc.alloc_sbuf_tensor("rep4k2", [128, D], F32)
    scratch4k = nc.alloc_sbuf_tensor("scratch4k", [128, 1024], F32)
    scratchq = nc.alloc_sbuf_tensor("scratchq", [128, 2 * 1024], F32)
    scratchx = nc.alloc_sbuf_tensor("scratchx", [128, 1024], F32)
    scratchqi = nc.alloc_sbuf_tensor("scratchqi", [128, 1024], I32)

    pT = nc.alloc_psum_tensor("pT", [128, 512], F32)
    pTb = nc.alloc_psum_tensor("pTb", [128, 1024], BF16)
    p1 = nc.alloc_psum_tensor("p1", [128, 1024], F32)
    p2 = nc.alloc_psum_tensor("p2", [128, 1024], F32)
    pS = nc.alloc_psum_tensor("pS", [128, 512], F32)

    ser = Ser()

    _tapoff = [0]

    def tap(name, ap_f32, ncols, at=None):
        if not dbg or (name not in dbg):
            return
        off = _tapoff[0] if at is None else at
        _tapoff[0] = off + ncols

        def f(e, sem):
            e.dma_start(out=ddbg.ap()[:, off:off + ncols], in_=ap_f32).then_inc(sem, 16)
        ser.dma_n("gpsimd", 1, f)

    def ld_consts():
        def f(e, sem):
            e.dma_start(out=ident[:], in_=dident.ap()).then_inc(sem, 16)
            e.dma_start(out=identb[:], in_=didentb.ap()).then_inc(sem, 16)
            e.dma_start(out=onesrow[:], in_=donesrow.ap()).then_inc(sem, 16)
            e.dma_start(out=onescol[:], in_=donescol.ap()).then_inc(sem, 16)
        ser.dma_n("sync", 4, f)

    def ln_stats_tile(src_ap, eps, rcol):
        def s1(e):
            return e.reduce_sum(stat2[:, rcol:rcol + 1], src_ap, AX)
        ser.step("vector", s1)

        def s2(e):
            return e.activation(scratchq[:, 0:D], src_ap, ACT.Square,
                                accum_out=stat2[:, rcol + 1:rcol + 2])
        ser.step("scalar", s2)

        def s3(e):
            e.tensor_scalar(stat2[:, rcol:rcol + 1], stat2[:, rcol:rcol + 1], 1.0 / D, None, ALU.mult)
            e.tensor_scalar(stat2[:, rcol + 1:rcol + 2], stat2[:, rcol + 1:rcol + 2], 1.0 / D, None, ALU.mult)
            e.nop()
            e.tensor_tensor(stat2[:, rcol + 3:rcol + 4], stat2[:, rcol:rcol + 1],
                            stat2[:, rcol:rcol + 1], ALU.mult)
            e.nop()
            e.tensor_tensor(stat2[:, rcol + 2:rcol + 3], stat2[:, rcol + 1:rcol + 2],
                            stat2[:, rcol + 3:rcol + 4], ALU.subtract)
            e.nop()
            return e.tensor_scalar(stat2[:, rcol + 2:rcol + 3], stat2[:, rcol + 2:rcol + 3],
                                   eps, None, ALU.add)
        ser.step("vector", s3)

        def s4(e):
            return e.activation(stat2[:, rcol + 2:rcol + 3], stat2[:, rcol + 2:rcol + 3], ACT.Sqrt)
        ser.step("scalar", s4)

        def s5a(e):
            return e.reciprocal(stat2[:, rcol + 3:rcol + 4], stat2[:, rcol + 2:rcol + 3])
        ser.step("vector", s5a)

        def s5(e):
            e.tensor_tensor(stat2[:, rcol + 1:rcol + 2], stat2[:, rcol:rcol + 1],
                            stat2[:, rcol + 3:rcol + 4], ALU.mult)
            e.nop()
            e.tensor_scalar(stat2[:, rcol + 1:rcol + 2], stat2[:, rcol + 1:rcol + 2],
                            -1.0, None, ALU.mult)
            e.nop()
            return e.tensor_copy(stat2[:, rcol:rcol + 1], stat2[:, rcol + 3:rcol + 4])
        ser.step("vector", s5)

    def ln_full(src_region, dst_region, eps=1e-5):
        n = NT
        src3 = src_region[:, 0:n * D].rearrange("p (n w) -> p n w", w=D)

        def sums(e):
            return e.reduce_sum(stat2[:, 0:n].unsqueeze(2), src3, AX)
        ser.step("vector", sums)
        for t in range(n):
            def sq(e, t=t):
                return e.activation(scratchq[:, 0:D],
                                    src_region[:, t * D:(t + 1) * D], ACT.Square,
                                    accum_out=stat2[:, n + t:n + t + 1])
            ser.step("scalar", sq)

        def var(e):
            e.tensor_scalar(stat2[:, 0:n], stat2[:, 0:n], 1.0 / D, None, ALU.mult)
            e.tensor_scalar(stat2[:, n:2 * n], stat2[:, n:2 * n], 1.0 / D, None, ALU.mult)
            e.nop()
            e.tensor_tensor(stat2[:, 3 * n:4 * n], stat2[:, 0:n], stat2[:, 0:n], ALU.mult)
            e.nop()
            e.tensor_tensor(stat2[:, 4 * n:5 * n], stat2[:, n:2 * n],
                            stat2[:, 3 * n:4 * n], ALU.subtract)
            e.nop()
            return e.tensor_scalar(stat2[:, 2 * n:3 * n], stat2[:, 4 * n:5 * n],
                                   eps, None, ALU.add)
        ser.step("vector", var)
        tap("lnstats", stat2[:, 0:128], 128)

        def sqrt_(e):
            return e.activation(stat2[:, 2 * n:3 * n], stat2[:, 2 * n:3 * n], ACT.Sqrt)
        ser.step("scalar", sqrt_)

        def fin0(e):
            return e.reciprocal(stat[:, 0:n], stat2[:, 2 * n:3 * n])
        ser.step("vector", fin0)

        def fin(e):
            e.tensor_tensor(stat[:, n:2 * n], stat2[:, 0:n], stat[:, 0:n], ALU.mult)
            e.nop()
            return e.tensor_scalar(stat[:, n:2 * n], stat[:, n:2 * n], -1.0, None, ALU.mult)
        ser.step("vector", fin)

        for t in range(n):
            def ap(e, t=t):
                return e.tensor_scalar(
                    dst_region[:, t * D:(t + 1) * D],
                    src_region[:, t * D:(t + 1) * D],
                    stat[:, t:t + 1], stat[:, n + t:n + t + 1],
                    ALU.mult, ALU.add)
            ser.step("vector", ap)

    def round_quant(src_ap, dst_ap, width, scale_ap, lo, hi):
        y = scratch4k[:, 0:width]
        ecol = scratchq[:, 0:width]
        cm = scratchq[:, 1024:1024 + width]
        ci = scratchqi[:, 0:width]

        def f(e):
            if scale_ap is not None:
                e.tensor_scalar(y, src_ap, scale_ap, None, ALU.mult)
                yy = y
            else:
                yy = src_ap
            e.tensor_copy(ci, yy)
            e.tensor_scalar(cm, ci, -1.0, None, ALU.add)
            e.tensor_tensor(ecol, yy, cm, ALU.subtract)
            e.scalar_tensor_tensor(cm, ecol, 0.5, cm, ALU.is_ge, ALU.add)
            e.scalar_tensor_tensor(cm, ecol, 1.5, cm, ALU.is_ge, ALU.add)
            return e.tensor_scalar(dst_ap, cm, float(lo), float(hi), ALU.max, ALU.min)
        ser.step("vector", f)

    def replicate_row(src_row_ap, dst_sb, width):
        def mm(e):
            last = None
            for h in range(0, width, 512):
                w = min(512, width - h)
                last = e.matmul(p1[:, h:h + w],
                                onesrow[0:1, :], src_row_ap[0:1, h:h + w],
                                start=True, stop=True)
            return last
        ser.step("tensor", mm)

        def cp(e):
            return e.tensor_copy(dst_sb[:, 0:width], p1[:, 0:width])
        ser.step("vector", cp)

    def transpose_b16(src_ap_list, dst_ap_list):
        nb = len(src_ap_list)
        assert nb <= 8

        def tr(e):
            last = None
            for k in range(nb):
                last = e.transpose(pTb[:, k * 128:(k + 1) * 128], src_ap_list[k], identb[:])
            return last
        ser.step("tensor", tr)

        def ev(e):
            last = None
            for k in range(nb):
                last = e.tensor_copy(dst_ap_list[k], pTb[:, k * 128:(k + 1) * 128])
            return last
        ser.step("vector", ev)

    ld_consts()
    x_tiled = dx.ap().rearrange("(n p) d -> n p d", p=128)

    def ldx(e, sem):
        for t in range(NT):
            e.dma_start(out=RX[:, t * D:(t + 1) * D], in_=x_tiled[t]).then_inc(sem, 16)
    ser.dma_n("sync", NT, ldx)

    def block_L0_ffn():
        xln = RA[:, 0:NT * D]
        ln_full(RX, RA, 1e-5)
        tap("xln", RA[:, 0:16384], 16384)
        tap("xa", xln[:, 4 * D:5 * D], D)

        n = NT
        src3 = xln.rearrange("p (n w) -> p n w", w=D)
        for t in range(n):
            def sq(e, t=t):
                return e.activation(scratchq[:, 0:D], xln[:, t * D:(t + 1) * D],
                                    ACT.Square, accum_out=stat2[:, t:t + 1])
            ser.step("scalar", sq)

        tap("xb", xln[:, 4 * D:5 * D], D)

        for t in range(n):
            def amax_(e, t=t):
                return e.reduce_max(stat2[:, n + t:n + t + 1], xln[:, t * D:(t + 1) * D],
                                    AX, apply_absolute_value=True)
            ser.step("vector", amax_)
        tap("amax", stat2[:, 0:64], 64)

        tap("xc", xln[:, 4 * D:5 * D], D)

        def ms_(e):
            return e.tensor_scalar(stat2[:, 0:n], stat2[:, 0:n], 1.0 / D, 1e-6,
                                   ALU.mult, ALU.add)
        ser.step("vector", ms_)

        def rsq_(e):
            return e.activation(stat2[:, 0:n], stat2[:, 0:n], ACT.Sqrt)
        ser.step("scalar", rsq_)

        def sc1(e):
            return e.reciprocal(stat2[:, 2 * n:3 * n], stat2[:, 0:n])
        ser.step("vector", sc1)

        def sc2(e):
            e.tensor_tensor(stat2[:, 3 * n:4 * n], stat2[:, 2 * n:3 * n],
                            stat2[:, n:2 * n], ALU.mult)
            e.nop()
            return e.tensor_scalar(stat2[:, 3 * n:4 * n], stat2[:, 3 * n:4 * n],
                                   1e-5, None, ALU.max)
        ser.step("vector", sc2)

        def sc3(e):
            return e.reciprocal(stat2[:, 4 * n:5 * n], stat2[:, 3 * n:4 * n])
        ser.step("vector", sc3)

        def sc4(e):
            e.tensor_scalar(stat2[:, 5 * n:6 * n], stat2[:, 4 * n:5 * n],
                            127.0, None, ALU.mult)
            e.nop()
            return e.tensor_tensor(stat[:, 32:32 + n], stat2[:, 2 * n:3 * n],
                                   stat2[:, 5 * n:6 * n], ALU.mult)
        ser.step("vector", sc4)

        def sc5(e):
            return e.reciprocal(stat[:, 32 + n:32 + 2 * n], stat2[:, 5 * n:6 * n])
        ser.step("vector", sc5)

        tap("xd", xln[:, 4 * D:5 * D], D)
        xq_tm = vb16(RB, 0, NT * D)
        for t in range(NT):
            round_quant(xln[:, t * D:(t + 1) * D], xq_tm[:, t * D:(t + 1) * D],
                        D, stat[:, 32 + t:32 + t + 1], -128, 127)
            if t == 3:
                tap("xe", xln[:, 4 * D:5 * D], D)
            if t == 4:
                tap("xf", xln[:, 4 * D:5 * D], D)
        tap("xg", xln[:, 4 * D:5 * D], D)

        tap("stat2q", stat2[:, 0:128], 128)
        tap("scales", stat[:, 0:128], 128)
        tap("xln4", xln[:, 4 * D:5 * D], D)
        tap("xq_tm", RB[:, 0:8192], 8192)
        xq_fm = vb16(RA, 0, ND * T)
        for c in range(ND):
            for tq in range(0, NT, 8):
                transpose_b16(
                    [xq_tm[:, (tq + k) * D + c * 128:(tq + k) * D + (c + 1) * 128]
                     for k in range(8)],
                    [xq_fm[:, c * T + (tq + k) * 128: c * T + (tq + k + 1) * 128]
                     for k in range(8)])

        w0T = vb16(RA, 32768, ND * D)
        wstage_i = vi32(RB, 16384, 1024)
        wstage_b = vb16(RB, 20480, 1024)
        wi = W["L0_ffn_w_int8"].ap()
        for o in range(ND):
            def ldw(e, sem, o=o):
                e.dma_start(out=wstage_i, in_=wi[o * 128:(o + 1) * 128, :]).then_inc(sem, 16)
            ser.dma_n("sync", 1, ldw)

            def cast(e, o=o):
                return e.tensor_copy(wstage_b, wstage_i)
            ser.step("vector", cast)
            for dq in range(0, ND, 8):
                transpose_b16(
                    [wstage_b[:, (dq + k) * 128:(dq + k + 1) * 128] for k in range(8)],
                    [w0T[:, (dq + k) * D + o * 128:(dq + k) * D + (o + 1) * 128]
                     for k in range(8)])

        tap("xq_fm", RA[:, 0:8192], 8192)
        tap("w0T", RA[:, 8192:12288], 4096)
        def ldb(e, sem):
            e.dma_start(out=rowflat[0:1, 0:D], in_=W["L0_ffn_bias"].ap()).then_inc(sem, 16)
            e.dma_start(out=tiny[0:1, 0:1], in_=W["L0_ffn_weight_scale"].ap()).then_inc(sem, 16)
        ser.dma_n("sync", 2, ldb)
        replicate_row(rowflat[0:1, 0:D], rep4k, D)

        def repws(e):
            return e.matmul(pS[:, 0:1], onesrow[0:1, :], tiny[0:1, 0:1],
                            start=True, stop=True)
        ser.step("tensor", repws)

        def cout(e):
            e.tensor_copy(tiny[:, 1:2], pS[:, 0:1])
            e.nop()
            return e.tensor_scalar(stat[:, 32 + NT:32 + 2 * NT],
                                   stat[:, 32 + NT:32 + 2 * NT],
                                   tiny[:, 1:2], None, ALU.mult)
        ser.step("vector", cout)

        for t in range(NT):
            def mm(e, t=t):
                last = None
                for dd in range(ND):
                    for half in range(2):
                        last = e.matmul(
                            p1[:, half * 512:(half + 1) * 512],
                            xq_fm[:, dd * T + t * 128: dd * T + (t + 1) * 128],
                            w0T[:, dd * D + half * 512: dd * D + (half + 1) * 512],
                            start=(dd == 0), stop=(dd == ND - 1))
                return last
            ser.step("tensor", mm)

            def ev(e, t=t):
                return e.scalar_tensor_tensor(
                    scratch4k[:, 0:D], p1[:, 0:D],
                    stat[:, 32 + NT + t:32 + NT + t + 1], rep4k[:, 0:D],
                    ALU.mult, ALU.add)
            ser.step("vector", ev)

            if t == 0:
                tap("blk0", scratch4k[:, 0:D], D)
            ln_stats_tile(scratch4k[:, 0:D], 1e-5, 8)

            def res(e, t=t):
                e.tensor_scalar(scratch4k[:, 0:D], scratch4k[:, 0:D],
                                stat2[:, 8:9], stat2[:, 9:10], ALU.mult, ALU.add)
                return e.tensor_tensor(RX[:, t * D:(t + 1) * D],
                                       RX[:, t * D:(t + 1) * D],
                                       scratch4k[:, 0:D], ALU.add)
            ser.step("vector", res)


    def block_L1_ffn():
        n = NT
        xln = RA[:, 0:NT * D]
        ln_full(RX, RA, 1e-5)

        for t in range(n):
            def amax_(e, t=t):
                return e.reduce_max(stat2[:, t:t + 1], xln[:, t * D:(t + 1) * D],
                                    AX, apply_absolute_value=True)
            ser.step("vector", amax_)

        def sc2(e):
            return e.tensor_scalar(stat2[:, n:2 * n], stat2[:, 0:n], 1e-5, None, ALU.max)
        ser.step("vector", sc2)

        def sc3(e):
            return e.reciprocal(stat2[:, 2 * n:3 * n], stat2[:, n:2 * n])
        ser.step("vector", sc3)

        def sc4(e):
            e.tensor_scalar(stat[:, 32:32 + n], stat2[:, 2 * n:3 * n], 127.0, None, ALU.mult)
            e.nop()
            return e.tensor_scalar(stat[:, 32 + n:32 + 2 * n], stat2[:, n:2 * n],
                                   1.0 / 127.0, None, ALU.mult)
        ser.step("vector", sc4)

        xq1_tm = vb16(RB, 0, NT * D)
        for t in range(NT):
            round_quant(xln[:, t * D:(t + 1) * D], xq1_tm[:, t * D:(t + 1) * D],
                        D, stat[:, 32 + t:32 + t + 1], -128, 127)

        xq1_fm = vb16(RA, 0, ND * T)
        for c in range(ND):
            for tq in range(0, NT, 8):
                transpose_b16(
                    [xq1_tm[:, (tq + k) * D + c * 128:(tq + k) * D + (c + 1) * 128]
                     for k in range(8)],
                    [xq1_fm[:, c * T + (tq + k) * 128: c * T + (tq + k + 1) * 128]
                     for k in range(8)])

        xs_tiled = d_xspill.ap().rearrange("(n p) d -> n p d", p=128)

        def spx(e, sem):
            for t in range(NT):
                e.dma_start(out=xs_tiled[t], in_=RX[:, t * D:(t + 1) * D]).then_inc(sem, 16)
        ser.dma_n("scalar", NT, spx)

        def spq(e, sem):
            for c in range(ND):
                e.dma_start(out=d_xq1.ap()[c * 128:(c + 1) * 128, :],
                            in_=xq1_fm[:, c * T:(c + 1) * T]).then_inc(sem, 16)
        ser.dma_n("sync", ND, spq)

        w1qT = vb16(RX, 0, 8 * 4096)
        w2qT = vb16(RA, 0, 32 * 1024)
        wst = RB[:, 0:4096]
        wqb = vb16(RB, 16384, 4096)

        def prep_ternary(dw, nchunks, width, statbase, qcol, wT, wT_colf):
            for ci in range(nchunks):
                def ldw(e, sem, ci=ci):
                    e.dma_start(out=wst[:, 0:width],
                                in_=dw.ap()[ci * 128:(ci + 1) * 128, :]).then_inc(sem, 16)
                ser.dma_n("sync", 1, ldw)

                def rs(e, ci=ci):
                    return e.reduce_sum(stat2[:, statbase + ci:statbase + ci + 1],
                                        wst[:, 0:width], AX, apply_absolute_value=True)
                ser.step("vector", rs)

            def rs2(e):
                return e.reduce_sum(stat2[:, statbase + nchunks:statbase + nchunks + 1],
                                    stat2[:, statbase:statbase + nchunks], AX)
            ser.step("vector", rs2)

            def xp(e):
                return e.matmul(pS[0:1, 0:1],
                                stat2[:, statbase + nchunks:statbase + nchunks + 1],
                                onescol[:], start=True, stop=True)
            ser.step("tensor", xp)

            def mean_(e):
                e.tensor_copy(tiny[0:1, qcol:qcol + 1], pS[0:1, 0:1])
                e.nop()
                e.tensor_scalar(tiny[0:1, qcol:qcol + 1], tiny[0:1, qcol:qcol + 1],
                                1.0 / (nchunks * 128 * width), None, ALU.mult)
                e.nop()
                return e.tensor_scalar(tiny[0:1, qcol:qcol + 1], tiny[0:1, qcol:qcol + 1],
                                       1e-5, None, ALU.max)
            ser.step("vector", mean_)

            def recw(e):
                return e.reciprocal(tiny[0:1, qcol + 1:qcol + 2], tiny[0:1, qcol:qcol + 1])
            ser.step("vector", recw)

            def repw(e):
                e.matmul(pS[:, 2:3], onesrow[0:1, :], tiny[0:1, qcol + 1:qcol + 2],
                         start=True, stop=True)
                return e.matmul(pS[:, 3:4], onesrow[0:1, :], tiny[0:1, qcol:qcol + 1],
                                start=True, stop=True)
            ser.step("tensor", repw)

            def cpw(e):
                e.tensor_copy(tiny[:, qcol + 2:qcol + 3], pS[:, 2:3])
                return e.tensor_copy(tiny[:, qcol + 3:qcol + 4], pS[:, 3:4])
            ser.step("vector", cpw)

            for ci in range(nchunks):
                def ldw(e, sem, ci=ci):
                    e.dma_start(out=wst[:, 0:width],
                                in_=dw.ap()[ci * 128:(ci + 1) * 128, :]).then_inc(sem, 16)
                ser.dma_n("sync", 1, ldw)
                for g in range(0, width, 1024):
                    gw = min(1024, width - g)
                    round_quant(wst[:, g:g + gw], wqb[:, g:g + gw], gw,
                                tiny[:, qcol + 2:qcol + 3], -1, 1)
                nblk = width // 128
                for bq in range(0, nblk, 8):
                    nb = min(8, nblk - bq)
                    transpose_b16(
                        [wqb[:, (bq + k) * 128:(bq + k + 1) * 128] for k in range(nb)],
                        [wT[:, wT_colf(ci, bq + k)] for k in range(nb)])

        prep_ternary(W["L1_ffn_w1"], 32, 1024, 34, 8,
                     w1qT, lambda hh, dd: slice(dd * 4096 + hh * 128, dd * 4096 + (hh + 1) * 128))
        prep_ternary(W["L1_ffn_w2"], 8, 4096, 34, 16,
                     w2qT, lambda oo, hh: slice(hh * 1024 + oo * 128, hh * 1024 + (oo + 1) * 128))

        def ldcv(e, sem):
            e.dma_start(out=cw[:].rearrange("p (c k) -> p c k", k=3),
                        in_=W["L1_ffn_conv_w"].ap().rearrange("(c p) k -> p c k", p=128)
                        ).then_inc(sem, 16)
            e.dma_start(out=cb[:].rearrange("p (c k) -> p c k", k=1),
                        in_=W["L1_ffn_conv_b"].ap().rearrange("(c p) k -> p c k", p=128),
                        allow_slow_non_contiguous=True).then_inc(sem, 16)
        ser.dma_n("sync", 2, ldcv)

        def c1c(e):
            return e.tensor_scalar(stat[:, 64:64 + n], stat[:, 32 + n:32 + 2 * n],
                                   tiny[:, 8 + 3:8 + 4], None, ALU.mult)
        ser.step("vector", c1c)

        def c1t(e):
            return e.transpose(pS[0:16, 0:128], stat[:, 64:64 + n], ident[:])
        ser.step("tensor", c1t)

        def c1cp(e):
            return e.tensor_copy(rowbuf[:, :], pS[0:16, 0:128])
        ser.step("vector", c1cp)

        def c1fl(e, sem):
            e.dma_start(out=rowflat[0:1, :], in_=rowbuf[:, :]).then_inc(sem, 16)
        ser.dma_n("gpsimd", 1, c1fl)

        h1 = RB[:, 0:32 * 130]
        h2 = RB[:, 32 * 130:32 * 130 + 32 * 128]
        xq2 = vb16(RB, 0, 32 * 128)
        out_tiled2 = dout.ap().rearrange("(n p) d -> n p d", p=128)

        for c in range(NT):
            lo = c * 128 - 1
            hi = c * 128 + 129

            def ldq(e, sem, c=c, lo=lo, hi=hi):
                clo, chi = max(lo, 0), min(hi, T)
                off = clo - lo
                for dd in range(ND):
                    e.dma_start(out=xq1c[:, dd * 130 + off: dd * 130 + off + (chi - clo)],
                                in_=d_xq1.ap()[dd * 128:(dd + 1) * 128, clo:chi]
                                ).then_inc(sem, 16)
            ser.dma_n("sync", ND, ldq)

            if c == 0 or c == NT - 1:
                def mz(e, c=c):
                    col = 0 if c == 0 else 129
                    last = None
                    for dd in range(ND):
                        last = e.memset(xq1c[:, dd * 130 + col: dd * 130 + col + 1], 0.0)
                    return last
                ser.step("vector", mz)

            def c1r(e, c=c, lo=lo):
                slo = max(lo, 0)
                width = 130 if 0 <= lo and lo + 130 <= T else 129
                return e.matmul(p2[:, 0:width], onesrow[0:1, :],
                                rowflat[0:1, slo:slo + width], start=True, stop=True)
            ser.step("tensor", c1r)

            def c1rc(e, c=c, lo=lo):
                slo = max(lo, 0)
                off = slo - lo
                width = 130 if 0 <= lo and lo + 130 <= T else 129
                last = e.tensor_copy(c1rep[:, off:off + width], p2[:, 0:width])
                if off:
                    last = e.memset(c1rep[:, 0:1], 0.0)
                if lo + 130 > T:
                    last = e.memset(c1rep[:, 129:130], 0.0)
                return last
            ser.step("vector", c1rc)

            for hb in range(16):
                def mm1(e, hb=hb):
                    last = None
                    for kk in range(2):
                        hh = hb * 2 + kk
                        for dd in range(ND):
                            last = e.matmul(p1[:, kk * 512: kk * 512 + 130],
                                            w1qT[:, dd * 4096 + hh * 128: dd * 4096 + (hh + 1) * 128],
                                            xq1c[:, dd * 130:(dd + 1) * 130],
                                            start=(dd == 0), stop=(dd == ND - 1))
                    return last
                ser.step("tensor", mm1)

                def ev1(e, hb=hb):
                    last = None
                    for kk in range(2):
                        hh = hb * 2 + kk
                        last = e.tensor_tensor(h1[:, hh * 130:(hh + 1) * 130],
                                               p1[:, kk * 512: kk * 512 + 130],
                                               c1rep[:, 0:130], ALU.mult)
                    return last
                ser.step("vector", ev1)

            for hh in range(0, 32, 4):
                def cv(e, hh=hh):
                    last = None
                    for k in range(4):
                        h = hh + k
                        e.tensor_scalar(h2[:, h * 128:(h + 1) * 128],
                                        h1[:, h * 130 + 1: h * 130 + 129],
                                        cw[:, h * 3 + 1:h * 3 + 2], cb[:, h:h + 1],
                                        ALU.mult, ALU.add)
                        e.scalar_tensor_tensor(h2[:, h * 128:(h + 1) * 128],
                                               h1[:, h * 130 + 0: h * 130 + 128],
                                               cw[:, h * 3 + 0:h * 3 + 1],
                                               h2[:, h * 128:(h + 1) * 128],
                                               ALU.mult, ALU.add)
                        last = e.scalar_tensor_tensor(h2[:, h * 128:(h + 1) * 128],
                                                      h1[:, h * 130 + 2: h * 130 + 130],
                                                      cw[:, h * 3 + 2:h * 3 + 3],
                                                      h2[:, h * 128:(h + 1) * 128],
                                                      ALU.mult, ALU.add)
                    return last
                ser.step("vector", cv)

            def sl(e):
                return e.activation(h2[:, 0:4096], h2[:, 0:4096], ACT.Silu)
            ser.step("scalar", sl)

            for g in range(8):
                def trp(e, g=g):
                    last = None
                    for k in range(4):
                        last = e.transpose(pT[:, k * 128:(k + 1) * 128],
                                          h2[:, (g * 4 + k) * 128:(g * 4 + k + 1) * 128],
                                          ident[:])
                    return last
                ser.step("tensor", trp)

                def rd(e, g=g):
                    return e.reduce_max(stat2[:, 64 + g:64 + g + 1], pT[:, 0:512],
                                        AX, apply_absolute_value=True)
                ser.step("vector", rd)

            def rmx(e):
                return e.reduce_max(stat2[:, 72:73], stat2[:, 64:72], AX)
            ser.step("vector", rmx)

            def s2a(e):
                e.tensor_scalar(stat2[:, 73:74], stat2[:, 72:73], 1e-5, None, ALU.max)
                e.nop()
                e.tensor_scalar(stat2[:, 74:75], stat2[:, 73:74], 1.0 / 127.0, None, ALU.mult)
                e.nop()
                return e.tensor_scalar(stat2[:, 75:76], stat2[:, 74:75],
                                       tiny[:, 16 + 3:16 + 4], None, ALU.mult)
            ser.step("vector", s2a)

            def s2b(e):
                return e.reciprocal(stat2[:, 76:77], stat2[:, 73:74])
            ser.step("vector", s2b)

            def s2c(e):
                return e.tensor_scalar(stat2[:, 77:78], stat2[:, 76:77], 127.0, None, ALU.mult)
            ser.step("vector", s2c)

            def s2t(e):
                return e.transpose(pS[0:1, 0:128], stat2[:, 77:78], ident[:])
            ser.step("tensor", s2t)

            def s2cp(e):
                return e.tensor_copy(rowbuf[0:1, 0:128], pS[0:1, 0:128])
            ser.step("vector", s2cp)

            def s2r(e):
                return e.matmul(p2[:, 0:128], onesrow[0:1, :], rowbuf[0:1, 0:128],
                                start=True, stop=True)
            ser.step("tensor", s2r)

            def s2rc(e):
                return e.tensor_copy(s2rep[:, :], p2[:, 0:128])
            ser.step("vector", s2rc)

            for g in range(4):
                def qg(e, g=g):
                    w = 1024
                    src3 = h2[:, g * w:(g + 1) * w].rearrange("p (a b) -> p a b", b=128)
                    s3 = s2rep[:, :].unsqueeze(1).broadcast_to((128, 8, 128))
                    y3 = scratch4k[:, 0:w].rearrange("p (a b) -> p a b", b=128)
                    e.tensor_tensor(y3, src3, s3, ALU.mult)
                    y = scratch4k[:, 0:w]
                    ecol = scratchq[:, 0:w]
                    cm = scratchq[:, 1024:1024 + w]
                    ci = scratchqi[:, 0:w]
                    e.tensor_copy(ci, y)
                    e.tensor_scalar(cm, ci, -1.0, None, ALU.add)
                    e.tensor_tensor(ecol, y, cm, ALU.subtract)
                    e.scalar_tensor_tensor(cm, ecol, 0.5, cm, ALU.is_ge, ALU.add)
                    e.scalar_tensor_tensor(cm, ecol, 1.5, cm, ALU.is_ge, ALU.add)
                    return e.tensor_scalar(xq2[:, g * w:(g + 1) * w], cm,
                                           -128.0, 127.0, ALU.max, ALU.min)
                ser.step("vector", qg)

            def mm2(e):
                last = None
                for hh in range(32):
                    for half in range(2):
                        last = e.matmul(p1[:, half * 512:(half + 1) * 512],
                                        xq2[:, hh * 128:(hh + 1) * 128],
                                        w2qT[:, hh * 1024 + half * 512: hh * 1024 + (half + 1) * 512],
                                        start=(hh == 0), stop=(hh == 31))
                return last
            ser.step("tensor", mm2)

            def ldx2(e, sem, c=c):
                e.dma_start(out=scratchx[:, 0:D], in_=xs_tiled[c]).then_inc(sem, 16)
            ser.dma_n("scalar", 1, ldx2)

            def ev2(e, c=c):
                return e.tensor_scalar(scratch4k[:, 0:D], p1[:, 0:D],
                                       stat2[:, 75:76], None, ALU.mult)
            ser.step("vector", ev2)

            ln_stats_tile(scratch4k[:, 0:D], 1e-5, 8)

            def res2(e, c=c):
                e.tensor_scalar(scratch4k[:, 0:D], scratch4k[:, 0:D],
                                stat2[:, 8:9], stat2[:, 9:10], ALU.mult, ALU.add)
                e.nop()
                return e.tensor_tensor(scratch4k[:, 0:D], scratchx[:, 0:D],
                                       scratch4k[:, 0:D], ALU.add)
            ser.step("vector", res2)

            def stc(e, sem, c=c):
                e.dma_start(out=out_tiled2[c], in_=scratch4k[:, 0:D]).then_inc(sem, 16)
            ser.dma_n("gpsimd", 1, stc)


    def block_L1_attn():
        n = NT
        IN = 512

        xln = RA[:, 0:NT * D]
        ln_full(RX, RA, 1e-5)

        xs_tiled = d_xspill.ap().rearrange("(n p) d -> n p d", p=128)

        def spx(e, sem):
            for t in range(NT):
                e.dma_start(out=xs_tiled[t], in_=RX[:, t * D:(t + 1) * D]).then_inc(sem, 16)
        ser.dma_n("scalar", NT, spx)

        xf_hi = vb16(RX, 0, ND * T)
        xf_lo = vb16(RX, 32768, ND * T)
        for c in range(ND):
            for tq in range(0, NT, 4):
                def tr(e, c=c, tq=tq):
                    last = None
                    for k in range(4):
                        t = tq + k
                        last = e.transpose(pT[:, k * 128:(k + 1) * 128],
                                           xln[:, t * D + c * 128: t * D + (c + 1) * 128],
                                           ident[:])
                    return last
                ser.step("tensor", tr)

                def ev(e, c=c, tq=tq):
                    e.tensor_copy(xf_hi[:, c * T + tq * 128: c * T + (tq + 4) * 128],
                                  pT[:, 0:512])
                    e.nop()
                    return e.scalar_tensor_tensor(
                        xf_lo[:, c * T + tq * 128: c * T + (tq + 4) * 128],
                        pT[:, 0:512], -1.0,
                        xf_hi[:, c * T + tq * 128: c * T + (tq + 4) * 128],
                        ALU.bypass, ALU.subtract)
                ser.step("vector", ev)

        if l1a_stop <= 1:
            return
        def ldsel(e, sem):
            e.dma_start(out=selq[:].rearrange("p (c h) -> p c h", h=8),
                        in_=W["sel_qk"].ap().rearrange("(c p) h -> p c h", p=128)
                        ).then_inc(sem, 16)
            e.dma_start(out=selrep[:], in_=W["sel_rep"].ap()).then_inc(sem, 16)
            e.dma_start(out=lachan[:, 0:4].rearrange("p (c k) -> p c k", k=1),
                        in_=W["L1_attn_alpha"].ap().rearrange("(c p) k -> p c k", p=128),
                        allow_slow_non_contiguous=True).then_inc(sem, 16)
            e.dma_start(out=lachan[:, 12:16].rearrange("p (c k) -> p c k", k=1),
                        in_=W["L1_attn_aug"].ap().rearrange("(c p) k -> p c k", p=128),
                        allow_slow_non_contiguous=True).then_inc(sem, 16)
            e.dma_start(out=lachan[:, 16:20].rearrange("p (c k) -> p c k", k=1),
                        in_=W["L1_attn_bq"].ap().rearrange("(c p) k -> p c k", p=128),
                        allow_slow_non_contiguous=True).then_inc(sem, 16)
            e.dma_start(out=lachan[:, 20:24].rearrange("p (c k) -> p c k", k=1),
                        in_=W["L1_attn_bk"].ap().rearrange("(c p) k -> p c k", p=128),
                        allow_slow_non_contiguous=True).then_inc(sem, 16)
            e.dma_start(out=lachan[:, 24:28].rearrange("p (c k) -> p c k", k=1),
                        in_=W["L1_attn_bv"].ap().rearrange("(c p) k -> p c k", p=128),
                        allow_slow_non_contiguous=True).then_inc(sem, 16)
        ser.dma_n("sync", 7, ldsel)

        def mkdec(e):
            e.activation(lachan[:, 4:8], lachan[:, 0:4], ACT.Sigmoid)
            return e.activation(lachan[:, 8:12], lachan[:, 0:4], ACT.Sigmoid, scale=-1.0)
        ser.step("scalar", mkdec)

        wstg = RB[:, 0:4096]
        w_hi = vb16(RB, 16384, 4096)
        w_lo = vb16(RB, 24576, 4096)

        def prep_w(dw):
            def ldw(e, sem):
                for dd in range(ND):
                    e.dma_start(out=wstg[:, dd * IN:(dd + 1) * IN],
                                in_=dw.ap()[dd * 128:(dd + 1) * 128, :]).then_inc(sem, 16)
            ser.dma_n("sync", ND, ldw)

            def sp(e):
                e.tensor_copy(w_hi[:, :], wstg[:, :])
                e.nop()
                return e.scalar_tensor_tensor(w_lo[:, :], wstg[:, :], -1.0,
                                              w_hi[:, :], ALU.bypass, ALU.subtract)
            ser.step("vector", sp)

        def project(dst_f32, bias_col):
            for cc in range(4):
                for tc in range(4):
                    def mm(e, cc=cc, tc=tc):
                        last = None
                        ops = [(dd, a, b) for dd in range(ND)
                               for (a, b) in ((xf_hi, w_hi), (xf_lo, w_hi), (xf_hi, w_lo))]
                        for i, (dd, a, b) in enumerate(ops):
                            last = e.matmul(
                                p1[:, 0:512],
                                b[:, dd * IN + cc * 128: dd * IN + (cc + 1) * 128],
                                a[:, dd * T + tc * 512: dd * T + (tc + 1) * 512],
                                start=(i == 0), stop=(i == len(ops) - 1))
                        return last
                    ser.step("tensor", mm)

                    def ev(e, cc=cc, tc=tc):
                        return e.tensor_scalar(
                            dst_f32[:, cc * T + tc * 512: cc * T + (tc + 1) * 512],
                            p1[:, 0:512], 1.0, lachan[:, bias_col + cc:bias_col + cc + 1],
                            ALU.mult, ALU.add)
                    ser.step("vector", ev)

        if l1a_stop <= 2:
            return
        q_f = RA[:, 0:4 * T]
        k_f = RA[:, 4 * T:8 * T]
        prep_w(W["L1_attn_Wq"])
        project(q_f, 16)
        prep_w(W["L1_attn_Wk"])
        project(k_f, 20)

        tap("qf", q_f[:, 0:8192], 8192)
        tap("kf", k_f[:, 0:8192], 8192)
        if l1a_stop <= 3:
            return
        qkbuf = RB[:, 0:2048]
        scsb = RB[:, 2048:2048 + 2048]
        for cc in range(4):
            def qk(e, cc=cc):
                return e.tensor_tensor(qkbuf[:, :], q_f[:, cc * T:(cc + 1) * T],
                                       k_f[:, cc * T:(cc + 1) * T], ALU.mult)
            ser.step("vector", qk)

            def scm(e, cc=cc):
                last = None
                for tc in range(4):
                    pp = p1 if tc < 2 else p2
                    last = e.matmul(pp[0:8, (tc % 2) * 512:(tc % 2) * 512 + 512],
                                    selq[:, cc * 8:(cc + 1) * 8],
                                    qkbuf[:, tc * 512:(tc + 1) * 512],
                                    start=(cc == 0), stop=(cc == 3))
                return last
            ser.step("tensor", scm)

        def sccp(e):
            e.tensor_copy(scsb[0:8, 0:1024], p1[0:8, 0:1024])
            return e.tensor_copy(scsb[0:8, 1024:2048], p2[0:8, 0:1024])
        ser.step("vector", sccp)

        def sct(e):
            last = None
            for t in range(NT):
                last = e.transpose(pT[:, t * 8:(t + 1) * 8],
                                   scsb[0:8, t * 128:(t + 1) * 128], ident[0:8, 0:8])
            return last
        ser.step("tensor", sct)

        sctm = scratchq[:, 0:128]

        def sce(e):
            return e.activation(sctm, pT[:, 0:128], ACT.Exp, scale=0.125)
        ser.step("scalar", sce)

        def scs(e):
            return e.reduce_sum(scratchq[:, 256:272].unsqueeze(2),
                                sctm.rearrange("p (a b) -> p a b", b=8), AX)
        ser.step("vector", scs)

        def scr(e):
            return e.reciprocal(scratchq[:, 272:288], scratchq[:, 256:272])
        ser.step("vector", scr)

        def dnt(e):
            return e.transpose(pS[0:16, 0:128], scratchq[:, 272:288], ident[:])
        ser.step("tensor", dnt)

        def dnc(e):
            return e.tensor_copy(rowbuf[:, :], pS[0:16, 0:128])
        ser.step("vector", dnc)

        def dnf(e, sem):
            e.dma_start(out=rowflat[0:1, :], in_=rowbuf[:, :]).then_inc(sem, 16)
        ser.dma_n("gpsimd", 1, dnf)

        attn_fm = scratchq[0:8, :]

        def efm(e):
            return e.activation(attn_fm[0:8, :], scsb[0:8, 0:2048], ACT.Exp, scale=0.125)
        ser.step("scalar", efm)

        def dnr(e):
            last = None
            for tc in range(4):
                pp = p1 if tc < 2 else p2
                last = e.matmul(pp[0:8, (tc % 2) * 512:(tc % 2) * 512 + 512],
                                onesrow[0:1, 0:8], rowflat[0:1, tc * 512:(tc + 1) * 512],
                                start=True, stop=True)
            return last
        ser.step("tensor", dnr)

        def anm(e):
            e.tensor_tensor(attn_fm[0:8, 0:1024], attn_fm[0:8, 0:1024],
                            p1[0:8, 0:1024], ALU.mult)
            e.nop()
            return e.tensor_tensor(attn_fm[0:8, 1024:2048], attn_fm[0:8, 1024:2048],
                                   p2[0:8, 0:1024], ALU.mult)
        ser.step("vector", anm)

        tap("attntm", scratchq[:, 0:512], 512)
        if l1a_stop <= 4:
            return
        v_f = k_f
        prep_w(W["L1_attn_Wv"])
        project(v_f, 24)
        tap("vf", v_f[:, 0:8192], 8192)
        tap("afm", scratchq[:, 0:2048], 2048)

        tap("vf", v_f[:, 0:8192], 8192)
        tap("afm", scsb[:, 0:2048], 2048)
        tap("lach", lachan[:, 0:32], 32)
        u_f = RB[:, 0:4 * T]
        for cc in range(4):
            def arm(e, cc=cc):
                last = None
                for tc in range(4):
                    pp = p1 if tc < 2 else p2
                    last = e.matmul(pp[:, (tc % 2) * 512:(tc % 2) * 512 + 512],
                                    selrep[0:8, cc * 128:(cc + 1) * 128],
                                    attn_fm[0:8, tc * 512:(tc + 1) * 512],
                                    start=True, stop=True)
                return last
            ser.step("tensor", arm)

            def mku(e, cc=cc):
                e.scalar_tensor_tensor(u_f[:, cc * T:cc * T + 1024], p1[:, 0:1024],
                                       lachan[:, 8 + cc:8 + cc + 1],
                                       v_f[:, cc * T:cc * T + 1024],
                                       ALU.mult, ALU.mult)
                e.nop()
                return e.scalar_tensor_tensor(u_f[:, cc * T + 1024:(cc + 1) * T], p2[:, 0:1024],
                                              lachan[:, 8 + cc:8 + cc + 1],
                                              v_f[:, cc * T + 1024:(cc + 1) * T],
                                              ALU.mult, ALU.mult)
            ser.step("vector", mku)

        if l1a_stop <= 5:
            return
        st_f = RX[:, 0:4 * T]
        for cc in range(4):
            def scan(e, cc=cc):
                return e.tensor_tensor_scan(
                    st_f[:, cc * T:(cc + 1) * T],
                    lachan[:, 4 + cc:4 + cc + 1].broadcast_to((128, T)),
                    u_f[:, cc * T:(cc + 1) * T],
                    0.0, ALU.mult, ALU.add)
            ser.step("vector", scan)

        tap("stf", st_f[:, 0:8192], 8192)
        tap("uf", u_f[:, 0:8192], 8192)
        if l1a_stop <= 6:
            return
        o_f = u_f
        for cc in range(4):
            def og1(e, cc=cc):
                return e.tensor_scalar(scratchq[:, 0:T], q_f[:, cc * T:(cc + 1) * T],
                                       lachan[:, 12 + cc:12 + cc + 1], None, ALU.mult)
            ser.step("vector", og1)

            def og2(e, cc=cc):
                return e.activation(scratchq[:, 0:T], scratchq[:, 0:T], ACT.Sigmoid)
            ser.step("scalar", og2)

            def og3(e, cc=cc):
                e.tensor_tensor(o_f[:, cc * T:(cc + 1) * T], q_f[:, cc * T:(cc + 1) * T],
                                st_f[:, cc * T:(cc + 1) * T], ALU.mult)
                e.nop()
                return e.tensor_tensor(o_f[:, cc * T:(cc + 1) * T],
                                       o_f[:, cc * T:(cc + 1) * T],
                                       scratchq[:, 0:T], ALU.add)
            ser.step("vector", og3)

        tap("of", o_f[:, 0:8192], 8192)
        if l1a_stop <= 7:
            return
        o_hi = vb16(RX, 32768, 4 * T)
        o_lo = vb16(RX, 49152, 4 * T)

        def osp(e):
            e.tensor_copy(o_hi[:, :], o_f[:, :])
            e.nop()
            return e.scalar_tensor_tensor(o_lo[:, :], o_f[:, :], -1.0,
                                          o_hi[:, :], ALU.bypass, ALU.subtract)
        ser.step("vector", osp)

        wo_st = RA[:, 0:4096]
        wo_hi = vb16(RA, 16384, 4096)
        wo_lo = vb16(RA, 24576, 4096)

        def ldwo(e, sem):
            for cc in range(4):
                e.dma_start(out=wo_st[:, cc * 1024:(cc + 1) * 1024],
                            in_=W["L1_attn_Wo"].ap()[cc * 128:(cc + 1) * 128, :]
                            ).then_inc(sem, 16)
        ser.dma_n("sync", 4, ldwo)

        def spwo(e):
            e.tensor_copy(wo_hi[:, :], wo_st[:, :])
            e.nop()
            return e.scalar_tensor_tensor(wo_lo[:, :], wo_st[:, :], -1.0,
                                          wo_hi[:, :], ALU.bypass, ALU.subtract)
        ser.step("vector", spwo)

        def ldg(e, sem):
            e.dma_start(out=rowflat[0:1, 0:D], in_=W["L1_attn_ln_scale"].ap()).then_inc(sem, 16)
            e.dma_start(out=rowflat[0:1, D:2 * D], in_=W["L1_attn_ln_bias"].ap()).then_inc(sem, 16)
        ser.dma_n("sync", 2, ldg)
        replicate_row(rowflat[0:1, 0:D], rep4k, D)
        replicate_row(rowflat[0:1, D:2 * D], rep4k2, D)

        def ldbo(e, sem):
            e.dma_start(out=rowflat[0:1, 0:D], in_=W["L1_attn_bo"].ap()).then_inc(sem, 16)
        ser.dma_n("sync", 1, ldbo)

        if l1a_stop <= 8:
            return
        xn_tiled = d_xspill2.ap().rearrange("(n p) d -> n p d", p=128)
        for t in range(NT):
            def mmo(e, t=t):
                last = None
                for cc in range(4):
                    for (a, b) in ((o_hi, wo_hi), (o_lo, wo_hi), (o_hi, wo_lo)):
                        for half in range(2):
                            last = e.matmul(
                                p1[:, half * 512:(half + 1) * 512],
                                a[:, cc * T + t * 128: cc * T + (t + 1) * 128],
                                b[:, cc * 1024 + half * 512: cc * 1024 + (half + 1) * 512],
                                start=(cc == 0 and a is o_hi and b is wo_hi),
                                stop=False)
                for half in range(2):
                    last = e.matmul(p1[:, half * 512:(half + 1) * 512],
                                    onesrow[0:1, :],
                                    rowflat[0:1, half * 512:(half + 1) * 512],
                                    start=False, stop=True)
                return last
            ser.step("tensor", mmo)
            if l1a_stop <= 9:
                continue

            def evo(e, t=t):
                return e.tensor_copy(scratch4k[:, 0:D], p1[:, 0:D])
            ser.step("vector", evo)

            ln_stats_tile(scratch4k[:, 0:D], 1e-6, 8)
            if l1a_stop <= 10:
                continue

            def inorm(e, t=t):
                e.tensor_scalar(scratch4k[:, 0:D], scratch4k[:, 0:D],
                                stat2[:, 8:9], stat2[:, 9:10], ALU.mult, ALU.add)
                e.nop()
                e.tensor_tensor(scratch4k[:, 0:D], scratch4k[:, 0:D], rep4k[:, 0:D], ALU.mult)
                e.nop()
                return e.tensor_tensor(scratch4k[:, 0:D], scratch4k[:, 0:D],
                                       rep4k2[:, 0:D], ALU.add)
            ser.step("vector", inorm)
            if l1a_stop <= 11:
                continue

            def ldxt(e, sem, t=t):
                e.dma_start(out=scratchx[:, 0:D], in_=xs_tiled[t]).then_inc(sem, 16)
            ser.dma_n("scalar", 1, ldxt)

            ln_stats_tile(scratch4k[:, 0:D], 1e-5, 16)
            if l1a_stop <= 12:
                continue

            def res3(e, t=t):
                e.tensor_scalar(scratch4k[:, 0:D], scratch4k[:, 0:D],
                                stat2[:, 16:17], stat2[:, 17:18], ALU.mult, ALU.add)
                e.nop()
                return e.tensor_tensor(scratch4k[:, 0:D], scratchx[:, 0:D],
                                       scratch4k[:, 0:D], ALU.add)
            ser.step("vector", res3)

            def stxt(e, sem, t=t):
                e.dma_start(out=xn_tiled[t], in_=scratch4k[:, 0:D]).then_inc(sem, 16)
            ser.dma_n("gpsimd", 1, stxt)

        if l1a_stop <= 13:
            return
        def relx(e, sem):
            for t in range(NT):
                e.dma_start(out=RX[:, t * D:(t + 1) * D], in_=xn_tiled[t]).then_inc(sem, 16)
        ser.dma_n("sync", NT, relx)


    def block_L0_attn():
        n = NT
        xln = RA[:, 0:NT * D]
        ln_full(RX, RA, 1e-5)

        xs_tiled = d_xspill.ap().rearrange("(n p) d -> n p d", p=128)

        def spx(e, sem):
            for t in range(NT):
                e.dma_start(out=xs_tiled[t], in_=RX[:, t * D:(t + 1) * D]).then_inc(sem, 16)
        ser.dma_n("scalar", NT, spx)

        xf_hi = vb16(RX, 0, ND * T)
        xf_lo = vb16(RX, 32768, ND * T)
        for c in range(ND):
            for tq in range(0, NT, 4):
                def tr(e, c=c, tq=tq):
                    last = None
                    for k in range(4):
                        t = tq + k
                        last = e.transpose(pT[:, k * 128:(k + 1) * 128],
                                           xln[:, t * D + c * 128: t * D + (c + 1) * 128],
                                           ident[:])
                    return last
                ser.step("tensor", tr)

                def ev(e, c=c, tq=tq):
                    e.tensor_copy(xf_hi[:, c * T + tq * 128: c * T + (tq + 4) * 128],
                                  pT[:, 0:512])
                    e.nop()
                    return e.scalar_tensor_tensor(
                        xf_lo[:, c * T + tq * 128: c * T + (tq + 4) * 128],
                        pT[:, 0:512], -1.0,
                        xf_hi[:, c * T + tq * 128: c * T + (tq + 4) * 128],
                        ALU.bypass, ALU.subtract)
                ser.step("vector", ev)

        PF32 = vf32(RA, 0, 2048)
        q_hi = vb16(RA, 8192, 2048)
        q_lo = vb16(RA, 12288, 2048)
        k_hi = vb16(RA, 16384, 2048)
        k_lo = vb16(RA, 20480, 2048)
        wT_hi = {"q": vb16(RA, 24576, 1024), "k": vb16(RA, 28672, 1024),
                 "v": vb16(RA, 32768, 1024)}
        wT_lo = {"q": vb16(RA, 26624, 1024), "k": vb16(RA, 30720, 1024),
                 "v": vb16(RA, 34816, 1024)}
        wstage = vf32(RA, 36864, 1024)
        vbuf = vb16(RB, 0, 16 * 260)
        E_f = vf32(RB, 8320, 1024)
        E_hi = vb16(RB, 12416, 1024)
        E_lo = vb16(RB, 14464, 1024)
        obuf = vf32(RB, 16512, 8 * 129)
        onorm = vf32(RB, 20640, 8 * 64)
        asm = vf32(RB, 22688, 1024)

        def vones(e):
            last = None
            for j in range(16):
                e.memset(vbuf[:, j * 260 + 128: j * 260 + 129], 1.0)
                last = e.memset(vbuf[:, j * 260 + 257: j * 260 + 258], 1.0)
            return last
        ser.step("vector", vones)

        def oslot(qc):
            if qc < 6:
                off = (qc // 3) * 512 + (qc % 3) * 170
                return p2[:, off:off + 129]
            off = (qc - 6) * 170
            return pS[:, off:off + 129]

        for pair in range(8):
            for wname, wdram in (("q", W["L0_attn_Wq"]), ("k", W["L0_attn_Wk"]),
                                 ("v", W["L0_attn_Wv"])):
                def ldw(e, sem, wdram=wdram, pair=pair):
                    e.dma_start(out=wstage[:, :],
                                in_=wdram.ap()[pair * 128:(pair + 1) * 128, :]
                                ).then_inc(sem, 16)
                ser.dma_n("sync", 1, ldw)
                for dq in range(0, 8, 4):
                    def trw(e, dq=dq):
                        last = None
                        for kk in range(4):
                            last = e.transpose(pT[:, kk * 128:(kk + 1) * 128],
                                               wstage[:, (dq + kk) * 128:(dq + kk + 1) * 128],
                                               ident[:])
                        return last
                    ser.step("tensor", trw)

                    def evw(e, dq=dq, wname=wname):
                        e.tensor_copy(wT_hi[wname][:, dq * 128:(dq + 4) * 128], pT[:, 0:512])
                        e.nop()
                        return e.scalar_tensor_tensor(
                            wT_lo[wname][:, dq * 128:(dq + 4) * 128], pT[:, 0:512], -1.0,
                            wT_hi[wname][:, dq * 128:(dq + 4) * 128],
                            ALU.bypass, ALU.subtract)
                    ser.step("vector", evw)

            def proj(wname):
                for tc in range(4):
                    def mm(e, tc=tc, wname=wname):
                        last = None
                        ops = [(dd, a, b) for dd in range(ND)
                               for (a, b) in ((xf_hi, wT_hi[wname]), (xf_lo, wT_hi[wname]),
                                              (xf_hi, wT_lo[wname]))]
                        for i, (dd, a, b) in enumerate(ops):
                            last = e.matmul(
                                p1[:, 0:512],
                                b[:, dd * 128:(dd + 1) * 128],
                                a[:, dd * T + tc * 512: dd * T + (tc + 1) * 512],
                                start=(i == 0), stop=(i == len(ops) - 1))
                        return last
                    ser.step("tensor", mm)

                    def ev(e, tc=tc):
                        return e.tensor_copy(PF32[:, tc * 512:(tc + 1) * 512], p1[:, 0:512])
                    ser.step("vector", ev)

            proj("q")

            def spq(e):
                e.tensor_copy(q_hi[:, :], PF32[:, :])
                e.nop()
                return e.scalar_tensor_tensor(q_lo[:, :], PF32[:, :], -1.0,
                                              q_hi[:, :], ALU.bypass, ALU.subtract)
            ser.step("vector", spq)

            proj("k")

            def spk(e):
                e.tensor_copy(k_hi[:, :], PF32[:, :])
                e.nop()
                return e.scalar_tensor_tensor(k_lo[:, :], PF32[:, :], -1.0,
                                              k_hi[:, :], ALU.bypass, ALU.subtract)
            ser.step("vector", spk)

            proj("v")
            for jq in range(0, 16, 4):
                def trv(e, jq=jq):
                    last = None
                    for kk in range(4):
                        j = jq + kk
                        last = e.transpose(pT[:, kk * 128:(kk + 1) * 128],
                                           PF32[:, j * 128:(j + 1) * 128], ident[:])
                    return last
                ser.step("tensor", trv)

                def evv(e, jq=jq):
                    last = None
                    for kk in range(4):
                        j = jq + kk
                        src = pT[:, kk * 128:(kk + 1) * 128]
                        e.tensor_copy(vbuf[:, j * 260 + 0: j * 260 + 64], src[:, 0:64])
                        e.tensor_copy(vbuf[:, j * 260 + 129: j * 260 + 193], src[:, 64:128])
                        e.nop()
                        e.scalar_tensor_tensor(vbuf[:, j * 260 + 64: j * 260 + 128],
                                               src[:, 0:64], -1.0,
                                               vbuf[:, j * 260 + 0: j * 260 + 64],
                                               ALU.bypass, ALU.subtract)
                        last = e.scalar_tensor_tensor(vbuf[:, j * 260 + 193: j * 260 + 257],
                                                      src[:, 64:128], -1.0,
                                                      vbuf[:, j * 260 + 129: j * 260 + 193],
                                                      ALU.bypass, ALU.subtract)
                    return last
                ser.step("vector", evv)

            for hp in range(2):
                ch0 = 64 * hp
                h = pair * 2 + hp
                for qq in range(4):
                    def oslot(qc):
                        return (p2[:, 0:129], p2[:, 512:641],
                                pS[:, 0:129], pT[:, 0:129])[qc]

                    for jb in range(8):
                        def smm(e, hp=hp, qq=qq, jb=jb, ch0=ch0):
                            last = None
                            for jj in range(2):
                                j = jb * 2 + jj
                                for pi, (ka, qa) in enumerate(((k_hi, q_hi), (k_hi, q_lo),
                                                               (k_lo, q_hi))):
                                    last = e.matmul(
                                        p1[:, jj * 512:(jj + 1) * 512],
                                        ka[ch0:ch0 + 64, j * 128:(j + 1) * 128],
                                        qa[ch0:ch0 + 64, qq * 512:(qq + 1) * 512],
                                        start=(pi == 0), stop=(pi == 2),
                                        tile_position=(ch0, 0))
                            return last
                        ser.step("tensor", smm)

                        def sexp(e):
                            return e.activation(E_f[:, 0:1024], p1[:, 0:1024], ACT.Exp,
                                                scale=0.125)
                        ser.step("scalar", sexp)

                        def ssplit(e):
                            e.tensor_copy(E_hi[:, 0:1024], E_f[:, 0:1024])
                            e.nop()
                            return e.scalar_tensor_tensor(E_lo[:, 0:1024], E_f[:, 0:1024],
                                                          -1.0, E_hi[:, 0:1024],
                                                          ALU.bypass, ALU.subtract)
                        ser.step("vector", ssplit)

                        def avmm(e, hp=hp, jb=jb):
                            last = None
                            for jj in range(2):
                                j = jb * 2 + jj
                                for qc in range(4):
                                    last = e.matmul(oslot(qc),
                                                    E_hi[:, jj * 512 + qc * 128: jj * 512 + (qc + 1) * 128],
                                                    vbuf[:, j * 260 + hp * 129: j * 260 + hp * 129 + 129],
                                                    start=(j == 0), stop=False)
                                    last = e.matmul(oslot(qc),
                                                    E_lo[:, jj * 512 + qc * 128: jj * 512 + (qc + 1) * 128],
                                                    vbuf[:, j * 260 + hp * 129: j * 260 + hp * 129 + 129],
                                                    start=False, stop=(j == 15))
                            return last
                        ser.step("tensor", avmm)

                    def oev(e):
                        last = None
                        for qc in range(4):
                            last = e.tensor_copy(obuf[:, qc * 129:(qc + 1) * 129],
                                                 oslot(qc))
                        return last
                    ser.step("vector", oev)

                    def oadd(e):
                        ob3 = obuf[:, 0:516].rearrange("p (s c) -> p s c", c=129)
                        e.tensor_copy(stat2[:, 80:84], ob3[:, :, 128:129].squeeze(2))
                        e.nop()
                        return e.tensor_tensor(
                            onorm[:, 0:256].rearrange("p (s c) -> p s c", c=64),
                            ob3[:, :, 0:64], ob3[:, :, 64:128], ALU.add)
                    ser.step("vector", oadd)

                    def orc(e):
                        return e.reciprocal(stat2[:, 88:92], stat2[:, 80:84])
                    ser.step("vector", orc)

                    def onrm(e):
                        on3 = onorm[:, 0:256].rearrange("p (s c) -> p s c", c=64)
                        return e.tensor_tensor(
                            on3, on3,
                            stat2[:, 88:92].unsqueeze(2).broadcast_to((128, 4, 64)),
                            ALU.mult)
                    ser.step("vector", onrm)

                    def otr(e):
                        last = None
                        for kk in range(4):
                            last = e.transpose(pT[0:64, kk * 128:(kk + 1) * 128],
                                               onorm[:, kk * 64:(kk + 1) * 64],
                                               ident[:])
                        return last
                    ser.step("tensor", otr)

                    def oas(e):
                        return e.tensor_copy(asm[0:64, 0:512], pT[0:64, 0:512])
                    ser.step("vector", oas)

                    def osp(e, sem, h=h, qq=qq):
                        e.dma_start(out=d_attns.ap()[h * 64:(h + 1) * 64,
                                                     qq * 512:(qq + 1) * 512],
                                    in_=asm[0:64, 0:512]).then_inc(sem, 16)
                    ser.dma_n("gpsimd", 1, osp)

        at_f32 = RX[:, 0:ND * T]
        at_hi = vb16(RA, 0, ND * T)
        at_lo = vb16(RA, 32768, ND * T)
        woT_hi = vb16(RB, 0, 8 * 1024)
        woT_lo = vb16(RB, 16384, 8 * 1024)

        def lda(e, sem):
            for c in range(ND):
                e.dma_start(out=at_f32[:, c * T:(c + 1) * T],
                            in_=d_attns.ap()[c * 128:(c + 1) * 128, :]).then_inc(sem, 16)
        ser.dma_n("sync", ND, lda)

        def spa(e):
            e.tensor_copy(at_hi[:, :], at_f32[:, :])
            e.nop()
            return e.scalar_tensor_tensor(at_lo[:, :], at_f32[:, :], -1.0,
                                          at_hi[:, :], ALU.bypass, ALU.subtract)
        ser.step("vector", spa)

        for oc in range(ND):
            def ldwo(e, sem, oc=oc):
                e.dma_start(out=scratch4k[:, 0:1024],
                            in_=W["L0_attn_Wo"].ap()[oc * 128:(oc + 1) * 128, :]
                            ).then_inc(sem, 16)
            ser.dma_n("sync", 1, ldwo)
            for dq in range(0, 8, 4):
                def trwo(e, dq=dq):
                    last = None
                    for kk in range(4):
                        last = e.transpose(pT[:, kk * 128:(kk + 1) * 128],
                                           scratch4k[:, (dq + kk) * 128:(dq + kk + 1) * 128],
                                           ident[:])
                    return last
                ser.step("tensor", trwo)

                def evwo(e, dq=dq, oc=oc):
                    ap_hi = woT_hi[:, :].rearrange("p (d o) -> p d o", o=1024)[
                        :, dq:dq + 4, oc * 128:(oc + 1) * 128]
                    ap_lo = woT_lo[:, :].rearrange("p (d o) -> p d o", o=1024)[
                        :, dq:dq + 4, oc * 128:(oc + 1) * 128]
                    src = pT[:, 0:512].rearrange("p (k c) -> p k c", c=128)
                    e.tensor_copy(ap_hi, src)
                    e.nop()
                    return e.scalar_tensor_tensor(ap_lo, src, -1.0, ap_hi,
                                                  ALU.bypass, ALU.subtract)
                ser.step("vector", evwo)

        def ldbo(e, sem):
            e.dma_start(out=rowflat[0:1, 0:D], in_=W["L0_attn_bo"].ap()).then_inc(sem, 16)
        ser.dma_n("sync", 1, ldbo)

        for t in range(NT):
            def mmo(e, t=t):
                last = None
                first = True
                for dd in range(ND):
                    for (a, b) in ((at_hi, woT_hi), (at_lo, woT_hi), (at_hi, woT_lo)):
                        for half in range(2):
                            last = e.matmul(
                                p1[:, half * 512:(half + 1) * 512],
                                a[:, dd * T + t * 128: dd * T + (t + 1) * 128],
                                b[:, dd * 1024 + half * 512: dd * 1024 + (half + 1) * 512],
                                start=first and half == 0 or (first and half == 1),
                                stop=False)
                        first = False
                for half in range(2):
                    last = e.matmul(p1[:, half * 512:(half + 1) * 512],
                                    onesrow[0:1, :],
                                    rowflat[0:1, half * 512:(half + 1) * 512],
                                    start=False, stop=True)
                return last
            ser.step("tensor", mmo)

            def evo(e, t=t):
                return e.tensor_copy(scratch4k[:, 0:D], p1[:, 0:D])
            ser.step("vector", evo)

            ln_stats_tile(scratch4k[:, 0:D], 1e-5, 8)

            def ldxt(e, sem, t=t):
                e.dma_start(out=scratchx[:, 0:D], in_=xs_tiled[t]).then_inc(sem, 16)
            ser.dma_n("scalar", 1, ldxt)

            def res0(e, t=t):
                e.tensor_scalar(scratch4k[:, 0:D], scratch4k[:, 0:D],
                                stat2[:, 8:9], stat2[:, 9:10], ALU.mult, ALU.add)
                e.nop()
                return e.tensor_tensor(RX[:, t * D:(t + 1) * D], scratchx[:, 0:D],
                                       scratch4k[:, 0:D], ALU.add)
            ser.step("vector", res0)

    blocks = {0: block_L0_attn, 1: block_L0_ffn, 2: block_L1_attn, 3: block_L1_ffn}
    for b in block_ids:
        if b in blocks:
            blocks[b]()

    if 3 not in block_ids:
        out_tiled = dout.ap().rearrange("(n p) d -> n p d", p=128)

        def stx(e, sem):
            for t in range(NT):
                e.dma_start(out=out_tiled[t], in_=RX[:, t * D:(t + 1) * D]).then_inc(sem, 16)
        ser.dma_n("gpsimd", NT, stx)

    def fin(e):
        return e.nop()
    ser.step("gpsimd", fin)

    with nc.semaphore("tok") as sem, nc.semaphore("gtok") as gsem, nc.Block() as block:
        ser.emit(block, sem, gsem)

    return nc


def _consts():
    import ml_dtypes
    c = {}
    c["ident"] = np.eye(128, dtype=np.float32)
    c["identb"] = np.eye(128).astype(ml_dtypes.bfloat16)
    c["onesrow"] = np.ones((1, 128), dtype=np.float32)
    c["onescol"] = np.ones((128, 1), dtype=np.float32)
    sel_qk = np.zeros((512, 8), dtype=np.float32)
    for ch in range(512):
        sel_qk[ch, ch // 64] = 1.0
    c["sel_qk"] = sel_qk
    sel_rep = np.zeros((8, 512), dtype=np.float32)
    for cc in range(4):
        for p in range(128):
            ch = cc * 128 + p
            sel_rep[ch // 64, cc * 128 + p] = 1.0
    c["sel_rep"] = sel_rep
    return c


def _in_map(core, inputs):
    m = dict(_consts())
    m["x"] = inputs["L0_attn_x"][core]
    for nm in ("L0_attn_Wq", "L0_attn_Wk", "L0_attn_Wv", "L0_attn_Wo",
               "L1_attn_Wq", "L1_attn_Wk", "L1_attn_Wv", "L1_attn_Wo",
               "L1_ffn_w1", "L1_ffn_w2", "L0_ffn_w_int8"):
        m[nm] = inputs[nm]
    m["L0_attn_bo"] = inputs["L0_attn_bo"].reshape(1, D)
    m["L0_ffn_weight_scale"] = np.asarray(inputs["L0_ffn_weight_scale"]).reshape(1, 1)
    m["L0_ffn_bias"] = inputs["L0_ffn_bias"].reshape(1, D)
    m["L1_attn_bq"] = inputs["L1_attn_bq"].reshape(512, 1)
    m["L1_attn_bk"] = inputs["L1_attn_bk"].reshape(512, 1)
    m["L1_attn_bv"] = inputs["L1_attn_bv"].reshape(512, 1)
    m["L1_attn_bo"] = inputs["L1_attn_bo"].reshape(1, D)
    m["L1_attn_alpha"] = np.asarray(inputs["L1_attn_alpha"]).reshape(512, 1)
    m["L1_attn_aug"] = np.asarray(inputs["L1_attn_aug_param"]).reshape(512, 1)
    m["L1_attn_ln_scale"] = inputs["L1_attn_ln_scale"].reshape(1, D)
    m["L1_attn_ln_bias"] = inputs["L1_attn_ln_bias"].reshape(1, D)
    m["L1_ffn_conv_w"] = np.asarray(inputs["L1_ffn_conv_w"]).reshape(4096, 3)
    m["L1_ffn_conv_b"] = np.asarray(inputs["L1_ffn_conv_b"]).reshape(4096, 1)
    for k, v in list(m.items()):
        m[k] = np.ascontiguousarray(v)
    return m


_BLOCK_IDS = (0, 1, 2, 3)


def kernel(**inputs):
    nc = build_nc(_BLOCK_IDS)
    in_maps = [_in_map(c, inputs) for c in range(8)]
    res = run_bass_kernel_spmd(nc, in_maps, core_ids=list(range(8)))
    return np.stack([np.asarray(r["out"], dtype=np.float32) for r in res.results], axis=0)


# version 09 (final)
# speedup vs baseline: 1.0116x; 1.0116x over previous
import numpy as np
import concourse.bass as bass
import concourse.mybir as mybir
from concourse.bass_utils import run_bass_kernel_spmd

F32 = mybir.dt.float32
BF16 = mybir.dt.bfloat16
I32 = mybir.dt.int32
AX = mybir.AxisListType.X
ALU = mybir.AluOpType
ACT = mybir.ActivationFunctionType

T = 2048
D = 1024
NT = T // 128
ND = D // 128


class Ser:

    def __init__(self):
        self.steps = []
        self.val = 0
        self.gval = 0

    def step(self, eng, fn):
        self.steps.append((eng, self.val, self.gval, fn, 0))
        self.val += 1

    def dma_n(self, eng, n, fn):
        if eng == "gpsimd":
            self.steps.append((eng, self.val, self.gval, fn, 2))
            self.gval += 16 * n
        else:
            self.steps.append((eng, self.val, self.gval, fn, 1))
            self.val += 16 * n

    def emit(self, block, sem, gsem):
        def section(name):
            def _(e):
                for eng, wv, gwv, fn, kind in self.steps:
                    if eng != name:
                        continue
                    if wv > 0:
                        e.wait_ge(sem, wv)
                    if gwv > 0:
                        e.wait_ge(gsem, gwv)
                    if kind == 2:
                        fn(e, gsem)
                    elif kind == 1:
                        fn(e, sem)
                    else:
                        fn(e).then_inc(sem, 1)
            return _
        block.sync(section("sync"))
        block.scalar(section("scalar"))
        block.vector(section("vector"))
        block.tensor(section("tensor"))
        block.gpsimd(section("gpsimd"))


def build_nc(block_ids=(0, 1, 2, 3), dbg=None, l1a_stop=99):
    nc = bass.Bass("TRN2", target_bir_lowering=False, detect_race_conditions=False)
    ddbg = nc.dram_tensor("dbg", [128, 16384], F32, kind="ExternalOutput") if dbg else None

    dx = nc.dram_tensor("x", [T, D], F32, kind="ExternalInput")
    dout = nc.dram_tensor("out", [T, D], F32, kind="ExternalOutput")
    dident = nc.dram_tensor("ident", [128, 128], F32, kind="ExternalInput")
    didentb = nc.dram_tensor("identb", [128, 128], BF16, kind="ExternalInput")
    donesrow = nc.dram_tensor("onesrow", [1, 128], F32, kind="ExternalInput")
    donescol = nc.dram_tensor("onescol", [128, 1], F32, kind="ExternalInput")

    W = {}
    for nm, shp, dt in [
        ("L0_attn_Wq", [D, D], F32), ("L0_attn_Wk", [D, D], F32),
        ("L0_attn_Wv", [D, D], F32), ("L0_attn_Wo", [D, D], F32),
        ("L0_attn_bo", [1, D], F32),
        ("L0_ffn_w_int8", [D, D], I32), ("L0_ffn_weight_scale", [1, 1], F32),
        ("L0_ffn_bias", [1, D], F32),
        ("L1_attn_Wq", [D, 512], F32), ("L1_attn_bq", [512, 1], F32),
        ("L1_attn_Wk", [D, 512], F32), ("L1_attn_bk", [512, 1], F32),
        ("L1_attn_Wv", [D, 512], F32), ("L1_attn_bv", [512, 1], F32),
        ("L1_attn_Wo", [512, D], F32), ("L1_attn_bo", [1, D], F32),
        ("L1_attn_alpha", [512, 1], F32), ("L1_attn_aug", [512, 1], F32),
        ("L1_attn_ln_scale", [1, D], F32), ("L1_attn_ln_bias", [1, D], F32),
        ("L1_ffn_w1", [4096, D], F32), ("L1_ffn_conv_w", [4096, 3], F32),
        ("L1_ffn_conv_b", [4096, 1], F32), ("L1_ffn_w2", [D, 4096], F32),
        ("sel_qk", [512, 8], F32), ("sel_rep", [8, 512], F32),
    ]:
        W[nm] = nc.dram_tensor(nm, shp, dt, kind="ExternalInput")

    d_xspill = nc.dram_tensor("xspill", [T, D], F32, kind="Internal")
    d_attns = nc.dram_tensor("attnspill", [D, T], F32, kind="Internal")
    d_xq1 = nc.dram_tensor("xq1spill", [D, T], BF16, kind="Internal")

    RX = nc.alloc_sbuf_tensor("RX", [128, 16384], F32)
    RA = nc.alloc_sbuf_tensor("RA", [128, 16384], F32)
    RB = nc.alloc_sbuf_tensor("RB", [128, 8448], F32)
    _ovl_b = {}
    _ovl_i = {}
    for _r in (RX, RA, RB):
        _addr = nc.lookup_mloc(_r).addr
        _cols = _r.shape[1]
        _ovl_b[_r.name] = nc.alloc_sbuf_tensor_at(
            _r.name + "_b16", [128, 2 * _cols], BF16, offset=_addr)
        _ovl_i[_r.name] = nc.alloc_sbuf_tensor_at(
            _r.name + "_i32", [128, _cols], I32, offset=_addr)

    def vf32(reg, byte_off, n):
        return reg[:, byte_off // 4: byte_off // 4 + n]

    def vb16(reg, byte_off, n):
        return _ovl_b[reg.tensor.name if hasattr(reg, "tensor") else reg.name][
            :, byte_off // 2: byte_off // 2 + n]

    def vi32(reg, byte_off, n):
        return _ovl_i[reg.tensor.name if hasattr(reg, "tensor") else reg.name][
            :, byte_off // 4: byte_off // 4 + n]

    ident = nc.alloc_sbuf_tensor("identS", [128, 128], F32)
    identb = nc.alloc_sbuf_tensor("identbS", [128, 128], BF16)
    onesrow = nc.alloc_sbuf_tensor("onesrowS", [1, 128], F32)
    onescol = nc.alloc_sbuf_tensor("onescolS", [128, 1], F32)
    cw = nc.alloc_sbuf_tensor("cwS", [128, 96], F32)
    cb = nc.alloc_sbuf_tensor("cbS", [128, 32], F32)
    xq1c = nc.alloc_sbuf_tensor("xq1cS", [128, 8 * 130], BF16)
    c1rep = nc.alloc_sbuf_tensor("c1repS", [128, 130], F32)
    s2rep = nc.alloc_sbuf_tensor("s2repS", [128, 128], F32)
    rowbuf = nc.alloc_sbuf_tensor("rowbufS", [16, 128], F32)
    selq = nc.alloc_sbuf_tensor("selqS", [128, 32], F32)
    selrep = nc.alloc_sbuf_tensor("selrepS", [8, 512], F32)
    lachan = nc.alloc_sbuf_tensor("lachanS", [128, 32], F32)
    d_xspill2 = nc.dram_tensor("xspill2", [T, D], F32, kind="Internal")
    rowflat = nc.alloc_sbuf_tensor("rowflatS", [1, 2048], F32)
    stat = nc.alloc_sbuf_tensor("stat", [128, 128], F32)
    stat2 = nc.alloc_sbuf_tensor("stat2", [128, 128], F32)
    tiny = nc.alloc_sbuf_tensor("tinyS", [128, 64], F32)
    rep4k = nc.alloc_sbuf_tensor("rep4k", [128, D], F32)
    rep4k2 = nc.alloc_sbuf_tensor("rep4k2", [128, D], F32)
    scratch4k = nc.alloc_sbuf_tensor("scratch4k", [128, 1024], F32)
    scratchq = nc.alloc_sbuf_tensor("scratchq", [128, 2 * 1024], F32)
    scratchx = nc.alloc_sbuf_tensor("scratchx", [128, 1024], F32)
    scratchqi = nc.alloc_sbuf_tensor("scratchqi", [128, 1024], I32)

    pT = nc.alloc_psum_tensor("pT", [128, 512], F32)
    pTb = nc.alloc_psum_tensor("pTb", [128, 1024], BF16)
    p1 = nc.alloc_psum_tensor("p1", [128, 1024], F32)
    p2 = nc.alloc_psum_tensor("p2", [128, 1024], F32)
    pS = nc.alloc_psum_tensor("pS", [128, 512], F32)

    ser = Ser()

    _tapoff = [0]

    def tap(name, ap_f32, ncols, at=None):
        if not dbg or (name not in dbg):
            return
        off = _tapoff[0] if at is None else at
        _tapoff[0] = off + ncols

        def f(e, sem):
            e.dma_start(out=ddbg.ap()[:, off:off + ncols], in_=ap_f32).then_inc(sem, 16)
        ser.dma_n("gpsimd", 1, f)

    def ld_consts():
        def f(e, sem):
            e.dma_start(out=ident[:], in_=dident.ap()).then_inc(sem, 16)
            e.dma_start(out=identb[:], in_=didentb.ap()).then_inc(sem, 16)
            e.dma_start(out=onesrow[:], in_=donesrow.ap()).then_inc(sem, 16)
            e.dma_start(out=onescol[:], in_=donescol.ap()).then_inc(sem, 16)
        ser.dma_n("sync", 4, f)

    def ln_stats_tile(src_ap, eps, rcol):
        def s1(e):
            return e.reduce_sum(stat2[:, rcol:rcol + 1], src_ap, AX)
        ser.step("vector", s1)

        def s2(e):
            return e.activation(scratchq[:, 0:D], src_ap, ACT.Square,
                                accum_out=stat2[:, rcol + 1:rcol + 2])
        ser.step("scalar", s2)

        def s3(e):
            e.tensor_scalar(stat2[:, rcol:rcol + 1], stat2[:, rcol:rcol + 1], 1.0 / D, None, ALU.mult)
            e.tensor_scalar(stat2[:, rcol + 1:rcol + 2], stat2[:, rcol + 1:rcol + 2], 1.0 / D, None, ALU.mult)
            e.nop()
            e.tensor_tensor(stat2[:, rcol + 3:rcol + 4], stat2[:, rcol:rcol + 1],
                            stat2[:, rcol:rcol + 1], ALU.mult)
            e.nop()
            e.tensor_tensor(stat2[:, rcol + 2:rcol + 3], stat2[:, rcol + 1:rcol + 2],
                            stat2[:, rcol + 3:rcol + 4], ALU.subtract)
            e.nop()
            return e.tensor_scalar(stat2[:, rcol + 2:rcol + 3], stat2[:, rcol + 2:rcol + 3],
                                   eps, None, ALU.add)
        ser.step("vector", s3)

        def s4(e):
            return e.activation(stat2[:, rcol + 2:rcol + 3], stat2[:, rcol + 2:rcol + 3], ACT.Sqrt)
        ser.step("scalar", s4)

        def s5a(e):
            return e.reciprocal(stat2[:, rcol + 3:rcol + 4], stat2[:, rcol + 2:rcol + 3])
        ser.step("vector", s5a)

        def s5(e):
            e.tensor_tensor(stat2[:, rcol + 1:rcol + 2], stat2[:, rcol:rcol + 1],
                            stat2[:, rcol + 3:rcol + 4], ALU.mult)
            e.nop()
            e.tensor_scalar(stat2[:, rcol + 1:rcol + 2], stat2[:, rcol + 1:rcol + 2],
                            -1.0, None, ALU.mult)
            e.nop()
            return e.tensor_copy(stat2[:, rcol:rcol + 1], stat2[:, rcol + 3:rcol + 4])
        ser.step("vector", s5)

    def ln_full(src_region, dst_region, eps=1e-5):
        n = NT
        src3 = src_region[:, 0:n * D].rearrange("p (n w) -> p n w", w=D)

        def sums(e):
            return e.reduce_sum(stat2[:, 0:n].unsqueeze(2), src3, AX)
        ser.step("vector", sums)
        for t in range(n):
            def sq(e, t=t):
                return e.activation(scratchq[:, 0:D],
                                    src_region[:, t * D:(t + 1) * D], ACT.Square,
                                    accum_out=stat2[:, n + t:n + t + 1])
            ser.step("scalar", sq)

        def var(e):
            e.tensor_scalar(stat2[:, 0:n], stat2[:, 0:n], 1.0 / D, None, ALU.mult)
            e.tensor_scalar(stat2[:, n:2 * n], stat2[:, n:2 * n], 1.0 / D, None, ALU.mult)
            e.nop()
            e.tensor_tensor(stat2[:, 3 * n:4 * n], stat2[:, 0:n], stat2[:, 0:n], ALU.mult)
            e.nop()
            e.tensor_tensor(stat2[:, 4 * n:5 * n], stat2[:, n:2 * n],
                            stat2[:, 3 * n:4 * n], ALU.subtract)
            e.nop()
            return e.tensor_scalar(stat2[:, 2 * n:3 * n], stat2[:, 4 * n:5 * n],
                                   eps, None, ALU.add)
        ser.step("vector", var)
        tap("lnstats", stat2[:, 0:128], 128)

        def sqrt_(e):
            return e.activation(stat2[:, 2 * n:3 * n], stat2[:, 2 * n:3 * n], ACT.Sqrt)
        ser.step("scalar", sqrt_)

        def fin0(e):
            return e.reciprocal(stat[:, 0:n], stat2[:, 2 * n:3 * n])
        ser.step("vector", fin0)

        def fin(e):
            e.tensor_tensor(stat[:, n:2 * n], stat2[:, 0:n], stat[:, 0:n], ALU.mult)
            e.nop()
            return e.tensor_scalar(stat[:, n:2 * n], stat[:, n:2 * n], -1.0, None, ALU.mult)
        ser.step("vector", fin)

        for t in range(n):
            def ap(e, t=t):
                return e.tensor_scalar(
                    dst_region[:, t * D:(t + 1) * D],
                    src_region[:, t * D:(t + 1) * D],
                    stat[:, t:t + 1], stat[:, n + t:n + t + 1],
                    ALU.mult, ALU.add)
            ser.step("vector", ap)

    def round_quant(src_ap, dst_ap, width, scale_ap, lo, hi):
        y = scratch4k[:, 0:width]
        ecol = scratchq[:, 0:width]
        cm = scratchq[:, 1024:1024 + width]
        ci = scratchqi[:, 0:width]

        def f(e):
            if scale_ap is not None:
                e.tensor_scalar(y, src_ap, scale_ap, None, ALU.mult)
                yy = y
            else:
                yy = src_ap
            e.tensor_copy(ci, yy)
            e.tensor_scalar(cm, ci, -1.0, None, ALU.add)
            e.tensor_tensor(ecol, yy, cm, ALU.subtract)
            e.scalar_tensor_tensor(cm, ecol, 0.5, cm, ALU.is_ge, ALU.add)
            e.scalar_tensor_tensor(cm, ecol, 1.5, cm, ALU.is_ge, ALU.add)
            return e.tensor_scalar(dst_ap, cm, float(lo), float(hi), ALU.max, ALU.min)
        ser.step("vector", f)

    def replicate_row(src_row_ap, dst_sb, width):
        def mm(e):
            last = None
            for h in range(0, width, 512):
                w = min(512, width - h)
                last = e.matmul(p1[:, h:h + w],
                                onesrow[0:1, :], src_row_ap[0:1, h:h + w],
                                start=True, stop=True)
            return last
        ser.step("tensor", mm)

        def cp(e):
            return e.tensor_copy(dst_sb[:, 0:width], p1[:, 0:width])
        ser.step("vector", cp)

    def transpose_b16(src_ap_list, dst_ap_list):
        nb = len(src_ap_list)
        assert nb <= 8

        def tr(e):
            last = None
            for k in range(nb):
                last = e.transpose(pTb[:, k * 128:(k + 1) * 128], src_ap_list[k], identb[:])
            return last
        ser.step("tensor", tr)

        def ev(e):
            last = None
            for k in range(nb):
                last = e.tensor_copy(dst_ap_list[k], pTb[:, k * 128:(k + 1) * 128])
            return last
        ser.step("vector", ev)

    ld_consts()
    x_tiled = dx.ap().rearrange("(n p) d -> n p d", p=128)

    def ldx(e, sem):
        for t in range(NT):
            e.dma_start(out=RX[:, t * D:(t + 1) * D], in_=x_tiled[t]).then_inc(sem, 16)
    ser.dma_n("sync", NT, ldx)

    def block_L0_ffn():
        xln = RA[:, 0:NT * D]
        ln_full(RX, RA, 1e-5)
        tap("xln", RA[:, 0:16384], 16384)
        tap("xa", xln[:, 4 * D:5 * D], D)

        n = NT
        src3 = xln.rearrange("p (n w) -> p n w", w=D)
        for t in range(n):
            def sq(e, t=t):
                return e.activation(scratchq[:, 0:D], xln[:, t * D:(t + 1) * D],
                                    ACT.Square, accum_out=stat2[:, t:t + 1])
            ser.step("scalar", sq)

        tap("xb", xln[:, 4 * D:5 * D], D)

        for t in range(n):
            def amax_(e, t=t):
                return e.reduce_max(stat2[:, n + t:n + t + 1], xln[:, t * D:(t + 1) * D],
                                    AX, apply_absolute_value=True)
            ser.step("vector", amax_)
        tap("amax", stat2[:, 0:64], 64)

        tap("xc", xln[:, 4 * D:5 * D], D)

        def ms_(e):
            return e.tensor_scalar(stat2[:, 0:n], stat2[:, 0:n], 1.0 / D, 1e-6,
                                   ALU.mult, ALU.add)
        ser.step("vector", ms_)

        def rsq_(e):
            return e.activation(stat2[:, 0:n], stat2[:, 0:n], ACT.Sqrt)
        ser.step("scalar", rsq_)

        def sc1(e):
            return e.reciprocal(stat2[:, 2 * n:3 * n], stat2[:, 0:n])
        ser.step("vector", sc1)

        def sc2(e):
            e.tensor_tensor(stat2[:, 3 * n:4 * n], stat2[:, 2 * n:3 * n],
                            stat2[:, n:2 * n], ALU.mult)
            e.nop()
            return e.tensor_scalar(stat2[:, 3 * n:4 * n], stat2[:, 3 * n:4 * n],
                                   1e-5, None, ALU.max)
        ser.step("vector", sc2)

        def sc3(e):
            return e.reciprocal(stat2[:, 4 * n:5 * n], stat2[:, 3 * n:4 * n])
        ser.step("vector", sc3)

        def sc4(e):
            e.tensor_scalar(stat2[:, 5 * n:6 * n], stat2[:, 4 * n:5 * n],
                            127.0, None, ALU.mult)
            e.nop()
            return e.tensor_tensor(stat[:, 32:32 + n], stat2[:, 2 * n:3 * n],
                                   stat2[:, 5 * n:6 * n], ALU.mult)
        ser.step("vector", sc4)

        def sc5(e):
            return e.reciprocal(stat[:, 32 + n:32 + 2 * n], stat2[:, 5 * n:6 * n])
        ser.step("vector", sc5)

        tap("xd", xln[:, 4 * D:5 * D], D)
        xq_tm = vb16(RB, 0, NT * D)
        for t in range(NT):
            round_quant(xln[:, t * D:(t + 1) * D], xq_tm[:, t * D:(t + 1) * D],
                        D, stat[:, 32 + t:32 + t + 1], -128, 127)
            if t == 3:
                tap("xe", xln[:, 4 * D:5 * D], D)
            if t == 4:
                tap("xf", xln[:, 4 * D:5 * D], D)
        tap("xg", xln[:, 4 * D:5 * D], D)

        tap("stat2q", stat2[:, 0:128], 128)
        tap("scales", stat[:, 0:128], 128)
        tap("xln4", xln[:, 4 * D:5 * D], D)
        tap("xq_tm", RB[:, 0:8192], 8192)
        xq_fm = vb16(RA, 0, ND * T)
        for c in range(ND):
            for tq in range(0, NT, 8):
                transpose_b16(
                    [xq_tm[:, (tq + k) * D + c * 128:(tq + k) * D + (c + 1) * 128]
                     for k in range(8)],
                    [xq_fm[:, c * T + (tq + k) * 128: c * T + (tq + k + 1) * 128]
                     for k in range(8)])

        w0T = vb16(RA, 32768, ND * D)
        wstage_i = vi32(RB, 16384, 1024)
        wstage_b = vb16(RB, 20480, 1024)
        wi = W["L0_ffn_w_int8"].ap()
        for o in range(ND):
            def ldw(e, sem, o=o):
                e.dma_start(out=wstage_i, in_=wi[o * 128:(o + 1) * 128, :]).then_inc(sem, 16)
            ser.dma_n("sync", 1, ldw)

            def cast(e, o=o):
                return e.tensor_copy(wstage_b, wstage_i)
            ser.step("vector", cast)
            for dq in range(0, ND, 8):
                transpose_b16(
                    [wstage_b[:, (dq + k) * 128:(dq + k + 1) * 128] for k in range(8)],
                    [w0T[:, (dq + k) * D + o * 128:(dq + k) * D + (o + 1) * 128]
                     for k in range(8)])

        tap("xq_fm", RA[:, 0:8192], 8192)
        tap("w0T", RA[:, 8192:12288], 4096)
        def ldb(e, sem):
            e.dma_start(out=rowflat[0:1, 0:D], in_=W["L0_ffn_bias"].ap()).then_inc(sem, 16)
            e.dma_start(out=tiny[0:1, 0:1], in_=W["L0_ffn_weight_scale"].ap()).then_inc(sem, 16)
        ser.dma_n("sync", 2, ldb)
        replicate_row(rowflat[0:1, 0:D], rep4k, D)

        def repws(e):
            return e.matmul(pS[:, 0:1], onesrow[0:1, :], tiny[0:1, 0:1],
                            start=True, stop=True)
        ser.step("tensor", repws)

        def cout(e):
            e.tensor_copy(tiny[:, 1:2], pS[:, 0:1])
            e.nop()
            return e.tensor_scalar(stat[:, 32 + NT:32 + 2 * NT],
                                   stat[:, 32 + NT:32 + 2 * NT],
                                   tiny[:, 1:2], None, ALU.mult)
        ser.step("vector", cout)

        for t in range(NT):
            def mm(e, t=t):
                last = None
                for dd in range(ND):
                    for half in range(2):
                        last = e.matmul(
                            p1[:, half * 512:(half + 1) * 512],
                            xq_fm[:, dd * T + t * 128: dd * T + (t + 1) * 128],
                            w0T[:, dd * D + half * 512: dd * D + (half + 1) * 512],
                            start=(dd == 0), stop=(dd == ND - 1))
                return last
            ser.step("tensor", mm)

            def ev(e, t=t):
                return e.scalar_tensor_tensor(
                    scratch4k[:, 0:D], p1[:, 0:D],
                    stat[:, 32 + NT + t:32 + NT + t + 1], rep4k[:, 0:D],
                    ALU.mult, ALU.add)
            ser.step("vector", ev)

            if t == 0:
                tap("blk0", scratch4k[:, 0:D], D)
            ln_stats_tile(scratch4k[:, 0:D], 1e-5, 8)

            def res(e, t=t):
                e.tensor_scalar(scratch4k[:, 0:D], scratch4k[:, 0:D],
                                stat2[:, 8:9], stat2[:, 9:10], ALU.mult, ALU.add)
                return e.tensor_tensor(RX[:, t * D:(t + 1) * D],
                                       RX[:, t * D:(t + 1) * D],
                                       scratch4k[:, 0:D], ALU.add)
            ser.step("vector", res)


    def block_L1_ffn():
        n = NT
        xln = RA[:, 0:NT * D]
        ln_full(RX, RA, 1e-5)

        for t in range(n):
            def amax_(e, t=t):
                return e.reduce_max(stat2[:, t:t + 1], xln[:, t * D:(t + 1) * D],
                                    AX, apply_absolute_value=True)
            ser.step("vector", amax_)

        def sc2(e):
            return e.tensor_scalar(stat2[:, n:2 * n], stat2[:, 0:n], 1e-5, None, ALU.max)
        ser.step("vector", sc2)

        def sc3(e):
            return e.reciprocal(stat2[:, 2 * n:3 * n], stat2[:, n:2 * n])
        ser.step("vector", sc3)

        def sc4(e):
            e.tensor_scalar(stat[:, 32:32 + n], stat2[:, 2 * n:3 * n], 127.0, None, ALU.mult)
            e.nop()
            return e.tensor_scalar(stat[:, 32 + n:32 + 2 * n], stat2[:, n:2 * n],
                                   1.0 / 127.0, None, ALU.mult)
        ser.step("vector", sc4)

        xq1_tm = vb16(RB, 0, NT * D)
        for t in range(NT):
            round_quant(xln[:, t * D:(t + 1) * D], xq1_tm[:, t * D:(t + 1) * D],
                        D, stat[:, 32 + t:32 + t + 1], -128, 127)

        xq1_fm = vb16(RA, 0, ND * T)
        for c in range(ND):
            for tq in range(0, NT, 8):
                transpose_b16(
                    [xq1_tm[:, (tq + k) * D + c * 128:(tq + k) * D + (c + 1) * 128]
                     for k in range(8)],
                    [xq1_fm[:, c * T + (tq + k) * 128: c * T + (tq + k + 1) * 128]
                     for k in range(8)])

        xs_tiled = d_xspill.ap().rearrange("(n p) d -> n p d", p=128)

        def spx(e, sem):
            for t in range(NT):
                e.dma_start(out=xs_tiled[t], in_=RX[:, t * D:(t + 1) * D]).then_inc(sem, 16)
        ser.dma_n("scalar", NT, spx)

        def spq(e, sem):
            for c in range(ND):
                e.dma_start(out=d_xq1.ap()[c * 128:(c + 1) * 128, :],
                            in_=xq1_fm[:, c * T:(c + 1) * T]).then_inc(sem, 16)
        ser.dma_n("sync", ND, spq)

        w1qT = vb16(RX, 0, 8 * 4096)
        w2qT = vb16(RA, 0, 32 * 1024)
        wst = RB[:, 0:4096]
        wqb = vb16(RB, 16384, 4096)

        def prep_ternary(dw, nchunks, width, statbase, qcol, wT, wT_colf):
            for ci in range(nchunks):
                def ldw(e, sem, ci=ci):
                    e.dma_start(out=wst[:, 0:width],
                                in_=dw.ap()[ci * 128:(ci + 1) * 128, :]).then_inc(sem, 16)
                ser.dma_n("sync", 1, ldw)

                def rs(e, ci=ci):
                    return e.reduce_sum(stat2[:, statbase + ci:statbase + ci + 1],
                                        wst[:, 0:width], AX, apply_absolute_value=True)
                ser.step("vector", rs)

            def rs2(e):
                return e.reduce_sum(stat2[:, statbase + nchunks:statbase + nchunks + 1],
                                    stat2[:, statbase:statbase + nchunks], AX)
            ser.step("vector", rs2)

            def xp(e):
                return e.matmul(pS[0:1, 0:1],
                                stat2[:, statbase + nchunks:statbase + nchunks + 1],
                                onescol[:], start=True, stop=True)
            ser.step("tensor", xp)

            def mean_(e):
                e.tensor_copy(tiny[0:1, qcol:qcol + 1], pS[0:1, 0:1])
                e.nop()
                e.tensor_scalar(tiny[0:1, qcol:qcol + 1], tiny[0:1, qcol:qcol + 1],
                                1.0 / (nchunks * 128 * width), None, ALU.mult)
                e.nop()
                return e.tensor_scalar(tiny[0:1, qcol:qcol + 1], tiny[0:1, qcol:qcol + 1],
                                       1e-5, None, ALU.max)
            ser.step("vector", mean_)

            def recw(e):
                return e.reciprocal(tiny[0:1, qcol + 1:qcol + 2], tiny[0:1, qcol:qcol + 1])
            ser.step("vector", recw)

            def repw(e):
                e.matmul(pS[:, 2:3], onesrow[0:1, :], tiny[0:1, qcol + 1:qcol + 2],
                         start=True, stop=True)
                return e.matmul(pS[:, 3:4], onesrow[0:1, :], tiny[0:1, qcol:qcol + 1],
                                start=True, stop=True)
            ser.step("tensor", repw)

            def cpw(e):
                e.tensor_copy(tiny[:, qcol + 2:qcol + 3], pS[:, 2:3])
                return e.tensor_copy(tiny[:, qcol + 3:qcol + 4], pS[:, 3:4])
            ser.step("vector", cpw)

            for ci in range(nchunks):
                def ldw(e, sem, ci=ci):
                    e.dma_start(out=wst[:, 0:width],
                                in_=dw.ap()[ci * 128:(ci + 1) * 128, :]).then_inc(sem, 16)
                ser.dma_n("sync", 1, ldw)
                for g in range(0, width, 1024):
                    gw = min(1024, width - g)
                    round_quant(wst[:, g:g + gw], wqb[:, g:g + gw], gw,
                                tiny[:, qcol + 2:qcol + 3], -1, 1)
                nblk = width // 128
                for bq in range(0, nblk, 8):
                    nb = min(8, nblk - bq)
                    transpose_b16(
                        [wqb[:, (bq + k) * 128:(bq + k + 1) * 128] for k in range(nb)],
                        [wT[:, wT_colf(ci, bq + k)] for k in range(nb)])

        prep_ternary(W["L1_ffn_w1"], 32, 1024, 34, 8,
                     w1qT, lambda hh, dd: slice(dd * 4096 + hh * 128, dd * 4096 + (hh + 1) * 128))
        prep_ternary(W["L1_ffn_w2"], 8, 4096, 34, 16,
                     w2qT, lambda oo, hh: slice(hh * 1024 + oo * 128, hh * 1024 + (oo + 1) * 128))

        def ldcv(e, sem):
            e.dma_start(out=cw[:].rearrange("p (c k) -> p c k", k=3),
                        in_=W["L1_ffn_conv_w"].ap().rearrange("(c p) k -> p c k", p=128)
                        ).then_inc(sem, 16)
            e.dma_start(out=cb[:].rearrange("p (c k) -> p c k", k=1),
                        in_=W["L1_ffn_conv_b"].ap().rearrange("(c p) k -> p c k", p=128),
                        allow_slow_non_contiguous=True).then_inc(sem, 16)
        ser.dma_n("sync", 2, ldcv)

        def c1c(e):
            return e.tensor_scalar(stat[:, 64:64 + n], stat[:, 32 + n:32 + 2 * n],
                                   tiny[:, 8 + 3:8 + 4], None, ALU.mult)
        ser.step("vector", c1c)

        def c1t(e):
            return e.transpose(pS[0:16, 0:128], stat[:, 64:64 + n], ident[:])
        ser.step("tensor", c1t)

        def c1cp(e):
            return e.tensor_copy(rowbuf[:, :], pS[0:16, 0:128])
        ser.step("vector", c1cp)

        def c1fl(e, sem):
            e.dma_start(out=rowflat[0:1, :], in_=rowbuf[:, :]).then_inc(sem, 16)
        ser.dma_n("gpsimd", 1, c1fl)

        h1 = RB[:, 0:32 * 130]
        h2 = RB[:, 32 * 130:32 * 130 + 32 * 128]
        xq2 = vb16(RB, 0, 32 * 128)
        out_tiled2 = dout.ap().rearrange("(n p) d -> n p d", p=128)

        for c in range(NT):
            lo = c * 128 - 1
            hi = c * 128 + 129

            def ldq(e, sem, c=c, lo=lo, hi=hi):
                clo, chi = max(lo, 0), min(hi, T)
                off = clo - lo
                for dd in range(ND):
                    e.dma_start(out=xq1c[:, dd * 130 + off: dd * 130 + off + (chi - clo)],
                                in_=d_xq1.ap()[dd * 128:(dd + 1) * 128, clo:chi]
                                ).then_inc(sem, 16)
            ser.dma_n("sync", ND, ldq)

            if c == 0 or c == NT - 1:
                def mz(e, c=c):
                    col = 0 if c == 0 else 129
                    last = None
                    for dd in range(ND):
                        last = e.memset(xq1c[:, dd * 130 + col: dd * 130 + col + 1], 0.0)
                    return last
                ser.step("vector", mz)

            def c1r(e, c=c, lo=lo):
                slo = max(lo, 0)
                width = 130 if 0 <= lo and lo + 130 <= T else 129
                return e.matmul(p2[:, 0:width], onesrow[0:1, :],
                                rowflat[0:1, slo:slo + width], start=True, stop=True)
            ser.step("tensor", c1r)

            def c1rc(e, c=c, lo=lo):
                slo = max(lo, 0)
                off = slo - lo
                width = 130 if 0 <= lo and lo + 130 <= T else 129
                last = e.tensor_copy(c1rep[:, off:off + width], p2[:, 0:width])
                if off:
                    last = e.memset(c1rep[:, 0:1], 0.0)
                if lo + 130 > T:
                    last = e.memset(c1rep[:, 129:130], 0.0)
                return last
            ser.step("vector", c1rc)

            for hb in range(16):
                def mm1(e, hb=hb):
                    last = None
                    for kk in range(2):
                        hh = hb * 2 + kk
                        for dd in range(ND):
                            last = e.matmul(p1[:, kk * 512: kk * 512 + 130],
                                            w1qT[:, dd * 4096 + hh * 128: dd * 4096 + (hh + 1) * 128],
                                            xq1c[:, dd * 130:(dd + 1) * 130],
                                            start=(dd == 0), stop=(dd == ND - 1))
                    return last
                ser.step("tensor", mm1)

                def ev1(e, hb=hb):
                    last = None
                    for kk in range(2):
                        hh = hb * 2 + kk
                        last = e.tensor_tensor(h1[:, hh * 130:(hh + 1) * 130],
                                               p1[:, kk * 512: kk * 512 + 130],
                                               c1rep[:, 0:130], ALU.mult)
                    return last
                ser.step("vector", ev1)

            for hh in range(0, 32, 4):
                def cv(e, hh=hh):
                    last = None
                    for k in range(4):
                        h = hh + k
                        e.tensor_scalar(h2[:, h * 128:(h + 1) * 128],
                                        h1[:, h * 130 + 1: h * 130 + 129],
                                        cw[:, h * 3 + 1:h * 3 + 2], cb[:, h:h + 1],
                                        ALU.mult, ALU.add)
                        e.scalar_tensor_tensor(h2[:, h * 128:(h + 1) * 128],
                                               h1[:, h * 130 + 0: h * 130 + 128],
                                               cw[:, h * 3 + 0:h * 3 + 1],
                                               h2[:, h * 128:(h + 1) * 128],
                                               ALU.mult, ALU.add)
                        last = e.scalar_tensor_tensor(h2[:, h * 128:(h + 1) * 128],
                                                      h1[:, h * 130 + 2: h * 130 + 130],
                                                      cw[:, h * 3 + 2:h * 3 + 3],
                                                      h2[:, h * 128:(h + 1) * 128],
                                                      ALU.mult, ALU.add)
                    return last
                ser.step("vector", cv)

            def sl(e):
                return e.activation(h2[:, 0:4096], h2[:, 0:4096], ACT.Silu)
            ser.step("scalar", sl)

            for g in range(8):
                def trp(e, g=g):
                    last = None
                    for k in range(4):
                        last = e.transpose(pT[:, k * 128:(k + 1) * 128],
                                          h2[:, (g * 4 + k) * 128:(g * 4 + k + 1) * 128],
                                          ident[:])
                    return last
                ser.step("tensor", trp)

                def rd(e, g=g):
                    return e.reduce_max(stat2[:, 64 + g:64 + g + 1], pT[:, 0:512],
                                        AX, apply_absolute_value=True)
                ser.step("vector", rd)

            def rmx(e):
                return e.reduce_max(stat2[:, 72:73], stat2[:, 64:72], AX)
            ser.step("vector", rmx)

            def s2a(e):
                e.tensor_scalar(stat2[:, 73:74], stat2[:, 72:73], 1e-5, None, ALU.max)
                e.nop()
                e.tensor_scalar(stat2[:, 74:75], stat2[:, 73:74], 1.0 / 127.0, None, ALU.mult)
                e.nop()
                return e.tensor_scalar(stat2[:, 75:76], stat2[:, 74:75],
                                       tiny[:, 16 + 3:16 + 4], None, ALU.mult)
            ser.step("vector", s2a)

            def s2b(e):
                return e.reciprocal(stat2[:, 76:77], stat2[:, 73:74])
            ser.step("vector", s2b)

            def s2c(e):
                return e.tensor_scalar(stat2[:, 77:78], stat2[:, 76:77], 127.0, None, ALU.mult)
            ser.step("vector", s2c)

            def s2t(e):
                return e.transpose(pS[0:1, 0:128], stat2[:, 77:78], ident[:])
            ser.step("tensor", s2t)

            def s2cp(e):
                return e.tensor_copy(rowbuf[0:1, 0:128], pS[0:1, 0:128])
            ser.step("vector", s2cp)

            def s2r(e):
                return e.matmul(p2[:, 0:128], onesrow[0:1, :], rowbuf[0:1, 0:128],
                                start=True, stop=True)
            ser.step("tensor", s2r)

            def s2rc(e):
                return e.tensor_copy(s2rep[:, :], p2[:, 0:128])
            ser.step("vector", s2rc)

            for g in range(4):
                def qg(e, g=g):
                    w = 1024
                    src3 = h2[:, g * w:(g + 1) * w].rearrange("p (a b) -> p a b", b=128)
                    s3 = s2rep[:, :].unsqueeze(1).broadcast_to((128, 8, 128))
                    y3 = scratch4k[:, 0:w].rearrange("p (a b) -> p a b", b=128)
                    e.tensor_tensor(y3, src3, s3, ALU.mult)
                    y = scratch4k[:, 0:w]
                    ecol = scratchq[:, 0:w]
                    cm = scratchq[:, 1024:1024 + w]
                    ci = scratchqi[:, 0:w]
                    e.tensor_copy(ci, y)
                    e.tensor_scalar(cm, ci, -1.0, None, ALU.add)
                    e.tensor_tensor(ecol, y, cm, ALU.subtract)
                    e.scalar_tensor_tensor(cm, ecol, 0.5, cm, ALU.is_ge, ALU.add)
                    e.scalar_tensor_tensor(cm, ecol, 1.5, cm, ALU.is_ge, ALU.add)
                    return e.tensor_scalar(xq2[:, g * w:(g + 1) * w], cm,
                                           -128.0, 127.0, ALU.max, ALU.min)
                ser.step("vector", qg)

            def mm2(e):
                last = None
                for hh in range(32):
                    for half in range(2):
                        last = e.matmul(p1[:, half * 512:(half + 1) * 512],
                                        xq2[:, hh * 128:(hh + 1) * 128],
                                        w2qT[:, hh * 1024 + half * 512: hh * 1024 + (half + 1) * 512],
                                        start=(hh == 0), stop=(hh == 31))
                return last
            ser.step("tensor", mm2)

            def ldx2(e, sem, c=c):
                e.dma_start(out=scratchx[:, 0:D], in_=xs_tiled[c]).then_inc(sem, 16)
            ser.dma_n("scalar", 1, ldx2)

            def ev2(e, c=c):
                return e.tensor_scalar(scratch4k[:, 0:D], p1[:, 0:D],
                                       stat2[:, 75:76], None, ALU.mult)
            ser.step("vector", ev2)

            ln_stats_tile(scratch4k[:, 0:D], 1e-5, 8)

            def res2(e, c=c):
                e.tensor_scalar(scratch4k[:, 0:D], scratch4k[:, 0:D],
                                stat2[:, 8:9], stat2[:, 9:10], ALU.mult, ALU.add)
                e.nop()
                return e.tensor_tensor(scratch4k[:, 0:D], scratchx[:, 0:D],
                                       scratch4k[:, 0:D], ALU.add)
            ser.step("vector", res2)

            def stc(e, sem, c=c):
                e.dma_start(out=out_tiled2[c], in_=scratch4k[:, 0:D]).then_inc(sem, 16)
            ser.dma_n("gpsimd", 1, stc)


    def block_L1_attn():
        n = NT
        IN = 512

        xln = RA[:, 0:NT * D]
        ln_full(RX, RA, 1e-5)

        xs_tiled = d_xspill.ap().rearrange("(n p) d -> n p d", p=128)

        def spx(e, sem):
            for t in range(NT):
                e.dma_start(out=xs_tiled[t], in_=RX[:, t * D:(t + 1) * D]).then_inc(sem, 16)
        ser.dma_n("scalar", NT, spx)

        xf_hi = vb16(RX, 0, ND * T)
        xf_lo = vb16(RX, 32768, ND * T)
        for c in range(ND):
            for tq in range(0, NT, 4):
                def tr(e, c=c, tq=tq):
                    last = None
                    for k in range(4):
                        t = tq + k
                        last = e.transpose(pT[:, k * 128:(k + 1) * 128],
                                           xln[:, t * D + c * 128: t * D + (c + 1) * 128],
                                           ident[:])
                    return last
                ser.step("tensor", tr)

                def ev(e, c=c, tq=tq):
                    e.tensor_copy(xf_hi[:, c * T + tq * 128: c * T + (tq + 4) * 128],
                                  pT[:, 0:512])
                    e.nop()
                    return e.scalar_tensor_tensor(
                        xf_lo[:, c * T + tq * 128: c * T + (tq + 4) * 128],
                        pT[:, 0:512], -1.0,
                        xf_hi[:, c * T + tq * 128: c * T + (tq + 4) * 128],
                        ALU.bypass, ALU.subtract)
                ser.step("vector", ev)

        if l1a_stop <= 1:
            return
        def ldsel(e, sem):
            e.dma_start(out=selq[:].rearrange("p (c h) -> p c h", h=8),
                        in_=W["sel_qk"].ap().rearrange("(c p) h -> p c h", p=128)
                        ).then_inc(sem, 16)
            e.dma_start(out=selrep[:], in_=W["sel_rep"].ap()).then_inc(sem, 16)
            e.dma_start(out=lachan[:, 0:4].rearrange("p (c k) -> p c k", k=1),
                        in_=W["L1_attn_alpha"].ap().rearrange("(c p) k -> p c k", p=128),
                        allow_slow_non_contiguous=True).then_inc(sem, 16)
            e.dma_start(out=lachan[:, 12:16].rearrange("p (c k) -> p c k", k=1),
                        in_=W["L1_attn_aug"].ap().rearrange("(c p) k -> p c k", p=128),
                        allow_slow_non_contiguous=True).then_inc(sem, 16)
            e.dma_start(out=lachan[:, 16:20].rearrange("p (c k) -> p c k", k=1),
                        in_=W["L1_attn_bq"].ap().rearrange("(c p) k -> p c k", p=128),
                        allow_slow_non_contiguous=True).then_inc(sem, 16)
            e.dma_start(out=lachan[:, 20:24].rearrange("p (c k) -> p c k", k=1),
                        in_=W["L1_attn_bk"].ap().rearrange("(c p) k -> p c k", p=128),
                        allow_slow_non_contiguous=True).then_inc(sem, 16)
            e.dma_start(out=lachan[:, 24:28].rearrange("p (c k) -> p c k", k=1),
                        in_=W["L1_attn_bv"].ap().rearrange("(c p) k -> p c k", p=128),
                        allow_slow_non_contiguous=True).then_inc(sem, 16)
        ser.dma_n("sync", 7, ldsel)

        def mkdec(e):
            e.activation(lachan[:, 4:8], lachan[:, 0:4], ACT.Sigmoid)
            return e.activation(lachan[:, 8:12], lachan[:, 0:4], ACT.Sigmoid, scale=-1.0)
        ser.step("scalar", mkdec)

        wstg = RB[:, 0:4096]
        w_hi = vb16(RB, 16384, 4096)
        w_lo = vb16(RB, 24576, 4096)

        def prep_w(dw):
            def ldw(e, sem):
                for dd in range(ND):
                    e.dma_start(out=wstg[:, dd * IN:(dd + 1) * IN],
                                in_=dw.ap()[dd * 128:(dd + 1) * 128, :]).then_inc(sem, 16)
            ser.dma_n("sync", ND, ldw)

            def sp(e):
                e.tensor_copy(w_hi[:, :], wstg[:, :])
                e.nop()
                return e.scalar_tensor_tensor(w_lo[:, :], wstg[:, :], -1.0,
                                              w_hi[:, :], ALU.bypass, ALU.subtract)
            ser.step("vector", sp)

        def project(dst_f32, bias_col):
            for cc in range(4):
                for tc in range(4):
                    def mm(e, cc=cc, tc=tc):
                        last = None
                        ops = [(dd, a, b) for dd in range(ND)
                               for (a, b) in ((xf_hi, w_hi), (xf_lo, w_hi), (xf_hi, w_lo))]
                        for i, (dd, a, b) in enumerate(ops):
                            last = e.matmul(
                                p1[:, 0:512],
                                b[:, dd * IN + cc * 128: dd * IN + (cc + 1) * 128],
                                a[:, dd * T + tc * 512: dd * T + (tc + 1) * 512],
                                start=(i == 0), stop=(i == len(ops) - 1))
                        return last
                    ser.step("tensor", mm)

                    def ev(e, cc=cc, tc=tc):
                        return e.tensor_scalar(
                            dst_f32[:, cc * T + tc * 512: cc * T + (tc + 1) * 512],
                            p1[:, 0:512], 1.0, lachan[:, bias_col + cc:bias_col + cc + 1],
                            ALU.mult, ALU.add)
                    ser.step("vector", ev)

        if l1a_stop <= 2:
            return
        q_f = RA[:, 0:4 * T]
        k_f = RA[:, 4 * T:8 * T]
        prep_w(W["L1_attn_Wq"])
        project(q_f, 16)
        prep_w(W["L1_attn_Wk"])
        project(k_f, 20)

        tap("qf", q_f[:, 0:8192], 8192)
        tap("kf", k_f[:, 0:8192], 8192)
        if l1a_stop <= 3:
            return
        qkbuf = RB[:, 0:2048]
        scsb = RB[:, 2048:2048 + 2048]
        for cc in range(4):
            def qk(e, cc=cc):
                return e.tensor_tensor(qkbuf[:, :], q_f[:, cc * T:(cc + 1) * T],
                                       k_f[:, cc * T:(cc + 1) * T], ALU.mult)
            ser.step("vector", qk)

            def scm(e, cc=cc):
                last = None
                for tc in range(4):
                    pp = p1 if tc < 2 else p2
                    last = e.matmul(pp[0:8, (tc % 2) * 512:(tc % 2) * 512 + 512],
                                    selq[:, cc * 8:(cc + 1) * 8],
                                    qkbuf[:, tc * 512:(tc + 1) * 512],
                                    start=(cc == 0), stop=(cc == 3))
                return last
            ser.step("tensor", scm)

        def sccp(e):
            e.tensor_copy(scsb[0:8, 0:1024], p1[0:8, 0:1024])
            return e.tensor_copy(scsb[0:8, 1024:2048], p2[0:8, 0:1024])
        ser.step("vector", sccp)

        def sct(e):
            last = None
            for t in range(NT):
                last = e.transpose(pT[:, t * 8:(t + 1) * 8],
                                   scsb[0:8, t * 128:(t + 1) * 128], ident[0:8, 0:8])
            return last
        ser.step("tensor", sct)

        sctm = scratchq[:, 0:128]

        def sce(e):
            return e.activation(sctm, pT[:, 0:128], ACT.Exp, scale=0.125)
        ser.step("scalar", sce)

        def scs(e):
            return e.reduce_sum(scratchq[:, 256:272].unsqueeze(2),
                                sctm.rearrange("p (a b) -> p a b", b=8), AX)
        ser.step("vector", scs)

        def scr(e):
            return e.reciprocal(scratchq[:, 272:288], scratchq[:, 256:272])
        ser.step("vector", scr)

        def dnt(e):
            return e.transpose(pS[0:16, 0:128], scratchq[:, 272:288], ident[:])
        ser.step("tensor", dnt)

        def dnc(e):
            return e.tensor_copy(rowbuf[:, :], pS[0:16, 0:128])
        ser.step("vector", dnc)

        def dnf(e, sem):
            e.dma_start(out=rowflat[0:1, :], in_=rowbuf[:, :]).then_inc(sem, 16)
        ser.dma_n("gpsimd", 1, dnf)

        attn_fm = scratchq[0:8, :]

        def efm(e):
            return e.activation(attn_fm[0:8, :], scsb[0:8, 0:2048], ACT.Exp, scale=0.125)
        ser.step("scalar", efm)

        def dnr(e):
            last = None
            for tc in range(4):
                pp = p1 if tc < 2 else p2
                last = e.matmul(pp[0:8, (tc % 2) * 512:(tc % 2) * 512 + 512],
                                onesrow[0:1, 0:8], rowflat[0:1, tc * 512:(tc + 1) * 512],
                                start=True, stop=True)
            return last
        ser.step("tensor", dnr)

        def anm(e):
            e.tensor_tensor(attn_fm[0:8, 0:1024], attn_fm[0:8, 0:1024],
                            p1[0:8, 0:1024], ALU.mult)
            e.nop()
            return e.tensor_tensor(attn_fm[0:8, 1024:2048], attn_fm[0:8, 1024:2048],
                                   p2[0:8, 0:1024], ALU.mult)
        ser.step("vector", anm)

        tap("attntm", scratchq[:, 0:512], 512)
        if l1a_stop <= 4:
            return
        v_f = k_f
        prep_w(W["L1_attn_Wv"])
        project(v_f, 24)
        tap("vf", v_f[:, 0:8192], 8192)
        tap("afm", scratchq[:, 0:2048], 2048)

        tap("vf", v_f[:, 0:8192], 8192)
        tap("afm", scsb[:, 0:2048], 2048)
        tap("lach", lachan[:, 0:32], 32)
        u_f = RB[:, 0:4 * T]
        for cc in range(4):
            def arm(e, cc=cc):
                last = None
                for tc in range(4):
                    pp = p1 if tc < 2 else p2
                    last = e.matmul(pp[:, (tc % 2) * 512:(tc % 2) * 512 + 512],
                                    selrep[0:8, cc * 128:(cc + 1) * 128],
                                    attn_fm[0:8, tc * 512:(tc + 1) * 512],
                                    start=True, stop=True)
                return last
            ser.step("tensor", arm)

            def mku(e, cc=cc):
                e.scalar_tensor_tensor(u_f[:, cc * T:cc * T + 1024], p1[:, 0:1024],
                                       lachan[:, 8 + cc:8 + cc + 1],
                                       v_f[:, cc * T:cc * T + 1024],
                                       ALU.mult, ALU.mult)
                e.nop()
                return e.scalar_tensor_tensor(u_f[:, cc * T + 1024:(cc + 1) * T], p2[:, 0:1024],
                                              lachan[:, 8 + cc:8 + cc + 1],
                                              v_f[:, cc * T + 1024:(cc + 1) * T],
                                              ALU.mult, ALU.mult)
            ser.step("vector", mku)

        if l1a_stop <= 5:
            return
        st_f = RX[:, 0:4 * T]
        for cc in range(4):
            def scan(e, cc=cc):
                return e.tensor_tensor_scan(
                    st_f[:, cc * T:(cc + 1) * T],
                    lachan[:, 4 + cc:4 + cc + 1].broadcast_to((128, T)),
                    u_f[:, cc * T:(cc + 1) * T],
                    0.0, ALU.mult, ALU.add)
            ser.step("vector", scan)

        tap("stf", st_f[:, 0:8192], 8192)
        tap("uf", u_f[:, 0:8192], 8192)
        if l1a_stop <= 6:
            return
        o_f = u_f
        for cc in range(4):
            def og1(e, cc=cc):
                return e.tensor_scalar(scratchq[:, 0:T], q_f[:, cc * T:(cc + 1) * T],
                                       lachan[:, 12 + cc:12 + cc + 1], None, ALU.mult)
            ser.step("vector", og1)

            def og2(e, cc=cc):
                return e.activation(scratchq[:, 0:T], scratchq[:, 0:T], ACT.Sigmoid)
            ser.step("scalar", og2)

            def og3(e, cc=cc):
                e.tensor_tensor(o_f[:, cc * T:(cc + 1) * T], q_f[:, cc * T:(cc + 1) * T],
                                st_f[:, cc * T:(cc + 1) * T], ALU.mult)
                e.nop()
                return e.tensor_tensor(o_f[:, cc * T:(cc + 1) * T],
                                       o_f[:, cc * T:(cc + 1) * T],
                                       scratchq[:, 0:T], ALU.add)
            ser.step("vector", og3)

        tap("of", o_f[:, 0:8192], 8192)
        if l1a_stop <= 7:
            return
        o_hi = vb16(RX, 32768, 4 * T)
        o_lo = vb16(RX, 49152, 4 * T)

        def osp(e):
            e.tensor_copy(o_hi[:, :], o_f[:, :])
            e.nop()
            return e.scalar_tensor_tensor(o_lo[:, :], o_f[:, :], -1.0,
                                          o_hi[:, :], ALU.bypass, ALU.subtract)
        ser.step("vector", osp)

        wo_st = RA[:, 0:4096]
        wo_hi = vb16(RA, 16384, 4096)
        wo_lo = vb16(RA, 24576, 4096)

        def ldwo(e, sem):
            for cc in range(4):
                e.dma_start(out=wo_st[:, cc * 1024:(cc + 1) * 1024],
                            in_=W["L1_attn_Wo"].ap()[cc * 128:(cc + 1) * 128, :]
                            ).then_inc(sem, 16)
        ser.dma_n("sync", 4, ldwo)

        def spwo(e):
            e.tensor_copy(wo_hi[:, :], wo_st[:, :])
            e.nop()
            return e.scalar_tensor_tensor(wo_lo[:, :], wo_st[:, :], -1.0,
                                          wo_hi[:, :], ALU.bypass, ALU.subtract)
        ser.step("vector", spwo)

        def ldg(e, sem):
            e.dma_start(out=rowflat[0:1, 0:D], in_=W["L1_attn_ln_scale"].ap()).then_inc(sem, 16)
            e.dma_start(out=rowflat[0:1, D:2 * D], in_=W["L1_attn_ln_bias"].ap()).then_inc(sem, 16)
        ser.dma_n("sync", 2, ldg)
        replicate_row(rowflat[0:1, 0:D], rep4k, D)
        replicate_row(rowflat[0:1, D:2 * D], rep4k2, D)

        def ldbo(e, sem):
            e.dma_start(out=rowflat[0:1, 0:D], in_=W["L1_attn_bo"].ap()).then_inc(sem, 16)
        ser.dma_n("sync", 1, ldbo)

        if l1a_stop <= 8:
            return
        xn_tiled = d_xspill2.ap().rearrange("(n p) d -> n p d", p=128)
        for t in range(NT):
            def mmo(e, t=t):
                last = None
                for cc in range(4):
                    for (a, b) in ((o_hi, wo_hi), (o_lo, wo_hi), (o_hi, wo_lo)):
                        for half in range(2):
                            last = e.matmul(
                                p1[:, half * 512:(half + 1) * 512],
                                a[:, cc * T + t * 128: cc * T + (t + 1) * 128],
                                b[:, cc * 1024 + half * 512: cc * 1024 + (half + 1) * 512],
                                start=(cc == 0 and a is o_hi and b is wo_hi),
                                stop=False)
                for half in range(2):
                    last = e.matmul(p1[:, half * 512:(half + 1) * 512],
                                    onesrow[0:1, :],
                                    rowflat[0:1, half * 512:(half + 1) * 512],
                                    start=False, stop=True)
                return last
            ser.step("tensor", mmo)
            if l1a_stop <= 9:
                continue

            def evo(e, t=t):
                return e.tensor_copy(scratch4k[:, 0:D], p1[:, 0:D])
            ser.step("vector", evo)

            ln_stats_tile(scratch4k[:, 0:D], 1e-6, 8)
            if l1a_stop <= 10:
                continue

            def inorm(e, t=t):
                e.tensor_scalar(scratch4k[:, 0:D], scratch4k[:, 0:D],
                                stat2[:, 8:9], stat2[:, 9:10], ALU.mult, ALU.add)
                e.nop()
                e.tensor_tensor(scratch4k[:, 0:D], scratch4k[:, 0:D], rep4k[:, 0:D], ALU.mult)
                e.nop()
                return e.tensor_tensor(scratch4k[:, 0:D], scratch4k[:, 0:D],
                                       rep4k2[:, 0:D], ALU.add)
            ser.step("vector", inorm)
            if l1a_stop <= 11:
                continue

            def ldxt(e, sem, t=t):
                e.dma_start(out=scratchx[:, 0:D], in_=xs_tiled[t]).then_inc(sem, 16)
            ser.dma_n("scalar", 1, ldxt)

            ln_stats_tile(scratch4k[:, 0:D], 1e-5, 16)
            if l1a_stop <= 12:
                continue

            def res3(e, t=t):
                e.tensor_scalar(scratch4k[:, 0:D], scratch4k[:, 0:D],
                                stat2[:, 16:17], stat2[:, 17:18], ALU.mult, ALU.add)
                e.nop()
                return e.tensor_tensor(scratch4k[:, 0:D], scratchx[:, 0:D],
                                       scratch4k[:, 0:D], ALU.add)
            ser.step("vector", res3)

            def stxt(e, sem, t=t):
                e.dma_start(out=xn_tiled[t], in_=scratch4k[:, 0:D]).then_inc(sem, 16)
            ser.dma_n("gpsimd", 1, stxt)

        if l1a_stop <= 13:
            return
        def relx(e, sem):
            for t in range(NT):
                e.dma_start(out=RX[:, t * D:(t + 1) * D], in_=xn_tiled[t]).then_inc(sem, 16)
        ser.dma_n("sync", NT, relx)


    def block_L0_attn():
        n = NT
        xln = RA[:, 0:NT * D]
        ln_full(RX, RA, 1e-5)

        xs_tiled = d_xspill.ap().rearrange("(n p) d -> n p d", p=128)

        def spx(e, sem):
            for t in range(NT):
                e.dma_start(out=xs_tiled[t], in_=RX[:, t * D:(t + 1) * D]).then_inc(sem, 16)
        ser.dma_n("scalar", NT, spx)

        xf_hi = vb16(RX, 0, ND * T)
        xf_lo = vb16(RX, 32768, ND * T)
        for c in range(ND):
            for tq in range(0, NT, 4):
                def tr(e, c=c, tq=tq):
                    last = None
                    for k in range(4):
                        t = tq + k
                        last = e.transpose(pT[:, k * 128:(k + 1) * 128],
                                           xln[:, t * D + c * 128: t * D + (c + 1) * 128],
                                           ident[:])
                    return last
                ser.step("tensor", tr)

                def ev(e, c=c, tq=tq):
                    e.tensor_copy(xf_hi[:, c * T + tq * 128: c * T + (tq + 4) * 128],
                                  pT[:, 0:512])
                    e.nop()
                    return e.scalar_tensor_tensor(
                        xf_lo[:, c * T + tq * 128: c * T + (tq + 4) * 128],
                        pT[:, 0:512], -1.0,
                        xf_hi[:, c * T + tq * 128: c * T + (tq + 4) * 128],
                        ALU.bypass, ALU.subtract)
                ser.step("vector", ev)

        PF32 = vf32(RA, 0, 2048)
        q_hi = vb16(RA, 8192, 2048)
        q_lo = vb16(RA, 12288, 2048)
        k_hi = vb16(RA, 16384, 2048)
        k_lo = vb16(RA, 20480, 2048)
        wT_hi = {"q": vb16(RA, 24576, 1024), "k": vb16(RA, 28672, 1024),
                 "v": vb16(RA, 32768, 1024)}
        wT_lo = {"q": vb16(RA, 26624, 1024), "k": vb16(RA, 30720, 1024),
                 "v": vb16(RA, 34816, 1024)}
        wstage = vf32(RA, 36864, 1024)
        vbuf = vb16(RB, 0, 16 * 260)
        E_f = vf32(RB, 8320, 1024)
        E_hi = vb16(RB, 12416, 1024)
        E_lo = vb16(RB, 14464, 1024)
        obuf = vf32(RB, 16512, 8 * 129)
        onorm = vf32(RB, 20640, 8 * 64)
        asm = vf32(RB, 22688, 1024)

        def vones(e):
            last = None
            for j in range(16):
                e.memset(vbuf[:, j * 260 + 128: j * 260 + 129], 1.0)
                last = e.memset(vbuf[:, j * 260 + 257: j * 260 + 258], 1.0)
            return last
        ser.step("vector", vones)

        def oslot(qc):
            if qc < 6:
                off = (qc // 3) * 512 + (qc % 3) * 170
                return p2[:, off:off + 129]
            off = (qc - 6) * 170
            return pS[:, off:off + 129]

        for pair in range(8):
            for wname, wdram in (("q", W["L0_attn_Wq"]), ("k", W["L0_attn_Wk"]),
                                 ("v", W["L0_attn_Wv"])):
                def ldw(e, sem, wdram=wdram, pair=pair):
                    e.dma_start(out=wstage[:, :],
                                in_=wdram.ap()[pair * 128:(pair + 1) * 128, :]
                                ).then_inc(sem, 16)
                ser.dma_n("sync", 1, ldw)
                for dq in range(0, 8, 4):
                    def trw(e, dq=dq):
                        last = None
                        for kk in range(4):
                            last = e.transpose(pT[:, kk * 128:(kk + 1) * 128],
                                               wstage[:, (dq + kk) * 128:(dq + kk + 1) * 128],
                                               ident[:])
                        return last
                    ser.step("tensor", trw)

                    def evw(e, dq=dq, wname=wname):
                        e.tensor_copy(wT_hi[wname][:, dq * 128:(dq + 4) * 128], pT[:, 0:512])
                        e.nop()
                        return e.scalar_tensor_tensor(
                            wT_lo[wname][:, dq * 128:(dq + 4) * 128], pT[:, 0:512], -1.0,
                            wT_hi[wname][:, dq * 128:(dq + 4) * 128],
                            ALU.bypass, ALU.subtract)
                    ser.step("vector", evw)

            def proj(wname):
                for tc in range(4):
                    def mm(e, tc=tc, wname=wname):
                        last = None
                        ops = [(dd, a, b) for dd in range(ND)
                               for (a, b) in ((xf_hi, wT_hi[wname]), (xf_lo, wT_hi[wname]),
                                              (xf_hi, wT_lo[wname]))]
                        for i, (dd, a, b) in enumerate(ops):
                            last = e.matmul(
                                p1[:, 0:512],
                                b[:, dd * 128:(dd + 1) * 128],
                                a[:, dd * T + tc * 512: dd * T + (tc + 1) * 512],
                                start=(i == 0), stop=(i == len(ops) - 1))
                        return last
                    ser.step("tensor", mm)

                    def ev(e, tc=tc):
                        return e.tensor_copy(PF32[:, tc * 512:(tc + 1) * 512], p1[:, 0:512])
                    ser.step("vector", ev)

            proj("q")

            def spq(e):
                e.tensor_copy(q_hi[:, :], PF32[:, :])
                e.nop()
                return e.scalar_tensor_tensor(q_lo[:, :], PF32[:, :], -1.0,
                                              q_hi[:, :], ALU.bypass, ALU.subtract)
            ser.step("vector", spq)

            proj("k")

            def spk(e):
                e.tensor_copy(k_hi[:, :], PF32[:, :])
                e.nop()
                return e.scalar_tensor_tensor(k_lo[:, :], PF32[:, :], -1.0,
                                              k_hi[:, :], ALU.bypass, ALU.subtract)
            ser.step("vector", spk)

            proj("v")
            for jq in range(0, 16, 4):
                def trv(e, jq=jq):
                    last = None
                    for kk in range(4):
                        j = jq + kk
                        last = e.transpose(pT[:, kk * 128:(kk + 1) * 128],
                                           PF32[:, j * 128:(j + 1) * 128], ident[:])
                    return last
                ser.step("tensor", trv)

                def evv(e, jq=jq):
                    last = None
                    for kk in range(4):
                        j = jq + kk
                        src = pT[:, kk * 128:(kk + 1) * 128]
                        e.tensor_copy(vbuf[:, j * 260 + 0: j * 260 + 64], src[:, 0:64])
                        e.tensor_copy(vbuf[:, j * 260 + 129: j * 260 + 193], src[:, 64:128])
                        e.nop()
                        e.scalar_tensor_tensor(vbuf[:, j * 260 + 64: j * 260 + 128],
                                               src[:, 0:64], -1.0,
                                               vbuf[:, j * 260 + 0: j * 260 + 64],
                                               ALU.bypass, ALU.subtract)
                        last = e.scalar_tensor_tensor(vbuf[:, j * 260 + 193: j * 260 + 257],
                                                      src[:, 64:128], -1.0,
                                                      vbuf[:, j * 260 + 129: j * 260 + 193],
                                                      ALU.bypass, ALU.subtract)
                    return last
                ser.step("vector", evv)

            for hp in range(2):
                ch0 = 64 * hp
                h = pair * 2 + hp
                for qq in range(4):
                    def oslot(qc):
                        return (p2[:, 0:129], p2[:, 512:641],
                                pS[:, 0:129], pT[:, 0:129])[qc]

                    def mk_smm(jb):
                        def smm(e, hp=hp, qq=qq, jb=jb, ch0=ch0):
                            last = None
                            for jj in range(2):
                                j = jb * 2 + jj
                                for pi, (ka, qa) in enumerate(((k_hi, q_hi), (k_hi, q_lo),
                                                               (k_lo, q_hi))):
                                    last = e.matmul(
                                        p1[:, jj * 512:(jj + 1) * 512],
                                        ka[ch0:ch0 + 64, j * 128:(j + 1) * 128],
                                        qa[ch0:ch0 + 64, qq * 512:(qq + 1) * 512],
                                        start=(pi == 0), stop=(pi == 2),
                                        tile_position=(ch0, 0))
                            return last
                        return smm

                    def mk_avmm(jb):
                        def avmm(e, hp=hp, jb=jb):
                            last = None
                            for jj in range(2):
                                j = jb * 2 + jj
                                for qc in range(4):
                                    last = e.matmul(oslot(qc),
                                                    E_hi[:, jj * 512 + qc * 128: jj * 512 + (qc + 1) * 128],
                                                    vbuf[:, j * 260 + hp * 129: j * 260 + hp * 129 + 129],
                                                    start=(j == 0), stop=False)
                                    last = e.matmul(oslot(qc),
                                                    E_lo[:, jj * 512 + qc * 128: jj * 512 + (qc + 1) * 128],
                                                    vbuf[:, j * 260 + hp * 129: j * 260 + hp * 129 + 129],
                                                    start=False, stop=(j == 15))
                            return last
                        return avmm

                    for jb in range(8):
                        if jb == 0:
                            ser.step("tensor", mk_smm(0))
                        else:
                            def merged(e, a=mk_avmm(jb - 1), s=mk_smm(jb)):
                                a(e)
                                return s(e)
                            ser.step("tensor", merged)

                        def sexp(e):
                            return e.activation(E_f[:, 0:1024], p1[:, 0:1024], ACT.Exp,
                                                scale=0.125)
                        ser.step("scalar", sexp)

                        def ssplit(e):
                            e.tensor_copy(E_hi[:, 0:1024], E_f[:, 0:1024])
                            e.nop()
                            return e.scalar_tensor_tensor(E_lo[:, 0:1024], E_f[:, 0:1024],
                                                          -1.0, E_hi[:, 0:1024],
                                                          ALU.bypass, ALU.subtract)
                        ser.step("vector", ssplit)
                    ser.step("tensor", mk_avmm(7))

                    def oev(e):
                        last = None
                        for qc in range(4):
                            last = e.tensor_copy(obuf[:, qc * 129:(qc + 1) * 129],
                                                 oslot(qc))
                        return last
                    ser.step("vector", oev)

                    def oadd(e):
                        ob3 = obuf[:, 0:516].rearrange("p (s c) -> p s c", c=129)
                        e.tensor_copy(stat2[:, 80:84], ob3[:, :, 128:129].squeeze(2))
                        e.nop()
                        return e.tensor_tensor(
                            onorm[:, 0:256].rearrange("p (s c) -> p s c", c=64),
                            ob3[:, :, 0:64], ob3[:, :, 64:128], ALU.add)
                    ser.step("vector", oadd)

                    def orc(e):
                        return e.reciprocal(stat2[:, 88:92], stat2[:, 80:84])
                    ser.step("vector", orc)

                    def onrm(e):
                        on3 = onorm[:, 0:256].rearrange("p (s c) -> p s c", c=64)
                        return e.tensor_tensor(
                            on3, on3,
                            stat2[:, 88:92].unsqueeze(2).broadcast_to((128, 4, 64)),
                            ALU.mult)
                    ser.step("vector", onrm)

                    def otr(e):
                        last = None
                        for kk in range(4):
                            last = e.transpose(pT[0:64, kk * 128:(kk + 1) * 128],
                                               onorm[:, kk * 64:(kk + 1) * 64],
                                               ident[:])
                        return last
                    ser.step("tensor", otr)

                    def oas(e):
                        return e.tensor_copy(asm[0:64, 0:512], pT[0:64, 0:512])
                    ser.step("vector", oas)

                    def osp(e, sem, h=h, qq=qq):
                        e.dma_start(out=d_attns.ap()[h * 64:(h + 1) * 64,
                                                     qq * 512:(qq + 1) * 512],
                                    in_=asm[0:64, 0:512]).then_inc(sem, 16)
                    ser.dma_n("gpsimd", 1, osp)

        at_f32 = RX[:, 0:ND * T]
        at_hi = vb16(RA, 0, ND * T)
        at_lo = vb16(RA, 32768, ND * T)
        woT_hi = vb16(RB, 0, 8 * 1024)
        woT_lo = vb16(RB, 16384, 8 * 1024)

        def lda(e, sem):
            for c in range(ND):
                e.dma_start(out=at_f32[:, c * T:(c + 1) * T],
                            in_=d_attns.ap()[c * 128:(c + 1) * 128, :]).then_inc(sem, 16)
        ser.dma_n("sync", ND, lda)

        def spa(e):
            e.tensor_copy(at_hi[:, :], at_f32[:, :])
            e.nop()
            return e.scalar_tensor_tensor(at_lo[:, :], at_f32[:, :], -1.0,
                                          at_hi[:, :], ALU.bypass, ALU.subtract)
        ser.step("vector", spa)

        for oc in range(ND):
            def ldwo(e, sem, oc=oc):
                e.dma_start(out=scratch4k[:, 0:1024],
                            in_=W["L0_attn_Wo"].ap()[oc * 128:(oc + 1) * 128, :]
                            ).then_inc(sem, 16)
            ser.dma_n("sync", 1, ldwo)
            for dq in range(0, 8, 4):
                def trwo(e, dq=dq):
                    last = None
                    for kk in range(4):
                        last = e.transpose(pT[:, kk * 128:(kk + 1) * 128],
                                           scratch4k[:, (dq + kk) * 128:(dq + kk + 1) * 128],
                                           ident[:])
                    return last
                ser.step("tensor", trwo)

                def evwo(e, dq=dq, oc=oc):
                    ap_hi = woT_hi[:, :].rearrange("p (d o) -> p d o", o=1024)[
                        :, dq:dq + 4, oc * 128:(oc + 1) * 128]
                    ap_lo = woT_lo[:, :].rearrange("p (d o) -> p d o", o=1024)[
                        :, dq:dq + 4, oc * 128:(oc + 1) * 128]
                    src = pT[:, 0:512].rearrange("p (k c) -> p k c", c=128)
                    e.tensor_copy(ap_hi, src)
                    e.nop()
                    return e.scalar_tensor_tensor(ap_lo, src, -1.0, ap_hi,
                                                  ALU.bypass, ALU.subtract)
                ser.step("vector", evwo)

        def ldbo(e, sem):
            e.dma_start(out=rowflat[0:1, 0:D], in_=W["L0_attn_bo"].ap()).then_inc(sem, 16)
        ser.dma_n("sync", 1, ldbo)

        for t in range(NT):
            def mmo(e, t=t):
                last = None
                first = True
                for dd in range(ND):
                    for (a, b) in ((at_hi, woT_hi), (at_lo, woT_hi), (at_hi, woT_lo)):
                        for half in range(2):
                            last = e.matmul(
                                p1[:, half * 512:(half + 1) * 512],
                                a[:, dd * T + t * 128: dd * T + (t + 1) * 128],
                                b[:, dd * 1024 + half * 512: dd * 1024 + (half + 1) * 512],
                                start=first and half == 0 or (first and half == 1),
                                stop=False)
                        first = False
                for half in range(2):
                    last = e.matmul(p1[:, half * 512:(half + 1) * 512],
                                    onesrow[0:1, :],
                                    rowflat[0:1, half * 512:(half + 1) * 512],
                                    start=False, stop=True)
                return last
            ser.step("tensor", mmo)

            def evo(e, t=t):
                return e.tensor_copy(scratch4k[:, 0:D], p1[:, 0:D])
            ser.step("vector", evo)

            ln_stats_tile(scratch4k[:, 0:D], 1e-5, 8)

            def ldxt(e, sem, t=t):
                e.dma_start(out=scratchx[:, 0:D], in_=xs_tiled[t]).then_inc(sem, 16)
            ser.dma_n("scalar", 1, ldxt)

            def res0(e, t=t):
                e.tensor_scalar(scratch4k[:, 0:D], scratch4k[:, 0:D],
                                stat2[:, 8:9], stat2[:, 9:10], ALU.mult, ALU.add)
                e.nop()
                return e.tensor_tensor(RX[:, t * D:(t + 1) * D], scratchx[:, 0:D],
                                       scratch4k[:, 0:D], ALU.add)
            ser.step("vector", res0)

    blocks = {0: block_L0_attn, 1: block_L0_ffn, 2: block_L1_attn, 3: block_L1_ffn}
    for b in block_ids:
        if b in blocks:
            blocks[b]()

    if 3 not in block_ids:
        out_tiled = dout.ap().rearrange("(n p) d -> n p d", p=128)

        def stx(e, sem):
            for t in range(NT):
                e.dma_start(out=out_tiled[t], in_=RX[:, t * D:(t + 1) * D]).then_inc(sem, 16)
        ser.dma_n("gpsimd", NT, stx)

    def fin(e):
        return e.nop()
    ser.step("gpsimd", fin)

    with nc.semaphore("tok") as sem, nc.semaphore("gtok") as gsem, nc.Block() as block:
        ser.emit(block, sem, gsem)

    return nc


def _consts():
    import ml_dtypes
    c = {}
    c["ident"] = np.eye(128, dtype=np.float32)
    c["identb"] = np.eye(128).astype(ml_dtypes.bfloat16)
    c["onesrow"] = np.ones((1, 128), dtype=np.float32)
    c["onescol"] = np.ones((128, 1), dtype=np.float32)
    sel_qk = np.zeros((512, 8), dtype=np.float32)
    for ch in range(512):
        sel_qk[ch, ch // 64] = 1.0
    c["sel_qk"] = sel_qk
    sel_rep = np.zeros((8, 512), dtype=np.float32)
    for cc in range(4):
        for p in range(128):
            ch = cc * 128 + p
            sel_rep[ch // 64, cc * 128 + p] = 1.0
    c["sel_rep"] = sel_rep
    return c


def _in_map(core, inputs):
    m = dict(_consts())
    m["x"] = inputs["L0_attn_x"][core]
    for nm in ("L0_attn_Wq", "L0_attn_Wk", "L0_attn_Wv", "L0_attn_Wo",
               "L1_attn_Wq", "L1_attn_Wk", "L1_attn_Wv", "L1_attn_Wo",
               "L1_ffn_w1", "L1_ffn_w2", "L0_ffn_w_int8"):
        m[nm] = inputs[nm]
    m["L0_attn_bo"] = inputs["L0_attn_bo"].reshape(1, D)
    m["L0_ffn_weight_scale"] = np.asarray(inputs["L0_ffn_weight_scale"]).reshape(1, 1)
    m["L0_ffn_bias"] = inputs["L0_ffn_bias"].reshape(1, D)
    m["L1_attn_bq"] = inputs["L1_attn_bq"].reshape(512, 1)
    m["L1_attn_bk"] = inputs["L1_attn_bk"].reshape(512, 1)
    m["L1_attn_bv"] = inputs["L1_attn_bv"].reshape(512, 1)
    m["L1_attn_bo"] = inputs["L1_attn_bo"].reshape(1, D)
    m["L1_attn_alpha"] = np.asarray(inputs["L1_attn_alpha"]).reshape(512, 1)
    m["L1_attn_aug"] = np.asarray(inputs["L1_attn_aug_param"]).reshape(512, 1)
    m["L1_attn_ln_scale"] = inputs["L1_attn_ln_scale"].reshape(1, D)
    m["L1_attn_ln_bias"] = inputs["L1_attn_ln_bias"].reshape(1, D)
    m["L1_ffn_conv_w"] = np.asarray(inputs["L1_ffn_conv_w"]).reshape(4096, 3)
    m["L1_ffn_conv_b"] = np.asarray(inputs["L1_ffn_conv_b"]).reshape(4096, 1)
    for k, v in list(m.items()):
        m[k] = np.ascontiguousarray(v)
    return m


_BLOCK_IDS = (0, 1, 2, 3)


def kernel(**inputs):
    nc = build_nc(_BLOCK_IDS)
    in_maps = [_in_map(c, inputs) for c in range(8)]
    res = run_bass_kernel_spmd(nc, in_maps, core_ids=list(range(8)))
    return np.stack([np.asarray(r["out"], dtype=np.float32) for r in res.results], axis=0)


# version 10
# speedup vs baseline: 1.0782x; 1.0658x over previous
import numpy as np
import concourse.bass as bass
import concourse.mybir as mybir
from concourse.bass_utils import run_bass_kernel_spmd

F32 = mybir.dt.float32
BF16 = mybir.dt.bfloat16
I32 = mybir.dt.int32
AX = mybir.AxisListType.X
ALU = mybir.AluOpType
ACT = mybir.ActivationFunctionType

T = 2048
D = 1024
NT = T // 128
ND = D // 128


class Ser:

    def __init__(self):
        self.steps = []
        self.val = 0
        self.gval = 0

    def step(self, eng, fn, wait_t=None):
        self.steps.append((eng, self.val if wait_t is None else wait_t,
                           self.gval, fn, 0))
        self.val += 1

    def dma_n(self, eng, n, fn):
        if eng == "gpsimd":
            self.steps.append((eng, self.val, self.gval, fn, 2))
            self.gval += 16 * n
        else:
            self.steps.append((eng, self.val, self.gval, fn, 1))
            self.val += 16 * n

    def emit(self, block, sem, gsem):
        def section(name):
            def _(e):
                for eng, wv, gwv, fn, kind in self.steps:
                    if eng != name:
                        continue
                    if wv > 0:
                        e.wait_ge(sem, wv)
                    if gwv > 0:
                        e.wait_ge(gsem, gwv)
                    if kind == 2:
                        fn(e, gsem)
                    elif kind == 1:
                        fn(e, sem)
                    else:
                        fn(e).then_inc(sem, 1)
            return _
        block.sync(section("sync"))
        block.scalar(section("scalar"))
        block.vector(section("vector"))
        block.tensor(section("tensor"))
        block.gpsimd(section("gpsimd"))


def build_nc(block_ids=(0, 1, 2, 3), dbg=None, l1a_stop=99):
    nc = bass.Bass("TRN2", target_bir_lowering=False, detect_race_conditions=False)
    ddbg = nc.dram_tensor("dbg", [128, 16384], F32, kind="ExternalOutput") if dbg else None

    dx = nc.dram_tensor("x", [T, D], F32, kind="ExternalInput")
    dout = nc.dram_tensor("out", [T, D], F32, kind="ExternalOutput")
    dident = nc.dram_tensor("ident", [128, 128], F32, kind="ExternalInput")
    didentb = nc.dram_tensor("identb", [128, 128], BF16, kind="ExternalInput")
    donesrow = nc.dram_tensor("onesrow", [1, 128], F32, kind="ExternalInput")
    donescol = nc.dram_tensor("onescol", [128, 1], F32, kind="ExternalInput")

    W = {}
    for nm, shp, dt in [
        ("L0_attn_Wq", [D, D], F32), ("L0_attn_Wk", [D, D], F32),
        ("L0_attn_Wv", [D, D], F32), ("L0_attn_Wo", [D, D], F32),
        ("L0_attn_bo", [1, D], F32),
        ("L0_ffn_w_int8", [D, D], I32), ("L0_ffn_weight_scale", [1, 1], F32),
        ("L0_ffn_bias", [1, D], F32),
        ("L1_attn_Wq", [D, 512], F32), ("L1_attn_bq", [512, 1], F32),
        ("L1_attn_Wk", [D, 512], F32), ("L1_attn_bk", [512, 1], F32),
        ("L1_attn_Wv", [D, 512], F32), ("L1_attn_bv", [512, 1], F32),
        ("L1_attn_Wo", [512, D], F32), ("L1_attn_bo", [1, D], F32),
        ("L1_attn_alpha", [512, 1], F32), ("L1_attn_aug", [512, 1], F32),
        ("L1_attn_ln_scale", [1, D], F32), ("L1_attn_ln_bias", [1, D], F32),
        ("L1_ffn_w1", [4096, D], F32), ("L1_ffn_conv_w", [4096, 3], F32),
        ("L1_ffn_conv_b", [4096, 1], F32), ("L1_ffn_w2", [D, 4096], F32),
        ("sel_qk", [512, 8], F32), ("sel_rep", [8, 512], F32),
    ]:
        W[nm] = nc.dram_tensor(nm, shp, dt, kind="ExternalInput")

    d_xspill = nc.dram_tensor("xspill", [T, D], F32, kind="Internal")
    d_attns = nc.dram_tensor("attnspill", [D, T], F32, kind="Internal")
    d_xq1 = nc.dram_tensor("xq1spill", [D, T], BF16, kind="Internal")

    RX = nc.alloc_sbuf_tensor("RX", [128, 16384], F32)
    RA = nc.alloc_sbuf_tensor("RA", [128, 16384], F32)
    RB = nc.alloc_sbuf_tensor("RB", [128, 8448], F32)
    _ovl_b = {}
    _ovl_i = {}
    for _r in (RX, RA, RB):
        _addr = nc.lookup_mloc(_r).addr
        _cols = _r.shape[1]
        _ovl_b[_r.name] = nc.alloc_sbuf_tensor_at(
            _r.name + "_b16", [128, 2 * _cols], BF16, offset=_addr)
        _ovl_i[_r.name] = nc.alloc_sbuf_tensor_at(
            _r.name + "_i32", [128, _cols], I32, offset=_addr)

    def vf32(reg, byte_off, n):
        return reg[:, byte_off // 4: byte_off // 4 + n]

    def vb16(reg, byte_off, n):
        return _ovl_b[reg.tensor.name if hasattr(reg, "tensor") else reg.name][
            :, byte_off // 2: byte_off // 2 + n]

    def vi32(reg, byte_off, n):
        return _ovl_i[reg.tensor.name if hasattr(reg, "tensor") else reg.name][
            :, byte_off // 4: byte_off // 4 + n]

    ident = nc.alloc_sbuf_tensor("identS", [128, 128], F32)
    identb = nc.alloc_sbuf_tensor("identbS", [128, 128], BF16)
    onesrow = nc.alloc_sbuf_tensor("onesrowS", [1, 128], F32)
    onescol = nc.alloc_sbuf_tensor("onescolS", [128, 1], F32)
    cw = nc.alloc_sbuf_tensor("cwS", [128, 96], F32)
    cb = nc.alloc_sbuf_tensor("cbS", [128, 32], F32)
    xq1c = nc.alloc_sbuf_tensor("xq1cS", [128, 8 * 130], BF16)
    c1rep = nc.alloc_sbuf_tensor("c1repS", [128, 130], F32)
    s2rep = nc.alloc_sbuf_tensor("s2repS", [128, 128], F32)
    rowbuf = nc.alloc_sbuf_tensor("rowbufS", [16, 128], F32)
    selq = nc.alloc_sbuf_tensor("selqS", [128, 32], F32)
    selrep = nc.alloc_sbuf_tensor("selrepS", [8, 512], F32)
    lachan = nc.alloc_sbuf_tensor("lachanS", [128, 32], F32)
    d_xspill2 = nc.dram_tensor("xspill2", [T, D], F32, kind="Internal")
    rowflat = nc.alloc_sbuf_tensor("rowflatS", [1, 2048], F32)
    stat = nc.alloc_sbuf_tensor("stat", [128, 128], F32)
    stat2 = nc.alloc_sbuf_tensor("stat2", [128, 128], F32)
    tiny = nc.alloc_sbuf_tensor("tinyS", [128, 64], F32)
    rep4k = nc.alloc_sbuf_tensor("rep4k", [128, D], F32)
    rep4k2 = nc.alloc_sbuf_tensor("rep4k2", [128, D], F32)
    scratch4k = nc.alloc_sbuf_tensor("scratch4k", [128, 1024], F32)
    scratchq = nc.alloc_sbuf_tensor("scratchq", [128, 2 * 1024], F32)
    scratchx = nc.alloc_sbuf_tensor("scratchx", [128, 1024], F32)
    scratchqi = nc.alloc_sbuf_tensor("scratchqi", [128, 1024], I32)

    pT = nc.alloc_psum_tensor("pT", [128, 512], F32)
    pTb = nc.alloc_psum_tensor("pTb", [128, 1024], BF16)
    p1 = nc.alloc_psum_tensor("p1", [128, 1024], F32)
    p2 = nc.alloc_psum_tensor("p2", [128, 1024], F32)
    pS = nc.alloc_psum_tensor("pS", [128, 512], F32)

    ser = Ser()

    _tapoff = [0]

    def tap(name, ap_f32, ncols, at=None):
        if not dbg or (name not in dbg):
            return
        off = _tapoff[0] if at is None else at
        _tapoff[0] = off + ncols

        def f(e, sem):
            e.dma_start(out=ddbg.ap()[:, off:off + ncols], in_=ap_f32).then_inc(sem, 16)
        ser.dma_n("gpsimd", 1, f)

    def ld_consts():
        def f(e, sem):
            e.dma_start(out=ident[:], in_=dident.ap()).then_inc(sem, 16)
            e.dma_start(out=identb[:], in_=didentb.ap()).then_inc(sem, 16)
            e.dma_start(out=onesrow[:], in_=donesrow.ap()).then_inc(sem, 16)
            e.dma_start(out=onescol[:], in_=donescol.ap()).then_inc(sem, 16)
        ser.dma_n("sync", 4, f)

    def ln_stats_tile(src_ap, eps, rcol):
        def s1(e):
            return e.reduce_sum(stat2[:, rcol:rcol + 1], src_ap, AX)
        ser.step("vector", s1)

        def s2(e):
            return e.activation(scratchq[:, 0:D], src_ap, ACT.Square,
                                accum_out=stat2[:, rcol + 1:rcol + 2])
        ser.step("scalar", s2)

        def s3(e):
            e.tensor_scalar(stat2[:, rcol:rcol + 1], stat2[:, rcol:rcol + 1], 1.0 / D, None, ALU.mult)
            e.tensor_scalar(stat2[:, rcol + 1:rcol + 2], stat2[:, rcol + 1:rcol + 2], 1.0 / D, None, ALU.mult)
            e.nop()
            e.tensor_tensor(stat2[:, rcol + 3:rcol + 4], stat2[:, rcol:rcol + 1],
                            stat2[:, rcol:rcol + 1], ALU.mult)
            e.nop()
            e.tensor_tensor(stat2[:, rcol + 2:rcol + 3], stat2[:, rcol + 1:rcol + 2],
                            stat2[:, rcol + 3:rcol + 4], ALU.subtract)
            e.nop()
            return e.tensor_scalar(stat2[:, rcol + 2:rcol + 3], stat2[:, rcol + 2:rcol + 3],
                                   eps, None, ALU.add)
        ser.step("vector", s3)

        def s4(e):
            return e.activation(stat2[:, rcol + 2:rcol + 3], stat2[:, rcol + 2:rcol + 3], ACT.Sqrt)
        ser.step("scalar", s4)

        def s5a(e):
            return e.reciprocal(stat2[:, rcol + 3:rcol + 4], stat2[:, rcol + 2:rcol + 3])
        ser.step("vector", s5a)

        def s5(e):
            e.tensor_tensor(stat2[:, rcol + 1:rcol + 2], stat2[:, rcol:rcol + 1],
                            stat2[:, rcol + 3:rcol + 4], ALU.mult)
            e.nop()
            e.tensor_scalar(stat2[:, rcol + 1:rcol + 2], stat2[:, rcol + 1:rcol + 2],
                            -1.0, None, ALU.mult)
            e.nop()
            return e.tensor_copy(stat2[:, rcol:rcol + 1], stat2[:, rcol + 3:rcol + 4])
        ser.step("vector", s5)

    def ln_full(src_region, dst_region, eps=1e-5):
        n = NT
        src3 = src_region[:, 0:n * D].rearrange("p (n w) -> p n w", w=D)

        def sums(e):
            return e.reduce_sum(stat2[:, 0:n].unsqueeze(2), src3, AX)
        ser.step("vector", sums)
        for t in range(n):
            def sq(e, t=t):
                return e.activation(scratchq[:, 0:D],
                                    src_region[:, t * D:(t + 1) * D], ACT.Square,
                                    accum_out=stat2[:, n + t:n + t + 1])
            ser.step("scalar", sq)

        def var(e):
            e.tensor_scalar(stat2[:, 0:n], stat2[:, 0:n], 1.0 / D, None, ALU.mult)
            e.tensor_scalar(stat2[:, n:2 * n], stat2[:, n:2 * n], 1.0 / D, None, ALU.mult)
            e.nop()
            e.tensor_tensor(stat2[:, 3 * n:4 * n], stat2[:, 0:n], stat2[:, 0:n], ALU.mult)
            e.nop()
            e.tensor_tensor(stat2[:, 4 * n:5 * n], stat2[:, n:2 * n],
                            stat2[:, 3 * n:4 * n], ALU.subtract)
            e.nop()
            return e.tensor_scalar(stat2[:, 2 * n:3 * n], stat2[:, 4 * n:5 * n],
                                   eps, None, ALU.add)
        ser.step("vector", var)
        tap("lnstats", stat2[:, 0:128], 128)

        def sqrt_(e):
            return e.activation(stat2[:, 2 * n:3 * n], stat2[:, 2 * n:3 * n], ACT.Sqrt)
        ser.step("scalar", sqrt_)

        def fin0(e):
            return e.reciprocal(stat[:, 0:n], stat2[:, 2 * n:3 * n])
        ser.step("vector", fin0)

        def fin(e):
            e.tensor_tensor(stat[:, n:2 * n], stat2[:, 0:n], stat[:, 0:n], ALU.mult)
            e.nop()
            return e.tensor_scalar(stat[:, n:2 * n], stat[:, n:2 * n], -1.0, None, ALU.mult)
        ser.step("vector", fin)

        for t in range(n):
            def ap(e, t=t):
                return e.tensor_scalar(
                    dst_region[:, t * D:(t + 1) * D],
                    src_region[:, t * D:(t + 1) * D],
                    stat[:, t:t + 1], stat[:, n + t:n + t + 1],
                    ALU.mult, ALU.add)
            ser.step("vector", ap)

    def round_quant(src_ap, dst_ap, width, scale_ap, lo, hi):
        y = scratch4k[:, 0:width]
        ecol = scratchq[:, 0:width]
        cm = scratchq[:, 1024:1024 + width]
        ci = scratchqi[:, 0:width]

        def f(e):
            if scale_ap is not None:
                e.tensor_scalar(y, src_ap, scale_ap, None, ALU.mult)
                yy = y
            else:
                yy = src_ap
            e.tensor_copy(ci, yy)
            e.tensor_scalar(cm, ci, -1.0, None, ALU.add)
            e.tensor_tensor(ecol, yy, cm, ALU.subtract)
            e.scalar_tensor_tensor(cm, ecol, 0.5, cm, ALU.is_ge, ALU.add)
            e.scalar_tensor_tensor(cm, ecol, 1.5, cm, ALU.is_ge, ALU.add)
            return e.tensor_scalar(dst_ap, cm, float(lo), float(hi), ALU.max, ALU.min)
        ser.step("vector", f)

    def replicate_row(src_row_ap, dst_sb, width):
        def mm(e):
            last = None
            for h in range(0, width, 512):
                w = min(512, width - h)
                last = e.matmul(p1[:, h:h + w],
                                onesrow[0:1, :], src_row_ap[0:1, h:h + w],
                                start=True, stop=True)
            return last
        ser.step("tensor", mm)

        def cp(e):
            return e.tensor_copy(dst_sb[:, 0:width], p1[:, 0:width])
        ser.step("vector", cp)

    def transpose_b16(src_ap_list, dst_ap_list):
        nb = len(src_ap_list)
        assert nb <= 8

        def tr(e):
            last = None
            for k in range(nb):
                last = e.transpose(pTb[:, k * 128:(k + 1) * 128], src_ap_list[k], identb[:])
            return last
        ser.step("tensor", tr)

        def ev(e):
            last = None
            for k in range(nb):
                last = e.tensor_copy(dst_ap_list[k], pTb[:, k * 128:(k + 1) * 128])
            return last
        ser.step("vector", ev)

    ld_consts()
    x_tiled = dx.ap().rearrange("(n p) d -> n p d", p=128)

    def ldx(e, sem):
        for t in range(NT):
            e.dma_start(out=RX[:, t * D:(t + 1) * D], in_=x_tiled[t]).then_inc(sem, 16)
    ser.dma_n("sync", NT, ldx)

    def block_L0_ffn():
        xln = RA[:, 0:NT * D]
        ln_full(RX, RA, 1e-5)
        tap("xln", RA[:, 0:16384], 16384)
        tap("xa", xln[:, 4 * D:5 * D], D)

        n = NT
        src3 = xln.rearrange("p (n w) -> p n w", w=D)
        for t in range(n):
            def sq(e, t=t):
                return e.activation(scratchq[:, 0:D], xln[:, t * D:(t + 1) * D],
                                    ACT.Square, accum_out=stat2[:, t:t + 1])
            ser.step("scalar", sq)

        tap("xb", xln[:, 4 * D:5 * D], D)

        for t in range(n):
            def amax_(e, t=t):
                return e.reduce_max(stat2[:, n + t:n + t + 1], xln[:, t * D:(t + 1) * D],
                                    AX, apply_absolute_value=True)
            ser.step("vector", amax_)
        tap("amax", stat2[:, 0:64], 64)

        tap("xc", xln[:, 4 * D:5 * D], D)

        def ms_(e):
            return e.tensor_scalar(stat2[:, 0:n], stat2[:, 0:n], 1.0 / D, 1e-6,
                                   ALU.mult, ALU.add)
        ser.step("vector", ms_)

        def rsq_(e):
            return e.activation(stat2[:, 0:n], stat2[:, 0:n], ACT.Sqrt)
        ser.step("scalar", rsq_)

        def sc1(e):
            return e.reciprocal(stat2[:, 2 * n:3 * n], stat2[:, 0:n])
        ser.step("vector", sc1)

        def sc2(e):
            e.tensor_tensor(stat2[:, 3 * n:4 * n], stat2[:, 2 * n:3 * n],
                            stat2[:, n:2 * n], ALU.mult)
            e.nop()
            return e.tensor_scalar(stat2[:, 3 * n:4 * n], stat2[:, 3 * n:4 * n],
                                   1e-5, None, ALU.max)
        ser.step("vector", sc2)

        def sc3(e):
            return e.reciprocal(stat2[:, 4 * n:5 * n], stat2[:, 3 * n:4 * n])
        ser.step("vector", sc3)

        def sc4(e):
            e.tensor_scalar(stat2[:, 5 * n:6 * n], stat2[:, 4 * n:5 * n],
                            127.0, None, ALU.mult)
            e.nop()
            return e.tensor_tensor(stat[:, 32:32 + n], stat2[:, 2 * n:3 * n],
                                   stat2[:, 5 * n:6 * n], ALU.mult)
        ser.step("vector", sc4)

        def sc5(e):
            return e.reciprocal(stat[:, 32 + n:32 + 2 * n], stat2[:, 5 * n:6 * n])
        ser.step("vector", sc5)

        tap("xd", xln[:, 4 * D:5 * D], D)
        xq_tm = vb16(RB, 0, NT * D)
        for t in range(NT):
            round_quant(xln[:, t * D:(t + 1) * D], xq_tm[:, t * D:(t + 1) * D],
                        D, stat[:, 32 + t:32 + t + 1], -128, 127)
            if t == 3:
                tap("xe", xln[:, 4 * D:5 * D], D)
            if t == 4:
                tap("xf", xln[:, 4 * D:5 * D], D)
        tap("xg", xln[:, 4 * D:5 * D], D)

        tap("stat2q", stat2[:, 0:128], 128)
        tap("scales", stat[:, 0:128], 128)
        tap("xln4", xln[:, 4 * D:5 * D], D)
        tap("xq_tm", RB[:, 0:8192], 8192)
        xq_fm = vb16(RA, 0, ND * T)
        for c in range(ND):
            for tq in range(0, NT, 8):
                transpose_b16(
                    [xq_tm[:, (tq + k) * D + c * 128:(tq + k) * D + (c + 1) * 128]
                     for k in range(8)],
                    [xq_fm[:, c * T + (tq + k) * 128: c * T + (tq + k + 1) * 128]
                     for k in range(8)])

        w0T = vb16(RA, 32768, ND * D)
        wstage_i = vi32(RB, 16384, 1024)
        wstage_b = vb16(RB, 20480, 1024)
        wi = W["L0_ffn_w_int8"].ap()
        for o in range(ND):
            def ldw(e, sem, o=o):
                e.dma_start(out=wstage_i, in_=wi[o * 128:(o + 1) * 128, :]).then_inc(sem, 16)
            ser.dma_n("sync", 1, ldw)

            def cast(e, o=o):
                return e.tensor_copy(wstage_b, wstage_i)
            ser.step("vector", cast)
            for dq in range(0, ND, 8):
                transpose_b16(
                    [wstage_b[:, (dq + k) * 128:(dq + k + 1) * 128] for k in range(8)],
                    [w0T[:, (dq + k) * D + o * 128:(dq + k) * D + (o + 1) * 128]
                     for k in range(8)])

        tap("xq_fm", RA[:, 0:8192], 8192)
        tap("w0T", RA[:, 8192:12288], 4096)
        def ldb(e, sem):
            e.dma_start(out=rowflat[0:1, 0:D], in_=W["L0_ffn_bias"].ap()).then_inc(sem, 16)
            e.dma_start(out=tiny[0:1, 0:1], in_=W["L0_ffn_weight_scale"].ap()).then_inc(sem, 16)
        ser.dma_n("sync", 2, ldb)
        replicate_row(rowflat[0:1, 0:D], rep4k, D)

        def repws(e):
            return e.matmul(pS[:, 0:1], onesrow[0:1, :], tiny[0:1, 0:1],
                            start=True, stop=True)
        ser.step("tensor", repws)

        def cout(e):
            e.tensor_copy(tiny[:, 1:2], pS[:, 0:1])
            e.nop()
            return e.tensor_scalar(stat[:, 32 + NT:32 + 2 * NT],
                                   stat[:, 32 + NT:32 + 2 * NT],
                                   tiny[:, 1:2], None, ALU.mult)
        ser.step("vector", cout)

        for t in range(NT):
            def mm(e, t=t):
                last = None
                for dd in range(ND):
                    for half in range(2):
                        last = e.matmul(
                            p1[:, half * 512:(half + 1) * 512],
                            xq_fm[:, dd * T + t * 128: dd * T + (t + 1) * 128],
                            w0T[:, dd * D + half * 512: dd * D + (half + 1) * 512],
                            start=(dd == 0), stop=(dd == ND - 1))
                return last
            ser.step("tensor", mm)

            def ev(e, t=t):
                return e.scalar_tensor_tensor(
                    scratch4k[:, 0:D], p1[:, 0:D],
                    stat[:, 32 + NT + t:32 + NT + t + 1], rep4k[:, 0:D],
                    ALU.mult, ALU.add)
            ser.step("vector", ev)

            if t == 0:
                tap("blk0", scratch4k[:, 0:D], D)
            ln_stats_tile(scratch4k[:, 0:D], 1e-5, 8)

            def res(e, t=t):
                e.tensor_scalar(scratch4k[:, 0:D], scratch4k[:, 0:D],
                                stat2[:, 8:9], stat2[:, 9:10], ALU.mult, ALU.add)
                return e.tensor_tensor(RX[:, t * D:(t + 1) * D],
                                       RX[:, t * D:(t + 1) * D],
                                       scratch4k[:, 0:D], ALU.add)
            ser.step("vector", res)


    def block_L1_ffn():
        n = NT
        xln = RA[:, 0:NT * D]
        ln_full(RX, RA, 1e-5)

        for t in range(n):
            def amax_(e, t=t):
                return e.reduce_max(stat2[:, t:t + 1], xln[:, t * D:(t + 1) * D],
                                    AX, apply_absolute_value=True)
            ser.step("vector", amax_)

        def sc2(e):
            return e.tensor_scalar(stat2[:, n:2 * n], stat2[:, 0:n], 1e-5, None, ALU.max)
        ser.step("vector", sc2)

        def sc3(e):
            return e.reciprocal(stat2[:, 2 * n:3 * n], stat2[:, n:2 * n])
        ser.step("vector", sc3)

        def sc4(e):
            e.tensor_scalar(stat[:, 32:32 + n], stat2[:, 2 * n:3 * n], 127.0, None, ALU.mult)
            e.nop()
            return e.tensor_scalar(stat[:, 32 + n:32 + 2 * n], stat2[:, n:2 * n],
                                   1.0 / 127.0, None, ALU.mult)
        ser.step("vector", sc4)

        xq1_tm = vb16(RB, 0, NT * D)
        for t in range(NT):
            round_quant(xln[:, t * D:(t + 1) * D], xq1_tm[:, t * D:(t + 1) * D],
                        D, stat[:, 32 + t:32 + t + 1], -128, 127)

        xq1_fm = vb16(RA, 0, ND * T)
        for c in range(ND):
            for tq in range(0, NT, 8):
                transpose_b16(
                    [xq1_tm[:, (tq + k) * D + c * 128:(tq + k) * D + (c + 1) * 128]
                     for k in range(8)],
                    [xq1_fm[:, c * T + (tq + k) * 128: c * T + (tq + k + 1) * 128]
                     for k in range(8)])

        xs_tiled = d_xspill.ap().rearrange("(n p) d -> n p d", p=128)

        def spx(e, sem):
            for t in range(NT):
                e.dma_start(out=xs_tiled[t], in_=RX[:, t * D:(t + 1) * D]).then_inc(sem, 16)
        ser.dma_n("scalar", NT, spx)

        def spq(e, sem):
            for c in range(ND):
                e.dma_start(out=d_xq1.ap()[c * 128:(c + 1) * 128, :],
                            in_=xq1_fm[:, c * T:(c + 1) * T]).then_inc(sem, 16)
        ser.dma_n("sync", ND, spq)

        w1qT = vb16(RX, 0, 8 * 4096)
        w2qT = vb16(RA, 0, 32 * 1024)
        wst = RB[:, 0:4096]
        wqb = vb16(RB, 16384, 4096)

        def prep_ternary(dw, nchunks, width, statbase, qcol, wT, wT_colf):
            for ci in range(nchunks):
                def ldw(e, sem, ci=ci):
                    e.dma_start(out=wst[:, 0:width],
                                in_=dw.ap()[ci * 128:(ci + 1) * 128, :]).then_inc(sem, 16)
                ser.dma_n("sync", 1, ldw)

                def rs(e, ci=ci):
                    return e.reduce_sum(stat2[:, statbase + ci:statbase + ci + 1],
                                        wst[:, 0:width], AX, apply_absolute_value=True)
                ser.step("vector", rs)

            def rs2(e):
                return e.reduce_sum(stat2[:, statbase + nchunks:statbase + nchunks + 1],
                                    stat2[:, statbase:statbase + nchunks], AX)
            ser.step("vector", rs2)

            def xp(e):
                return e.matmul(pS[0:1, 0:1],
                                stat2[:, statbase + nchunks:statbase + nchunks + 1],
                                onescol[:], start=True, stop=True)
            ser.step("tensor", xp)

            def mean_(e):
                e.tensor_copy(tiny[0:1, qcol:qcol + 1], pS[0:1, 0:1])
                e.nop()
                e.tensor_scalar(tiny[0:1, qcol:qcol + 1], tiny[0:1, qcol:qcol + 1],
                                1.0 / (nchunks * 128 * width), None, ALU.mult)
                e.nop()
                return e.tensor_scalar(tiny[0:1, qcol:qcol + 1], tiny[0:1, qcol:qcol + 1],
                                       1e-5, None, ALU.max)
            ser.step("vector", mean_)

            def recw(e):
                return e.reciprocal(tiny[0:1, qcol + 1:qcol + 2], tiny[0:1, qcol:qcol + 1])
            ser.step("vector", recw)

            def repw(e):
                e.matmul(pS[:, 2:3], onesrow[0:1, :], tiny[0:1, qcol + 1:qcol + 2],
                         start=True, stop=True)
                return e.matmul(pS[:, 3:4], onesrow[0:1, :], tiny[0:1, qcol:qcol + 1],
                                start=True, stop=True)
            ser.step("tensor", repw)

            def cpw(e):
                e.tensor_copy(tiny[:, qcol + 2:qcol + 3], pS[:, 2:3])
                return e.tensor_copy(tiny[:, qcol + 3:qcol + 4], pS[:, 3:4])
            ser.step("vector", cpw)

            for ci in range(nchunks):
                def ldw(e, sem, ci=ci):
                    e.dma_start(out=wst[:, 0:width],
                                in_=dw.ap()[ci * 128:(ci + 1) * 128, :]).then_inc(sem, 16)
                ser.dma_n("sync", 1, ldw)
                for g in range(0, width, 1024):
                    gw = min(1024, width - g)
                    round_quant(wst[:, g:g + gw], wqb[:, g:g + gw], gw,
                                tiny[:, qcol + 2:qcol + 3], -1, 1)
                nblk = width // 128
                for bq in range(0, nblk, 8):
                    nb = min(8, nblk - bq)
                    transpose_b16(
                        [wqb[:, (bq + k) * 128:(bq + k + 1) * 128] for k in range(nb)],
                        [wT[:, wT_colf(ci, bq + k)] for k in range(nb)])

        prep_ternary(W["L1_ffn_w1"], 32, 1024, 34, 8,
                     w1qT, lambda hh, dd: slice(dd * 4096 + hh * 128, dd * 4096 + (hh + 1) * 128))
        prep_ternary(W["L1_ffn_w2"], 8, 4096, 34, 16,
                     w2qT, lambda oo, hh: slice(hh * 1024 + oo * 128, hh * 1024 + (oo + 1) * 128))

        def ldcv(e, sem):
            e.dma_start(out=cw[:].rearrange("p (c k) -> p c k", k=3),
                        in_=W["L1_ffn_conv_w"].ap().rearrange("(c p) k -> p c k", p=128)
                        ).then_inc(sem, 16)
            e.dma_start(out=cb[:].rearrange("p (c k) -> p c k", k=1),
                        in_=W["L1_ffn_conv_b"].ap().rearrange("(c p) k -> p c k", p=128),
                        allow_slow_non_contiguous=True).then_inc(sem, 16)
        ser.dma_n("sync", 2, ldcv)

        def c1c(e):
            return e.tensor_scalar(stat[:, 64:64 + n], stat[:, 32 + n:32 + 2 * n],
                                   tiny[:, 8 + 3:8 + 4], None, ALU.mult)
        ser.step("vector", c1c)

        def c1t(e):
            return e.transpose(pS[0:16, 0:128], stat[:, 64:64 + n], ident[:])
        ser.step("tensor", c1t)

        def c1cp(e):
            return e.tensor_copy(rowbuf[:, :], pS[0:16, 0:128])
        ser.step("vector", c1cp)

        def c1fl(e, sem):
            e.dma_start(out=rowflat[0:1, :], in_=rowbuf[:, :]).then_inc(sem, 16)
        ser.dma_n("gpsimd", 1, c1fl)

        h1 = RB[:, 0:32 * 130]
        h2 = RB[:, 32 * 130:32 * 130 + 32 * 128]
        xq2 = vb16(RB, 0, 32 * 128)
        out_tiled2 = dout.ap().rearrange("(n p) d -> n p d", p=128)

        for c in range(NT):
            lo = c * 128 - 1
            hi = c * 128 + 129

            def ldq(e, sem, c=c, lo=lo, hi=hi):
                clo, chi = max(lo, 0), min(hi, T)
                off = clo - lo
                for dd in range(ND):
                    e.dma_start(out=xq1c[:, dd * 130 + off: dd * 130 + off + (chi - clo)],
                                in_=d_xq1.ap()[dd * 128:(dd + 1) * 128, clo:chi]
                                ).then_inc(sem, 16)
            ser.dma_n("sync", ND, ldq)

            if c == 0 or c == NT - 1:
                def mz(e, c=c):
                    col = 0 if c == 0 else 129
                    last = None
                    for dd in range(ND):
                        last = e.memset(xq1c[:, dd * 130 + col: dd * 130 + col + 1], 0.0)
                    return last
                ser.step("vector", mz)

            def c1r(e, c=c, lo=lo):
                slo = max(lo, 0)
                width = 130 if 0 <= lo and lo + 130 <= T else 129
                return e.matmul(p2[:, 0:width], onesrow[0:1, :],
                                rowflat[0:1, slo:slo + width], start=True, stop=True)
            ser.step("tensor", c1r)

            def c1rc(e, c=c, lo=lo):
                slo = max(lo, 0)
                off = slo - lo
                width = 130 if 0 <= lo and lo + 130 <= T else 129
                last = e.tensor_copy(c1rep[:, off:off + width], p2[:, 0:width])
                if off:
                    last = e.memset(c1rep[:, 0:1], 0.0)
                if lo + 130 > T:
                    last = e.memset(c1rep[:, 129:130], 0.0)
                return last
            ser.step("vector", c1rc)

            for hb in range(16):
                def mm1(e, hb=hb):
                    last = None
                    for kk in range(2):
                        hh = hb * 2 + kk
                        for dd in range(ND):
                            last = e.matmul(p1[:, kk * 512: kk * 512 + 130],
                                            w1qT[:, dd * 4096 + hh * 128: dd * 4096 + (hh + 1) * 128],
                                            xq1c[:, dd * 130:(dd + 1) * 130],
                                            start=(dd == 0), stop=(dd == ND - 1))
                    return last
                ser.step("tensor", mm1)

                def ev1(e, hb=hb):
                    last = None
                    for kk in range(2):
                        hh = hb * 2 + kk
                        last = e.tensor_tensor(h1[:, hh * 130:(hh + 1) * 130],
                                               p1[:, kk * 512: kk * 512 + 130],
                                               c1rep[:, 0:130], ALU.mult)
                    return last
                ser.step("vector", ev1)

            for hh in range(0, 32, 4):
                def cv(e, hh=hh):
                    last = None
                    for k in range(4):
                        h = hh + k
                        e.tensor_scalar(h2[:, h * 128:(h + 1) * 128],
                                        h1[:, h * 130 + 1: h * 130 + 129],
                                        cw[:, h * 3 + 1:h * 3 + 2], cb[:, h:h + 1],
                                        ALU.mult, ALU.add)
                        e.scalar_tensor_tensor(h2[:, h * 128:(h + 1) * 128],
                                               h1[:, h * 130 + 0: h * 130 + 128],
                                               cw[:, h * 3 + 0:h * 3 + 1],
                                               h2[:, h * 128:(h + 1) * 128],
                                               ALU.mult, ALU.add)
                        last = e.scalar_tensor_tensor(h2[:, h * 128:(h + 1) * 128],
                                                      h1[:, h * 130 + 2: h * 130 + 130],
                                                      cw[:, h * 3 + 2:h * 3 + 3],
                                                      h2[:, h * 128:(h + 1) * 128],
                                                      ALU.mult, ALU.add)
                    return last
                ser.step("vector", cv)

            def sl(e):
                return e.activation(h2[:, 0:4096], h2[:, 0:4096], ACT.Silu)
            ser.step("scalar", sl)

            for g in range(8):
                def trp(e, g=g):
                    last = None
                    for k in range(4):
                        last = e.transpose(pT[:, k * 128:(k + 1) * 128],
                                          h2[:, (g * 4 + k) * 128:(g * 4 + k + 1) * 128],
                                          ident[:])
                    return last
                ser.step("tensor", trp)

                def rd(e, g=g):
                    return e.reduce_max(stat2[:, 64 + g:64 + g + 1], pT[:, 0:512],
                                        AX, apply_absolute_value=True)
                ser.step("vector", rd)

            def rmx(e):
                return e.reduce_max(stat2[:, 72:73], stat2[:, 64:72], AX)
            ser.step("vector", rmx)

            def s2a(e):
                e.tensor_scalar(stat2[:, 73:74], stat2[:, 72:73], 1e-5, None, ALU.max)
                e.nop()
                e.tensor_scalar(stat2[:, 74:75], stat2[:, 73:74], 1.0 / 127.0, None, ALU.mult)
                e.nop()
                return e.tensor_scalar(stat2[:, 75:76], stat2[:, 74:75],
                                       tiny[:, 16 + 3:16 + 4], None, ALU.mult)
            ser.step("vector", s2a)

            def s2b(e):
                return e.reciprocal(stat2[:, 76:77], stat2[:, 73:74])
            ser.step("vector", s2b)

            def s2c(e):
                return e.tensor_scalar(stat2[:, 77:78], stat2[:, 76:77], 127.0, None, ALU.mult)
            ser.step("vector", s2c)

            def s2t(e):
                return e.transpose(pS[0:1, 0:128], stat2[:, 77:78], ident[:])
            ser.step("tensor", s2t)

            def s2cp(e):
                return e.tensor_copy(rowbuf[0:1, 0:128], pS[0:1, 0:128])
            ser.step("vector", s2cp)

            def s2r(e):
                return e.matmul(p2[:, 0:128], onesrow[0:1, :], rowbuf[0:1, 0:128],
                                start=True, stop=True)
            ser.step("tensor", s2r)

            def s2rc(e):
                return e.tensor_copy(s2rep[:, :], p2[:, 0:128])
            ser.step("vector", s2rc)

            for g in range(4):
                def qg(e, g=g):
                    w = 1024
                    src3 = h2[:, g * w:(g + 1) * w].rearrange("p (a b) -> p a b", b=128)
                    s3 = s2rep[:, :].unsqueeze(1).broadcast_to((128, 8, 128))
                    y3 = scratch4k[:, 0:w].rearrange("p (a b) -> p a b", b=128)
                    e.tensor_tensor(y3, src3, s3, ALU.mult)
                    y = scratch4k[:, 0:w]
                    ecol = scratchq[:, 0:w]
                    cm = scratchq[:, 1024:1024 + w]
                    ci = scratchqi[:, 0:w]
                    e.tensor_copy(ci, y)
                    e.tensor_scalar(cm, ci, -1.0, None, ALU.add)
                    e.tensor_tensor(ecol, y, cm, ALU.subtract)
                    e.scalar_tensor_tensor(cm, ecol, 0.5, cm, ALU.is_ge, ALU.add)
                    e.scalar_tensor_tensor(cm, ecol, 1.5, cm, ALU.is_ge, ALU.add)
                    return e.tensor_scalar(xq2[:, g * w:(g + 1) * w], cm,
                                           -128.0, 127.0, ALU.max, ALU.min)
                ser.step("vector", qg)

            def mm2(e):
                last = None
                for hh in range(32):
                    for half in range(2):
                        last = e.matmul(p1[:, half * 512:(half + 1) * 512],
                                        xq2[:, hh * 128:(hh + 1) * 128],
                                        w2qT[:, hh * 1024 + half * 512: hh * 1024 + (half + 1) * 512],
                                        start=(hh == 0), stop=(hh == 31))
                return last
            ser.step("tensor", mm2)

            def ldx2(e, sem, c=c):
                e.dma_start(out=scratchx[:, 0:D], in_=xs_tiled[c]).then_inc(sem, 16)
            ser.dma_n("scalar", 1, ldx2)

            def ev2(e, c=c):
                return e.tensor_scalar(scratch4k[:, 0:D], p1[:, 0:D],
                                       stat2[:, 75:76], None, ALU.mult)
            ser.step("vector", ev2)

            ln_stats_tile(scratch4k[:, 0:D], 1e-5, 8)

            def res2(e, c=c):
                e.tensor_scalar(scratch4k[:, 0:D], scratch4k[:, 0:D],
                                stat2[:, 8:9], stat2[:, 9:10], ALU.mult, ALU.add)
                e.nop()
                return e.tensor_tensor(scratch4k[:, 0:D], scratchx[:, 0:D],
                                       scratch4k[:, 0:D], ALU.add)
            ser.step("vector", res2)

            def stc(e, sem, c=c):
                e.dma_start(out=out_tiled2[c], in_=scratch4k[:, 0:D]).then_inc(sem, 16)
            ser.dma_n("gpsimd", 1, stc)


    def block_L1_attn():
        n = NT
        IN = 512

        xln = RA[:, 0:NT * D]
        ln_full(RX, RA, 1e-5)

        xs_tiled = d_xspill.ap().rearrange("(n p) d -> n p d", p=128)

        def spx(e, sem):
            for t in range(NT):
                e.dma_start(out=xs_tiled[t], in_=RX[:, t * D:(t + 1) * D]).then_inc(sem, 16)
        ser.dma_n("scalar", NT, spx)

        xf_hi = vb16(RX, 0, ND * T)
        xf_lo = vb16(RX, 32768, ND * T)
        for c in range(ND):
            for tq in range(0, NT, 4):
                def tr(e, c=c, tq=tq):
                    last = None
                    for k in range(4):
                        t = tq + k
                        last = e.transpose(pT[:, k * 128:(k + 1) * 128],
                                           xln[:, t * D + c * 128: t * D + (c + 1) * 128],
                                           ident[:])
                    return last
                ser.step("tensor", tr)

                def ev(e, c=c, tq=tq):
                    e.tensor_copy(xf_hi[:, c * T + tq * 128: c * T + (tq + 4) * 128],
                                  pT[:, 0:512])
                    e.nop()
                    return e.scalar_tensor_tensor(
                        xf_lo[:, c * T + tq * 128: c * T + (tq + 4) * 128],
                        pT[:, 0:512], -1.0,
                        xf_hi[:, c * T + tq * 128: c * T + (tq + 4) * 128],
                        ALU.bypass, ALU.subtract)
                ser.step("vector", ev)

        if l1a_stop <= 1:
            return
        def ldsel(e, sem):
            e.dma_start(out=selq[:].rearrange("p (c h) -> p c h", h=8),
                        in_=W["sel_qk"].ap().rearrange("(c p) h -> p c h", p=128)
                        ).then_inc(sem, 16)
            e.dma_start(out=selrep[:], in_=W["sel_rep"].ap()).then_inc(sem, 16)
            e.dma_start(out=lachan[:, 0:4].rearrange("p (c k) -> p c k", k=1),
                        in_=W["L1_attn_alpha"].ap().rearrange("(c p) k -> p c k", p=128),
                        allow_slow_non_contiguous=True).then_inc(sem, 16)
            e.dma_start(out=lachan[:, 12:16].rearrange("p (c k) -> p c k", k=1),
                        in_=W["L1_attn_aug"].ap().rearrange("(c p) k -> p c k", p=128),
                        allow_slow_non_contiguous=True).then_inc(sem, 16)
            e.dma_start(out=lachan[:, 16:20].rearrange("p (c k) -> p c k", k=1),
                        in_=W["L1_attn_bq"].ap().rearrange("(c p) k -> p c k", p=128),
                        allow_slow_non_contiguous=True).then_inc(sem, 16)
            e.dma_start(out=lachan[:, 20:24].rearrange("p (c k) -> p c k", k=1),
                        in_=W["L1_attn_bk"].ap().rearrange("(c p) k -> p c k", p=128),
                        allow_slow_non_contiguous=True).then_inc(sem, 16)
            e.dma_start(out=lachan[:, 24:28].rearrange("p (c k) -> p c k", k=1),
                        in_=W["L1_attn_bv"].ap().rearrange("(c p) k -> p c k", p=128),
                        allow_slow_non_contiguous=True).then_inc(sem, 16)
        ser.dma_n("sync", 7, ldsel)

        def mkdec(e):
            e.activation(lachan[:, 4:8], lachan[:, 0:4], ACT.Sigmoid)
            return e.activation(lachan[:, 8:12], lachan[:, 0:4], ACT.Sigmoid, scale=-1.0)
        ser.step("scalar", mkdec)

        wstg = RB[:, 0:4096]
        w_hi = vb16(RB, 16384, 4096)
        w_lo = vb16(RB, 24576, 4096)

        def prep_w(dw):
            def ldw(e, sem):
                for dd in range(ND):
                    e.dma_start(out=wstg[:, dd * IN:(dd + 1) * IN],
                                in_=dw.ap()[dd * 128:(dd + 1) * 128, :]).then_inc(sem, 16)
            ser.dma_n("sync", ND, ldw)

            def sp(e):
                e.tensor_copy(w_hi[:, :], wstg[:, :])
                e.nop()
                return e.scalar_tensor_tensor(w_lo[:, :], wstg[:, :], -1.0,
                                              w_hi[:, :], ALU.bypass, ALU.subtract)
            ser.step("vector", sp)

        def project(dst_f32, bias_col):
            for cc in range(4):
                for tc in range(4):
                    def mm(e, cc=cc, tc=tc):
                        last = None
                        ops = [(dd, a, b) for dd in range(ND)
                               for (a, b) in ((xf_hi, w_hi), (xf_lo, w_hi), (xf_hi, w_lo))]
                        for i, (dd, a, b) in enumerate(ops):
                            last = e.matmul(
                                p1[:, 0:512],
                                b[:, dd * IN + cc * 128: dd * IN + (cc + 1) * 128],
                                a[:, dd * T + tc * 512: dd * T + (tc + 1) * 512],
                                start=(i == 0), stop=(i == len(ops) - 1))
                        return last
                    ser.step("tensor", mm)

                    def ev(e, cc=cc, tc=tc):
                        return e.tensor_scalar(
                            dst_f32[:, cc * T + tc * 512: cc * T + (tc + 1) * 512],
                            p1[:, 0:512], 1.0, lachan[:, bias_col + cc:bias_col + cc + 1],
                            ALU.mult, ALU.add)
                    ser.step("vector", ev)

        if l1a_stop <= 2:
            return
        q_f = RA[:, 0:4 * T]
        k_f = RA[:, 4 * T:8 * T]
        prep_w(W["L1_attn_Wq"])
        project(q_f, 16)
        prep_w(W["L1_attn_Wk"])
        project(k_f, 20)

        tap("qf", q_f[:, 0:8192], 8192)
        tap("kf", k_f[:, 0:8192], 8192)
        if l1a_stop <= 3:
            return
        qkbuf = RB[:, 0:2048]
        scsb = RB[:, 2048:2048 + 2048]
        for cc in range(4):
            def qk(e, cc=cc):
                return e.tensor_tensor(qkbuf[:, :], q_f[:, cc * T:(cc + 1) * T],
                                       k_f[:, cc * T:(cc + 1) * T], ALU.mult)
            ser.step("vector", qk)

            def scm(e, cc=cc):
                last = None
                for tc in range(4):
                    pp = p1 if tc < 2 else p2
                    last = e.matmul(pp[0:8, (tc % 2) * 512:(tc % 2) * 512 + 512],
                                    selq[:, cc * 8:(cc + 1) * 8],
                                    qkbuf[:, tc * 512:(tc + 1) * 512],
                                    start=(cc == 0), stop=(cc == 3))
                return last
            ser.step("tensor", scm)

        def sccp(e):
            e.tensor_copy(scsb[0:8, 0:1024], p1[0:8, 0:1024])
            return e.tensor_copy(scsb[0:8, 1024:2048], p2[0:8, 0:1024])
        ser.step("vector", sccp)

        def sct(e):
            last = None
            for t in range(NT):
                last = e.transpose(pT[:, t * 8:(t + 1) * 8],
                                   scsb[0:8, t * 128:(t + 1) * 128], ident[0:8, 0:8])
            return last
        ser.step("tensor", sct)

        sctm = scratchq[:, 0:128]

        def sce(e):
            return e.activation(sctm, pT[:, 0:128], ACT.Exp, scale=0.125)
        ser.step("scalar", sce)

        def scs(e):
            return e.reduce_sum(scratchq[:, 256:272].unsqueeze(2),
                                sctm.rearrange("p (a b) -> p a b", b=8), AX)
        ser.step("vector", scs)

        def scr(e):
            return e.reciprocal(scratchq[:, 272:288], scratchq[:, 256:272])
        ser.step("vector", scr)

        def dnt(e):
            return e.transpose(pS[0:16, 0:128], scratchq[:, 272:288], ident[:])
        ser.step("tensor", dnt)

        def dnc(e):
            return e.tensor_copy(rowbuf[:, :], pS[0:16, 0:128])
        ser.step("vector", dnc)

        def dnf(e, sem):
            e.dma_start(out=rowflat[0:1, :], in_=rowbuf[:, :]).then_inc(sem, 16)
        ser.dma_n("gpsimd", 1, dnf)

        attn_fm = scratchq[0:8, :]

        def efm(e):
            return e.activation(attn_fm[0:8, :], scsb[0:8, 0:2048], ACT.Exp, scale=0.125)
        ser.step("scalar", efm)

        def dnr(e):
            last = None
            for tc in range(4):
                pp = p1 if tc < 2 else p2
                last = e.matmul(pp[0:8, (tc % 2) * 512:(tc % 2) * 512 + 512],
                                onesrow[0:1, 0:8], rowflat[0:1, tc * 512:(tc + 1) * 512],
                                start=True, stop=True)
            return last
        ser.step("tensor", dnr)

        def anm(e):
            e.tensor_tensor(attn_fm[0:8, 0:1024], attn_fm[0:8, 0:1024],
                            p1[0:8, 0:1024], ALU.mult)
            e.nop()
            return e.tensor_tensor(attn_fm[0:8, 1024:2048], attn_fm[0:8, 1024:2048],
                                   p2[0:8, 0:1024], ALU.mult)
        ser.step("vector", anm)

        tap("attntm", scratchq[:, 0:512], 512)
        if l1a_stop <= 4:
            return
        v_f = k_f
        prep_w(W["L1_attn_Wv"])
        project(v_f, 24)
        tap("vf", v_f[:, 0:8192], 8192)
        tap("afm", scratchq[:, 0:2048], 2048)

        tap("vf", v_f[:, 0:8192], 8192)
        tap("afm", scsb[:, 0:2048], 2048)
        tap("lach", lachan[:, 0:32], 32)
        u_f = RB[:, 0:4 * T]
        for cc in range(4):
            def arm(e, cc=cc):
                last = None
                for tc in range(4):
                    pp = p1 if tc < 2 else p2
                    last = e.matmul(pp[:, (tc % 2) * 512:(tc % 2) * 512 + 512],
                                    selrep[0:8, cc * 128:(cc + 1) * 128],
                                    attn_fm[0:8, tc * 512:(tc + 1) * 512],
                                    start=True, stop=True)
                return last
            ser.step("tensor", arm)

            def mku(e, cc=cc):
                e.scalar_tensor_tensor(u_f[:, cc * T:cc * T + 1024], p1[:, 0:1024],
                                       lachan[:, 8 + cc:8 + cc + 1],
                                       v_f[:, cc * T:cc * T + 1024],
                                       ALU.mult, ALU.mult)
                e.nop()
                return e.scalar_tensor_tensor(u_f[:, cc * T + 1024:(cc + 1) * T], p2[:, 0:1024],
                                              lachan[:, 8 + cc:8 + cc + 1],
                                              v_f[:, cc * T + 1024:(cc + 1) * T],
                                              ALU.mult, ALU.mult)
            ser.step("vector", mku)

        if l1a_stop <= 5:
            return
        st_f = RX[:, 0:4 * T]
        for cc in range(4):
            def scan(e, cc=cc):
                return e.tensor_tensor_scan(
                    st_f[:, cc * T:(cc + 1) * T],
                    lachan[:, 4 + cc:4 + cc + 1].broadcast_to((128, T)),
                    u_f[:, cc * T:(cc + 1) * T],
                    0.0, ALU.mult, ALU.add)
            ser.step("vector", scan)

        tap("stf", st_f[:, 0:8192], 8192)
        tap("uf", u_f[:, 0:8192], 8192)
        if l1a_stop <= 6:
            return
        o_f = u_f
        for cc in range(4):
            def og1(e, cc=cc):
                return e.tensor_scalar(scratchq[:, 0:T], q_f[:, cc * T:(cc + 1) * T],
                                       lachan[:, 12 + cc:12 + cc + 1], None, ALU.mult)
            ser.step("vector", og1)

            def og2(e, cc=cc):
                return e.activation(scratchq[:, 0:T], scratchq[:, 0:T], ACT.Sigmoid)
            ser.step("scalar", og2)

            def og3(e, cc=cc):
                e.tensor_tensor(o_f[:, cc * T:(cc + 1) * T], q_f[:, cc * T:(cc + 1) * T],
                                st_f[:, cc * T:(cc + 1) * T], ALU.mult)
                e.nop()
                return e.tensor_tensor(o_f[:, cc * T:(cc + 1) * T],
                                       o_f[:, cc * T:(cc + 1) * T],
                                       scratchq[:, 0:T], ALU.add)
            ser.step("vector", og3)

        tap("of", o_f[:, 0:8192], 8192)
        if l1a_stop <= 7:
            return
        o_hi = vb16(RX, 32768, 4 * T)
        o_lo = vb16(RX, 49152, 4 * T)

        def osp(e):
            e.tensor_copy(o_hi[:, :], o_f[:, :])
            e.nop()
            return e.scalar_tensor_tensor(o_lo[:, :], o_f[:, :], -1.0,
                                          o_hi[:, :], ALU.bypass, ALU.subtract)
        ser.step("vector", osp)

        wo_st = RA[:, 0:4096]
        wo_hi = vb16(RA, 16384, 4096)
        wo_lo = vb16(RA, 24576, 4096)

        def ldwo(e, sem):
            for cc in range(4):
                e.dma_start(out=wo_st[:, cc * 1024:(cc + 1) * 1024],
                            in_=W["L1_attn_Wo"].ap()[cc * 128:(cc + 1) * 128, :]
                            ).then_inc(sem, 16)
        ser.dma_n("sync", 4, ldwo)

        def spwo(e):
            e.tensor_copy(wo_hi[:, :], wo_st[:, :])
            e.nop()
            return e.scalar_tensor_tensor(wo_lo[:, :], wo_st[:, :], -1.0,
                                          wo_hi[:, :], ALU.bypass, ALU.subtract)
        ser.step("vector", spwo)

        def ldg(e, sem):
            e.dma_start(out=rowflat[0:1, 0:D], in_=W["L1_attn_ln_scale"].ap()).then_inc(sem, 16)
            e.dma_start(out=rowflat[0:1, D:2 * D], in_=W["L1_attn_ln_bias"].ap()).then_inc(sem, 16)
        ser.dma_n("sync", 2, ldg)
        replicate_row(rowflat[0:1, 0:D], rep4k, D)
        replicate_row(rowflat[0:1, D:2 * D], rep4k2, D)

        def ldbo(e, sem):
            e.dma_start(out=rowflat[0:1, 0:D], in_=W["L1_attn_bo"].ap()).then_inc(sem, 16)
        ser.dma_n("sync", 1, ldbo)

        if l1a_stop <= 8:
            return
        xn_tiled = d_xspill2.ap().rearrange("(n p) d -> n p d", p=128)
        for t in range(NT):
            def mmo(e, t=t):
                last = None
                for cc in range(4):
                    for (a, b) in ((o_hi, wo_hi), (o_lo, wo_hi), (o_hi, wo_lo)):
                        for half in range(2):
                            last = e.matmul(
                                p1[:, half * 512:(half + 1) * 512],
                                a[:, cc * T + t * 128: cc * T + (t + 1) * 128],
                                b[:, cc * 1024 + half * 512: cc * 1024 + (half + 1) * 512],
                                start=(cc == 0 and a is o_hi and b is wo_hi),
                                stop=False)
                for half in range(2):
                    last = e.matmul(p1[:, half * 512:(half + 1) * 512],
                                    onesrow[0:1, :],
                                    rowflat[0:1, half * 512:(half + 1) * 512],
                                    start=False, stop=True)
                return last
            ser.step("tensor", mmo)
            if l1a_stop <= 9:
                continue

            def evo(e, t=t):
                return e.tensor_copy(scratch4k[:, 0:D], p1[:, 0:D])
            ser.step("vector", evo)

            ln_stats_tile(scratch4k[:, 0:D], 1e-6, 8)
            if l1a_stop <= 10:
                continue

            def inorm(e, t=t):
                e.tensor_scalar(scratch4k[:, 0:D], scratch4k[:, 0:D],
                                stat2[:, 8:9], stat2[:, 9:10], ALU.mult, ALU.add)
                e.nop()
                e.tensor_tensor(scratch4k[:, 0:D], scratch4k[:, 0:D], rep4k[:, 0:D], ALU.mult)
                e.nop()
                return e.tensor_tensor(scratch4k[:, 0:D], scratch4k[:, 0:D],
                                       rep4k2[:, 0:D], ALU.add)
            ser.step("vector", inorm)
            if l1a_stop <= 11:
                continue

            def ldxt(e, sem, t=t):
                e.dma_start(out=scratchx[:, 0:D], in_=xs_tiled[t]).then_inc(sem, 16)
            ser.dma_n("scalar", 1, ldxt)

            ln_stats_tile(scratch4k[:, 0:D], 1e-5, 16)
            if l1a_stop <= 12:
                continue

            def res3(e, t=t):
                e.tensor_scalar(scratch4k[:, 0:D], scratch4k[:, 0:D],
                                stat2[:, 16:17], stat2[:, 17:18], ALU.mult, ALU.add)
                e.nop()
                return e.tensor_tensor(scratch4k[:, 0:D], scratchx[:, 0:D],
                                       scratch4k[:, 0:D], ALU.add)
            ser.step("vector", res3)

            def stxt(e, sem, t=t):
                e.dma_start(out=xn_tiled[t], in_=scratch4k[:, 0:D]).then_inc(sem, 16)
            ser.dma_n("gpsimd", 1, stxt)

        if l1a_stop <= 13:
            return
        def relx(e, sem):
            for t in range(NT):
                e.dma_start(out=RX[:, t * D:(t + 1) * D], in_=xn_tiled[t]).then_inc(sem, 16)
        ser.dma_n("sync", NT, relx)


    def block_L0_attn():
        n = NT
        xln = RA[:, 0:NT * D]
        ln_full(RX, RA, 1e-5)

        xs_tiled = d_xspill.ap().rearrange("(n p) d -> n p d", p=128)

        def spx(e, sem):
            for t in range(NT):
                e.dma_start(out=xs_tiled[t], in_=RX[:, t * D:(t + 1) * D]).then_inc(sem, 16)
        ser.dma_n("scalar", NT, spx)

        xf_hi = vb16(RX, 0, ND * T)
        xf_lo = vb16(RX, 32768, ND * T)
        for c in range(ND):
            for tq in range(0, NT, 4):
                def tr(e, c=c, tq=tq):
                    last = None
                    for k in range(4):
                        t = tq + k
                        last = e.transpose(pT[:, k * 128:(k + 1) * 128],
                                           xln[:, t * D + c * 128: t * D + (c + 1) * 128],
                                           ident[:])
                    return last
                ser.step("tensor", tr)

                def ev(e, c=c, tq=tq):
                    e.tensor_copy(xf_hi[:, c * T + tq * 128: c * T + (tq + 4) * 128],
                                  pT[:, 0:512])
                    e.nop()
                    return e.scalar_tensor_tensor(
                        xf_lo[:, c * T + tq * 128: c * T + (tq + 4) * 128],
                        pT[:, 0:512], -1.0,
                        xf_hi[:, c * T + tq * 128: c * T + (tq + 4) * 128],
                        ALU.bypass, ALU.subtract)
                ser.step("vector", ev)

        PF32 = vf32(RA, 0, 2048)
        q_hi = vb16(RA, 8192, 2048)
        q_lo = vb16(RA, 12288, 2048)
        k_hi = vb16(RA, 16384, 2048)
        k_lo = vb16(RA, 20480, 2048)
        wT_hi = {"q": vb16(RA, 24576, 1024), "k": vb16(RA, 28672, 1024),
                 "v": vb16(RA, 32768, 1024)}
        wT_lo = {"q": vb16(RA, 26624, 1024), "k": vb16(RA, 30720, 1024),
                 "v": vb16(RA, 34816, 1024)}
        wstage = vf32(RA, 36864, 1024)
        vbuf = vb16(RB, 0, 16 * 260)
        E_f = vf32(RB, 8320, 1024)
        E_hi = vb16(RB, 12416, 1024)
        E_lo = vb16(RB, 14464, 1024)
        obuf = vf32(RB, 16512, 8 * 129)
        onorm = vf32(RB, 20640, 8 * 64)
        asm = vf32(RB, 22688, 1024)

        def vones(e):
            last = None
            for j in range(16):
                e.memset(vbuf[:, j * 260 + 128: j * 260 + 129], 1.0)
                last = e.memset(vbuf[:, j * 260 + 257: j * 260 + 258], 1.0)
            return last
        ser.step("vector", vones)

        def oslot(qc):
            if qc < 6:
                off = (qc // 3) * 512 + (qc % 3) * 170
                return p2[:, off:off + 129]
            off = (qc - 6) * 170
            return pS[:, off:off + 129]

        for pair in range(8):
            for wname, wdram in (("q", W["L0_attn_Wq"]), ("k", W["L0_attn_Wk"]),
                                 ("v", W["L0_attn_Wv"])):
                def ldw(e, sem, wdram=wdram, pair=pair):
                    e.dma_start(out=wstage[:, :],
                                in_=wdram.ap()[pair * 128:(pair + 1) * 128, :]
                                ).then_inc(sem, 16)
                ser.dma_n("sync", 1, ldw)
                for dq in range(0, 8, 4):
                    def trw(e, dq=dq):
                        last = None
                        for kk in range(4):
                            last = e.transpose(pT[:, kk * 128:(kk + 1) * 128],
                                               wstage[:, (dq + kk) * 128:(dq + kk + 1) * 128],
                                               ident[:])
                        return last
                    ser.step("tensor", trw)

                    def evw(e, dq=dq, wname=wname):
                        e.tensor_copy(wT_hi[wname][:, dq * 128:(dq + 4) * 128], pT[:, 0:512])
                        e.nop()
                        return e.scalar_tensor_tensor(
                            wT_lo[wname][:, dq * 128:(dq + 4) * 128], pT[:, 0:512], -1.0,
                            wT_hi[wname][:, dq * 128:(dq + 4) * 128],
                            ALU.bypass, ALU.subtract)
                    ser.step("vector", evw)

            def proj(wname):
                for tc in range(4):
                    def mm(e, tc=tc, wname=wname):
                        last = None
                        ops = [(dd, a, b) for dd in range(ND)
                               for (a, b) in ((xf_hi, wT_hi[wname]), (xf_lo, wT_hi[wname]),
                                              (xf_hi, wT_lo[wname]))]
                        for i, (dd, a, b) in enumerate(ops):
                            last = e.matmul(
                                p1[:, 0:512],
                                b[:, dd * 128:(dd + 1) * 128],
                                a[:, dd * T + tc * 512: dd * T + (tc + 1) * 512],
                                start=(i == 0), stop=(i == len(ops) - 1))
                        return last
                    ser.step("tensor", mm)

                    def ev(e, tc=tc):
                        return e.tensor_copy(PF32[:, tc * 512:(tc + 1) * 512], p1[:, 0:512])
                    ser.step("vector", ev)

            proj("q")

            def spq(e):
                e.tensor_copy(q_hi[:, :], PF32[:, :])
                e.nop()
                return e.scalar_tensor_tensor(q_lo[:, :], PF32[:, :], -1.0,
                                              q_hi[:, :], ALU.bypass, ALU.subtract)
            ser.step("vector", spq)

            proj("k")

            def spk(e):
                e.tensor_copy(k_hi[:, :], PF32[:, :])
                e.nop()
                return e.scalar_tensor_tensor(k_lo[:, :], PF32[:, :], -1.0,
                                              k_hi[:, :], ALU.bypass, ALU.subtract)
            ser.step("vector", spk)

            proj("v")
            for jq in range(0, 16, 4):
                def trv(e, jq=jq):
                    last = None
                    for kk in range(4):
                        j = jq + kk
                        last = e.transpose(pT[:, kk * 128:(kk + 1) * 128],
                                           PF32[:, j * 128:(j + 1) * 128], ident[:])
                    return last
                ser.step("tensor", trv)

                def evv(e, jq=jq):
                    last = None
                    for kk in range(4):
                        j = jq + kk
                        src = pT[:, kk * 128:(kk + 1) * 128]
                        e.tensor_copy(vbuf[:, j * 260 + 0: j * 260 + 64], src[:, 0:64])
                        e.tensor_copy(vbuf[:, j * 260 + 129: j * 260 + 193], src[:, 64:128])
                        e.nop()
                        e.scalar_tensor_tensor(vbuf[:, j * 260 + 64: j * 260 + 128],
                                               src[:, 0:64], -1.0,
                                               vbuf[:, j * 260 + 0: j * 260 + 64],
                                               ALU.bypass, ALU.subtract)
                        last = e.scalar_tensor_tensor(vbuf[:, j * 260 + 193: j * 260 + 257],
                                                      src[:, 64:128], -1.0,
                                                      vbuf[:, j * 260 + 129: j * 260 + 193],
                                                      ALU.bypass, ALU.subtract)
                    return last
                ser.step("vector", evv)

            for hp in range(2):
                ch0 = 64 * hp
                h = pair * 2 + hp
                for qq in range(4):
                    def oslot(qc):
                        return (p2[:, 0:129], p2[:, 512:641],
                                pS[:, 0:129], pT[:, 0:129])[qc]

                    def mk_smm(jb):
                        def smm(e, hp=hp, qq=qq, jb=jb, ch0=ch0):
                            last = None
                            for jj in range(2):
                                j = jb * 2 + jj
                                for pi, (ka, qa) in enumerate(((k_hi, q_hi), (k_hi, q_lo),
                                                               (k_lo, q_hi))):
                                    last = e.matmul(
                                        p1[:, jj * 512:(jj + 1) * 512],
                                        ka[ch0:ch0 + 64, j * 128:(j + 1) * 128],
                                        qa[ch0:ch0 + 64, qq * 512:(qq + 1) * 512],
                                        start=(pi == 0), stop=(pi == 2),
                                        tile_position=(ch0, 0))
                            return last
                        return smm

                    def mk_avmm(jb):
                        def avmm(e, hp=hp, jb=jb):
                            last = None
                            for jj in range(2):
                                j = jb * 2 + jj
                                for qc in range(4):
                                    last = e.matmul(oslot(qc),
                                                    E_hi[:, jj * 512 + qc * 128: jj * 512 + (qc + 1) * 128],
                                                    vbuf[:, j * 260 + hp * 129: j * 260 + hp * 129 + 129],
                                                    start=(j == 0), stop=False)
                                    last = e.matmul(oslot(qc),
                                                    E_lo[:, jj * 512 + qc * 128: jj * 512 + (qc + 1) * 128],
                                                    vbuf[:, j * 260 + hp * 129: j * 260 + hp * 129 + 129],
                                                    start=False, stop=(j == 15))
                            return last
                        return avmm

                    wt = None
                    for jb in range(8):
                        if jb == 0:
                            ser.step("tensor", mk_smm(0))
                        else:
                            ser.step("tensor", mk_smm(jb), wait_t=wt)
                            ser.step("tensor", mk_avmm(jb - 1))

                        def sexp(e):
                            return e.activation(E_f[:, 0:1024], p1[:, 0:1024], ACT.Exp,
                                                scale=0.125)
                        ser.step("scalar", sexp)
                        wt = ser.val

                        def ssplit(e):
                            e.tensor_copy(E_hi[:, 0:1024], E_f[:, 0:1024])
                            e.nop()
                            return e.scalar_tensor_tensor(E_lo[:, 0:1024], E_f[:, 0:1024],
                                                          -1.0, E_hi[:, 0:1024],
                                                          ALU.bypass, ALU.subtract)
                        ser.step("vector", ssplit)
                    ser.step("tensor", mk_avmm(7))

                    def oev(e):
                        last = None
                        for qc in range(4):
                            last = e.tensor_copy(obuf[:, qc * 129:(qc + 1) * 129],
                                                 oslot(qc))
                        return last
                    ser.step("vector", oev)

                    def oadd(e):
                        ob3 = obuf[:, 0:516].rearrange("p (s c) -> p s c", c=129)
                        e.tensor_copy(stat2[:, 80:84], ob3[:, :, 128:129].squeeze(2))
                        e.nop()
                        return e.tensor_tensor(
                            onorm[:, 0:256].rearrange("p (s c) -> p s c", c=64),
                            ob3[:, :, 0:64], ob3[:, :, 64:128], ALU.add)
                    ser.step("vector", oadd)

                    def orc(e):
                        return e.reciprocal(stat2[:, 88:92], stat2[:, 80:84])
                    ser.step("vector", orc)

                    def onrm(e):
                        on3 = onorm[:, 0:256].rearrange("p (s c) -> p s c", c=64)
                        return e.tensor_tensor(
                            on3, on3,
                            stat2[:, 88:92].unsqueeze(2).broadcast_to((128, 4, 64)),
                            ALU.mult)
                    ser.step("vector", onrm)

                    def otr(e):
                        last = None
                        for kk in range(4):
                            last = e.transpose(pT[0:64, kk * 128:(kk + 1) * 128],
                                               onorm[:, kk * 64:(kk + 1) * 64],
                                               ident[:])
                        return last
                    ser.step("tensor", otr)

                    def oas(e):
                        return e.tensor_copy(asm[0:64, 0:512], pT[0:64, 0:512])
                    ser.step("vector", oas)

                    def osp(e, sem, h=h, qq=qq):
                        e.dma_start(out=d_attns.ap()[h * 64:(h + 1) * 64,
                                                     qq * 512:(qq + 1) * 512],
                                    in_=asm[0:64, 0:512]).then_inc(sem, 16)
                    ser.dma_n("gpsimd", 1, osp)

        at_f32 = RX[:, 0:ND * T]
        at_hi = vb16(RA, 0, ND * T)
        at_lo = vb16(RA, 32768, ND * T)
        woT_hi = vb16(RB, 0, 8 * 1024)
        woT_lo = vb16(RB, 16384, 8 * 1024)

        def lda(e, sem):
            for c in range(ND):
                e.dma_start(out=at_f32[:, c * T:(c + 1) * T],
                            in_=d_attns.ap()[c * 128:(c + 1) * 128, :]).then_inc(sem, 16)
        ser.dma_n("sync", ND, lda)

        def spa(e):
            e.tensor_copy(at_hi[:, :], at_f32[:, :])
            e.nop()
            return e.scalar_tensor_tensor(at_lo[:, :], at_f32[:, :], -1.0,
                                          at_hi[:, :], ALU.bypass, ALU.subtract)
        ser.step("vector", spa)

        for oc in range(ND):
            def ldwo(e, sem, oc=oc):
                e.dma_start(out=scratch4k[:, 0:1024],
                            in_=W["L0_attn_Wo"].ap()[oc * 128:(oc + 1) * 128, :]
                            ).then_inc(sem, 16)
            ser.dma_n("sync", 1, ldwo)
            for dq in range(0, 8, 4):
                def trwo(e, dq=dq):
                    last = None
                    for kk in range(4):
                        last = e.transpose(pT[:, kk * 128:(kk + 1) * 128],
                                           scratch4k[:, (dq + kk) * 128:(dq + kk + 1) * 128],
                                           ident[:])
                    return last
                ser.step("tensor", trwo)

                def evwo(e, dq=dq, oc=oc):
                    ap_hi = woT_hi[:, :].rearrange("p (d o) -> p d o", o=1024)[
                        :, dq:dq + 4, oc * 128:(oc + 1) * 128]
                    ap_lo = woT_lo[:, :].rearrange("p (d o) -> p d o", o=1024)[
                        :, dq:dq + 4, oc * 128:(oc + 1) * 128]
                    src = pT[:, 0:512].rearrange("p (k c) -> p k c", c=128)
                    e.tensor_copy(ap_hi, src)
                    e.nop()
                    return e.scalar_tensor_tensor(ap_lo, src, -1.0, ap_hi,
                                                  ALU.bypass, ALU.subtract)
                ser.step("vector", evwo)

        def ldbo(e, sem):
            e.dma_start(out=rowflat[0:1, 0:D], in_=W["L0_attn_bo"].ap()).then_inc(sem, 16)
        ser.dma_n("sync", 1, ldbo)

        for t in range(NT):
            def mmo(e, t=t):
                last = None
                first = True
                for dd in range(ND):
                    for (a, b) in ((at_hi, woT_hi), (at_lo, woT_hi), (at_hi, woT_lo)):
                        for half in range(2):
                            last = e.matmul(
                                p1[:, half * 512:(half + 1) * 512],
                                a[:, dd * T + t * 128: dd * T + (t + 1) * 128],
                                b[:, dd * 1024 + half * 512: dd * 1024 + (half + 1) * 512],
                                start=first and half == 0 or (first and half == 1),
                                stop=False)
                        first = False
                for half in range(2):
                    last = e.matmul(p1[:, half * 512:(half + 1) * 512],
                                    onesrow[0:1, :],
                                    rowflat[0:1, half * 512:(half + 1) * 512],
                                    start=False, stop=True)
                return last
            ser.step("tensor", mmo)

            def evo(e, t=t):
                return e.tensor_copy(scratch4k[:, 0:D], p1[:, 0:D])
            ser.step("vector", evo)

            ln_stats_tile(scratch4k[:, 0:D], 1e-5, 8)

            def ldxt(e, sem, t=t):
                e.dma_start(out=scratchx[:, 0:D], in_=xs_tiled[t]).then_inc(sem, 16)
            ser.dma_n("scalar", 1, ldxt)

            def res0(e, t=t):
                e.tensor_scalar(scratch4k[:, 0:D], scratch4k[:, 0:D],
                                stat2[:, 8:9], stat2[:, 9:10], ALU.mult, ALU.add)
                e.nop()
                return e.tensor_tensor(RX[:, t * D:(t + 1) * D], scratchx[:, 0:D],
                                       scratch4k[:, 0:D], ALU.add)
            ser.step("vector", res0)

    blocks = {0: block_L0_attn, 1: block_L0_ffn, 2: block_L1_attn, 3: block_L1_ffn}
    for b in block_ids:
        if b in blocks:
            blocks[b]()

    if 3 not in block_ids:
        out_tiled = dout.ap().rearrange("(n p) d -> n p d", p=128)

        def stx(e, sem):
            for t in range(NT):
                e.dma_start(out=out_tiled[t], in_=RX[:, t * D:(t + 1) * D]).then_inc(sem, 16)
        ser.dma_n("gpsimd", NT, stx)

    def fin(e):
        return e.nop()
    ser.step("gpsimd", fin)

    with nc.semaphore("tok") as sem, nc.semaphore("gtok") as gsem, nc.Block() as block:
        ser.emit(block, sem, gsem)

    return nc


def _consts():
    import ml_dtypes
    c = {}
    c["ident"] = np.eye(128, dtype=np.float32)
    c["identb"] = np.eye(128).astype(ml_dtypes.bfloat16)
    c["onesrow"] = np.ones((1, 128), dtype=np.float32)
    c["onescol"] = np.ones((128, 1), dtype=np.float32)
    sel_qk = np.zeros((512, 8), dtype=np.float32)
    for ch in range(512):
        sel_qk[ch, ch // 64] = 1.0
    c["sel_qk"] = sel_qk
    sel_rep = np.zeros((8, 512), dtype=np.float32)
    for cc in range(4):
        for p in range(128):
            ch = cc * 128 + p
            sel_rep[ch // 64, cc * 128 + p] = 1.0
    c["sel_rep"] = sel_rep
    return c


def _in_map(core, inputs):
    m = dict(_consts())
    m["x"] = inputs["L0_attn_x"][core]
    for nm in ("L0_attn_Wq", "L0_attn_Wk", "L0_attn_Wv", "L0_attn_Wo",
               "L1_attn_Wq", "L1_attn_Wk", "L1_attn_Wv", "L1_attn_Wo",
               "L1_ffn_w1", "L1_ffn_w2", "L0_ffn_w_int8"):
        m[nm] = inputs[nm]
    m["L0_attn_bo"] = inputs["L0_attn_bo"].reshape(1, D)
    m["L0_ffn_weight_scale"] = np.asarray(inputs["L0_ffn_weight_scale"]).reshape(1, 1)
    m["L0_ffn_bias"] = inputs["L0_ffn_bias"].reshape(1, D)
    m["L1_attn_bq"] = inputs["L1_attn_bq"].reshape(512, 1)
    m["L1_attn_bk"] = inputs["L1_attn_bk"].reshape(512, 1)
    m["L1_attn_bv"] = inputs["L1_attn_bv"].reshape(512, 1)
    m["L1_attn_bo"] = inputs["L1_attn_bo"].reshape(1, D)
    m["L1_attn_alpha"] = np.asarray(inputs["L1_attn_alpha"]).reshape(512, 1)
    m["L1_attn_aug"] = np.asarray(inputs["L1_attn_aug_param"]).reshape(512, 1)
    m["L1_attn_ln_scale"] = inputs["L1_attn_ln_scale"].reshape(1, D)
    m["L1_attn_ln_bias"] = inputs["L1_attn_ln_bias"].reshape(1, D)
    m["L1_ffn_conv_w"] = np.asarray(inputs["L1_ffn_conv_w"]).reshape(4096, 3)
    m["L1_ffn_conv_b"] = np.asarray(inputs["L1_ffn_conv_b"]).reshape(4096, 1)
    for k, v in list(m.items()):
        m[k] = np.ascontiguousarray(v)
    return m


_BLOCK_IDS = (0, 1, 2, 3)


def kernel(**inputs):
    nc = build_nc(_BLOCK_IDS)
    in_maps = [_in_map(c, inputs) for c in range(8)]
    res = run_bass_kernel_spmd(nc, in_maps, core_ids=list(range(8)))
    return np.stack([np.asarray(r["out"], dtype=np.float32) for r in res.results], axis=0)
